# version 04
# baseline (speedup 1.0000x reference)
import contextlib
import numpy as np

import concourse.bass as bass
import concourse.mybir as mybir
from concourse.bass_utils import run_bass_kernel_spmd

F32 = mybir.dt.float32
F16 = mybir.dt.float16
I32 = mybir.dt.int32
U8 = mybir.dt.uint8
AF = mybir.ActivationFunctionType
ALU = mybir.AluOpType
AX = mybir.AxisListType

D = 1024
S = 2048
NCH = 8
DH = 64
WIN = 512
SCALE = DH ** -0.5
NG = 8
KS = 4
FT = 256
NFT = S // FT
L1T = 256
NL1T = S // L1T
CH = 64
NCK = S // CH
CPT = L1T // CH
EPS_LN = 1e-5
EPS_L2 = 1e-12
EPS_Q = 1e-8

ENGS = ("sync", "tensor", "vector", "scalar", "gpsimd")

_TAPS = ()


class Prog:
    def __init__(self, nc):
        self.nc = nc
        self.q = {e: [] for e in ENGS}
        self.cnt = {}
        self.order = []

    def sem(self, name):
        if name not in self.cnt:
            self.cnt[name] = 0
            self.order.append(name)
        return name

    def op(self, eng, meth, *args, inc=(), **kw):
        incs = []
        for s, k in inc:
            self.sem(s)
            self.cnt[s] += k
            incs.append((s, k))
        self.q[eng].append(("op", meth, args, kw, incs))

    def wait(self, eng, semname, val=None):
        self.sem(semname)
        v = self.cnt[semname] if val is None else val
        if v > 0:
            self.q[eng].append(("wait", semname, v))

    def mark(self, semname):
        self.sem(semname)
        return self.cnt[semname]

    def _last_real(self, eng):
        for item in reversed(self.q[eng]):
            if item[0] == "op" and item[1] not in ("nop", "dma_start"):
                return item
        return None

    def bump(self, eng, sem):
        self.sem(sem)
        item = self._last_real(eng)
        if item is not None and any(s0 == sem for s0, _ in item[4]):
            return
        if item is not None and not item[4]:
            self.cnt[sem] += 1
            item[4].append((sem, 1))
            return
        if item is None:
            self.cnt[sem] += 1
            self.q[eng].append(("op", "nop", (), {}, [(sem, 1)]))
            return
        self.cnt[sem] += 1
        if eng == "vector":
            self.q[eng].append(("op", "memset", (self.fence_buf[0:1, 0:1], 0.0),
                                {}, [(sem, 1)]))
        elif eng == "scalar":
            self.q[eng].append(("op", "copy",
                                (self.fence_buf[0:1, 1:2], self.fence_buf[0:1, 0:1]),
                                {}, [(sem, 1)]))
        elif eng == "gpsimd":
            self.q[eng].append(("op", "memset", (self.fence_buf[0:1, 2:3], 0.0),
                                {}, [(sem, 1)]))
        else:
            raise RuntimeError("no filler op for engine %s (sem %s)" % (eng, sem))

    def _fence(self, eng):
        item = self._last_real(eng)
        if item is not None and item[4]:
            s0, _ = item[4][-1]
            return (s0, self.cnt[s0])
        s0 = "fn_" + eng
        self.sem(s0)
        self.cnt[s0] += 1
        if item is not None:
            item[4].append((s0, 1))
        else:
            self.q[eng].append(("op", "nop", (), {}, [(s0, 1)]))
        return (s0, self.cnt[s0])

    def bar(self, engs=ENGS):
        fences = {e: self._fence(e) for e in engs}
        dw = self.cnt.get("dw", 0)
        dg = self.cnt.get("dg", 0)
        for e in engs:
            for e2 in engs:
                if e2 != e:
                    self.wait(e, *fences[e2])
            self.wait(e, "dw", dw)
            self.wait(e, "dg", dg)

    def finalize(self):
        nc = self.nc
        with contextlib.ExitStack() as st:
            handles = {n: st.enter_context(nc.semaphore(n)) for n in self.order}
            blk = st.enter_context(nc.Block())

            def make_runner(lst):
                def runner(E):
                    for item in lst:
                        if item[0] == "op":
                            _, meth, args, kw, incs = item
                            r = getattr(E, meth)(*args, **kw)
                            for s, k in incs:
                                r = r.then_inc(handles[s], k)
                        else:
                            _, s, v = item
                            E.wait_ge(handles[s], v)
                return runner

            for eng in ENGS:
                getattr(blk, eng)(make_runner(self.q[eng]))



def _host_consts():
    c = {}
    inv_freq = (1.0 / 10000.0 ** (np.arange(0, DH, 2, dtype=np.float32) / DH)).astype(np.float32)
    t = np.arange(S, dtype=np.float32)
    freqs = t[:, None] * inv_freq[None, :]
    emb = np.concatenate([freqs, freqs], axis=-1).astype(np.float32)
    cos = np.cos(emb).astype(np.float32)
    sin = np.sin(emb).astype(np.float32)
    ssin = sin.copy()
    ssin[:, :DH // 2] *= -1.0
    c["cosT"] = np.ascontiguousarray(np.tile(cos.T, (2, 1)))
    c["ssinT"] = np.ascontiguousarray(np.tile(ssin.T, (2, 1)))
    i = np.arange(128)
    c["maskd"] = (i[None, :] >= i[:, None]).astype(np.float32)
    c["mask4"] = (i[None, :] < i[:, None]).astype(np.float32)
    c["ident"] = np.eye(128, dtype=np.float32)
    c["ident16"] = np.eye(128, dtype=np.float16)
    c["onesr"] = np.ones((128, 128), dtype=np.float32)
    ind2 = np.zeros((128, 2), dtype=np.float32)
    ind2[:64, 0] = 1.0
    ind2[64:, 1] = 1.0
    c["ind2"] = ind2
    c["ind2T"] = np.ascontiguousarray(ind2.T)
    stack4 = np.zeros((32, 128), dtype=np.float32)
    for b in range(4):
        stack4[:, 32 * b:32 * (b + 1)] = np.eye(32, dtype=np.float32)
    c["stack4"] = stack4
    sels = np.zeros((128, 8 * 128), dtype=np.float32)
    for q in range(32):
        ty, j = divmod(q, 8)
        base = 32 * (q % 4)
        fb = 128 * (q // 4)
        sgn = -1.0 if ty == 1 else 1.0
        off = 0 if ty in (0, 1) else 16
        for p in range(128):
            r = off + 2 * j + (1 if p >= 64 else 0)
            sels[base + r, fb + p] = sgn
    c["sels"] = sels
    return c


def _host_stage(inputs):
    f = np.ascontiguousarray
    cm = dict(_host_consts())

    cm["l0_wq"] = f(inputs["L0_attn_Wq"].T.reshape(D, NG, 128).transpose(1, 0, 2))
    cm["l0_wk"] = f(inputs["L0_attn_Wk"].T.reshape(D, NG, 128).transpose(1, 0, 2))
    cm["l0_wv"] = f(inputs["L0_attn_Wv"].T.reshape(D, NG, 128).transpose(1, 0, 2))
    cm["l0_wo"] = f(inputs["L0_attn_Wo"].T.reshape(NG, 128, D))
    for nm, key in (("l0_bq", "L0_attn_bq"), ("l0_bk", "L0_attn_bk"),
                    ("l0_bv", "L0_attn_bv"), ("l0_bo", "L0_attn_bo"),
                    ("l0_cqb", "L0_attn_cqb"), ("l0_ckb", "L0_attn_ckb"),
                    ("l0_cvb", "L0_attn_cvb")):
        cm[nm] = f(inputs[key].reshape(NG, 128).T)
    for nm, key in (("l0_cq", "L0_attn_cqw"), ("l0_ck", "L0_attn_ckw"),
                    ("l0_cv", "L0_attn_cvw")):
        cm[nm] = f(inputs[key].reshape(NG, 128, KS).transpose(1, 0, 2).reshape(128, NG * KS))

    cm["l0_k0"] = f(inputs["L0_ffn_k0"].view(np.uint8))
    cm["l0_k1"] = f(inputs["L0_ffn_k1"].view(np.uint8))
    cm["l0_k2"] = f(inputs["L0_ffn_k2"].view(np.uint8))
    cm["l0_s"] = f(np.stack([np.asarray(inputs["L0_ffn_s0"]),
                             np.asarray(inputs["L0_ffn_s1"]),
                             np.asarray(inputs["L0_ffn_s2"])]).reshape(1, 3).astype(np.float32))

    qkv = np.stack([inputs["L1_attn_Wq"].T, inputs["L1_attn_Wk"].T,
                    inputs["L1_attn_Wv"].T])
    cm["l1_wqkv"] = f(qkv.reshape(3, D, NG, 128).transpose(0, 2, 1, 3))
    cm["l1_wo"] = f(inputs["L1_attn_Wo"].T.reshape(NG, 128, D))
    cm["l1_wlr"] = f(inputs["L1_attn_Wlr"].T)
    cm["l1_wbeta"] = f(inputs["L1_attn_Wbeta"].T)
    wii = inputs["L1_attn_W_in_init"][0]
    woi = inputs["L1_attn_W_out_init"][0]
    cm["l1_winit"] = f(wii.transpose(1, 2, 0).reshape(NG, 2 * DH, DH))
    cm["l1_wout"] = f(woi.transpose(1, 0, 2).reshape(NG, 2 * DH, DH))
    cm["l1_woutT"] = f(woi.transpose(1, 2, 0).reshape(NG, 2 * DH, DH))
    cm["l1_fw"] = f(inputs["L1_ffn_weight"].T)

    x = inputs["L0_attn_x"]
    pc = [{"x_in": f(x[b].T.reshape(NCH, 128, S))} for b in range(8)]
    return cm, pc



def build_program(taps=()):
    import os as _os
    _PH = int(_os.environ.get("KERNEL_PHASES", "9"))
    _SC = int(_os.environ.get("KERNEL_SCAN", "5"))
    nc = bass.Bass("TRN2", target_bir_lowering=False,
                   detect_race_conditions=not _os.environ.get("KERNEL_NO_RD"))
    taps = set(taps)

    dr = {}

    def din(name, shape, dt=F32):
        dr[name] = nc.dram_tensor(name, list(shape), dt, kind="ExternalInput")

    din("x_in", (NCH, 128, S))
    din("cosT", (128, S)); din("ssinT", (128, S))
    din("maskd", (128, 128)); din("mask4", (128, 128))
    din("ident", (128, 128)); din("ident16", (128, 128), F16)
    din("onesr", (128, 128))
    din("ind2", (128, 2)); din("ind2T", (2, 128))
    din("stack4", (32, 128)); din("sels", (128, 1024))
    din("l0_wq", (NG, D, 128)); din("l0_wk", (NG, D, 128)); din("l0_wv", (NG, D, 128))
    din("l0_wo", (NG, 128, D))
    for nm in ("l0_bq", "l0_bk", "l0_bv", "l0_bo", "l0_cqb", "l0_ckb", "l0_cvb"):
        din(nm, (128, NG))
    for nm in ("l0_cq", "l0_ck", "l0_cv"):
        din(nm, (128, NG * KS))
    din("l0_k0", (2 * D, 4096), U8)
    din("l0_k1", (4096, 4096), U8)
    din("l0_k2", (4096, D), U8)
    din("l0_s", (1, 3))
    din("l1_wqkv", (3, NG, D, 128)); din("l1_wo", (NG, 128, D))
    din("l1_wlr", (D, 32)); din("l1_wbeta", (D, 32))
    din("l1_winit", (NG, 128, DH)); din("l1_wout", (NG, 128, DH)); din("l1_woutT", (NG, 128, DH))
    din("l1_fw", (D, D))

    out = nc.dram_tensor("out", [NCH, 128, S], F32, kind="ExternalOutput")
    w0scr = nc.dram_tensor("w0scr", [16, NCH, 128, 256], F16, kind="Internal")
    w1scr = nc.dram_tensor("w1scr", [32, 32, 128, 128], F16, kind="Internal")
    w2scr = nc.dram_tensor("w2scr", [32, 128, D], F16, kind="Internal")
    wqscr = nc.dram_tensor("wqscr", [NCH, 128, D], F16, kind="Internal")

    x_sb = nc.alloc_sbuf_tensor("x_sb", [128, NCH * S], F32)
    xn_h = nc.alloc_sbuf_tensor("xn_h", [128, NCH * S], F16)
    xn_l = nc.alloc_sbuf_tensor("xn_l", [128, NCH * S], F16)
    ident = nc.alloc_sbuf_tensor("ident_b", [128, 128], F32)
    ident16 = nc.alloc_sbuf_tensor("ident16_b", [128, 128], F16)
    onesr = nc.alloc_sbuf_tensor("onesr_b", [128, 128], F32)
    bias = {}
    for nm in ("l0_bq", "l0_bk", "l0_bv", "l0_bo", "l0_cqb", "l0_ckb", "l0_cvb"):
        bias[nm] = nc.alloc_sbuf_tensor(nm + "_s", [128, NG], F32)
    convw = {}
    for nm in ("l0_cq", "l0_ck", "l0_cv"):
        convw[nm] = nc.alloc_sbuf_tensor(nm + "_s", [128, NG * KS], F32)
    s012 = nc.alloc_sbuf_tensor("s012_b", [1, 3], F32)
    s012b = nc.alloc_sbuf_tensor("s012b", [128, 3], F32)
    fence_buf = nc.alloc_sbuf_tensor("fence_buf", [1, 4], F32)
    wsb = nc.alloc_sbuf_tensor("wsb_p", [128, 8], F32)

    AR0 = ((nc.sbuf_base + 63) // 64) * 64
    TOPB = nc.sbuf_top
    _sz = {F32: 4, F16: 2, I32: 4, U8: 1}

    def at(name, shape, dt, off):
        assert AR0 + off + shape[1] * _sz[dt] <= TOPB, (
            "arena overflow: %s needs %d..%d top %d" %
            (name, AR0 + off, AR0 + off + shape[1] * _sz[dt], TOPB))
        return nc.alloc_sbuf_tensor_at(name, shape, dt, offset=AR0 + off)

    cosT = at("cosT_b", [128, S], F32, 0)
    ssinT = at("ssinT_b", [128, S], F32, 8192)
    maskd = at("maskd_b", [128, 128], F32, 16384)
    mask4 = at("mask4_b", [128, 128], F32, 16896)
    ind2T = at("ind2T_b", [2, 128], F32, 17408)
    ind2 = at("ind2_b", [128, 2], F32, 17920)
    vaug0 = at("vaug0", [128, 16 * 65], F32, 17952)
    vaug1 = at("vaug1", [128, 16 * 65], F32, 22112)
    pbuf = at("pbuf", [128, 2 * 128], F32, 26272)
    l0qkv = at("l0qkv", [128, 3 * S], F32, 27296)
    ln_sq = at("ln_sq", [128, 512], F32, 27296)
    ln_t0 = at("ln_t0", [128, 512], F32, 29344)
    lnr = at("lnr", [1, 2560], F32, 31392)
    att_scr = at("att_scr", [128, 512], F32, 51872)
    att_pad = at("att_pad", [128, 520], F32, 53920)
    og_h = at("og_h", [128, S], F16, 56000)
    og_l = at("og_l", [128, S], F16, 60096)
    land = at("land", [128, D], F32, 64192)
    w_h = at("w_h", [128, D], F16, 68288)
    w_l = at("w_l", [128, D], F16, 70336)
    rz = at("rz", [128, 128], F32, 72384)
    o_f = at("o_f", [128, 128], F32, 72896)
    oh_loc = at("oh_loc", [128, 128], F16, 73408)
    ol_loc = at("ol_loc", [128, 128], F16, 73664)
    att_s = at("att_s", [2, 1024], F32, 73920)

    k0b_a = at("k0b_a", [128, 4096], U8, 0)
    k0b_b = at("k0b_b", [128, 4096], U8, 4096)
    h0_h = at("h0_h", [128, 32 * FT], F16, 8192)
    h0_l = at("h0_l", [128, 32 * FT], F16, 24576)
    ftmp = at("ftmp", [128, 4096], F32, 8192)
    f16t = at("f16t", [128, 4096], F16, 24576)
    fb32 = at("fb32", [128, 2048], F32, 32768)
    w0bt = at("w0bt", [128, 2 * 2048], F16, 40960)
    w1bt = at("w1bt", [128, 2 * 4096], F16, 49152)
    w2bt = at("w2bt", [128, 2 * 1024], F16, 65536)
    h1t = at("h1t", [128, 512], F16, 69632)
    ffn_h0f = at("ffn_h0f", [128, FT], F32, 70656)

    scal = at("scal", [128, 1024], F32, 0)
    st_win = at("st_win", [128, NG * DH], F32, 4096)
    st_wout = at("st_wout", [128, NG * DH], F32, 6144)
    st_woutT = at("st_woutT", [128, NG * DH], F32, 8192)
    knd = at("knd", [128, NG * DH], F32, 10240)
    vnd = at("vnd", [128, NG * DH], F32, 12288)
    ptb = at("ptb", [128, NG * DH], F32, 14336)
    gzb = at("gzb", [128, NG * DH], F32, 16384)
    sm_s = at("sm_s", [128, 64], F32, 18432)
    pq_sb = at("pq_sb", [128, NG * 2 * DH], F32, 18688)
    scan_t = at("scan_t", [128, 64], F32, 22816)
    lrc_a = at("lrc_a", [32, NCK], F32, 23072)
    lrc_b = at("lrc_b", [32, NCK], F32, 23072 + 4 * NCK)
    lrc4_sb = at("lrc4_sb", [128, 2 * NCK], F32, 23328)
    sels_sb = at("sels_sb", [128, 1024], F32, 23584)
    stack4_sb = at("stack4_sb", [32, 128], F32, 27680)
    lr_sb = at("lr_sb", [32, S], F32, 28192)
    wlr_sb = at("wlr_sb", [128, 256], F32, 36384)
    wbe_sb = at("wbe_sb", [128, 256], F32, 37408)
    wlr_h = at("wlr_h", [128, 256], F16, 38432)
    wlr_l = at("wlr_l", [128, 256], F16, 38944)
    wbe_h = at("wbe_h", [128, 256], F16, 39456)
    wbe_l = at("wbe_l", [128, 256], F16, 39968)
    land2 = at("land2", [128, D], F32, 40480)
    u_t = at("u_t", [128, D], F32, 44576)
    g_t = at("g_t", [128, D], F32, 48672)
    absr = at("absr", [128, 16], F32, 52768)
    wq16c = at("wq16c", [128, D], F16, 52832)
    l1qkv = at("l1qkv", [128, 3 * S], F32, 23584)
    l1_o = at("l1_o", [128, S], F32, 48160)
    xt_sub = at("xt_sub", [128, 1024], F32, 48160)
    c16 = at("c16", [128, 1024], F16, 52256)
    rnd_m = at("rnd_m", [128, 1024], F32, 52256)
    land3b = at("land3b", [128, D], F32, 52256)
    rnd_t = at("rnd_t", [128, 1024], F32, 57376)
    ojh = at("ojh", [128, 256], F16, 56352)
    ojl = at("ojl", [128, 256], F16, 56864)
    land3 = at("land3", [128, D], F32, 57376)
    w3h = at("w3h", [128, D], F16, 61472)
    w3l = at("w3l", [128, D], F16, 63520)
    i32b = at("i32b", [128, 512], I32, 65568)
    wqb0 = at("wqb0", [128, D], F16, 67616)
    wqb1 = at("wqb1", [128, D], F16, 69664)
    ffn_s = at("ffn_s", [128, 768], F32, 71712)
    aqp16 = at("aqp16", [128, 128], F16, 74784)

    PS = [nc.alloc_psum_tensor("ps%d" % i, [128, 512], F32) for i in range(8)]

    P = Prog(nc)
    P.fence_buf = fence_buf
    DW, DG, DS = "dw", "dg", "ds"
    TT, VV, SS, OUT = "t", "v", "s", "o"
    for s_ in (DW, DG, DS, TT, VV, SS, OUT):
        P.sem(s_)

    def dma(eng, sem, out_, in__):
        P.op(eng, "dma_start", inc=[(sem, 16)], out=out_, in_=in__)

    tap_names = []

    def tap(name, ap, shape, dt=F32):
        if name not in taps:
            return
        t_ = nc.dram_tensor("tap_" + name, list(shape), dt, kind="ExternalOutput")
        tap_names.append("tap_" + name)
        P.bar()
        dma("gpsimd", OUT, t_.ap(), ap)
        P.bar()

    dma("gpsimd", DG, x_sb[:].rearrange("p (c t) -> p c t", t=S),
        dr["x_in"].ap().rearrange("c p t -> p c t"))
    for sb, nm in ((cosT, "cosT"), (ssinT, "ssinT"), (maskd, "maskd"),
                   (mask4, "mask4"), (ident, "ident"), (ident16, "ident16"),
                   (onesr, "onesr"),
                   (ind2, "ind2"), (ind2T, "ind2T"), (s012, "l0_s")):
        dma("sync", DW, sb[:], dr[nm].ap())
    for nm in bias:
        dma("sync", DW, bias[nm][:], dr[nm].ap())
    for nm in convw:
        dma("sync", DW, convw[nm][:], dr[nm].ap())
    for e in ("tensor", "vector", "scalar"):
        P.wait(e, DW)
        P.wait(e, DG)

    P.op("tensor", "matmul", PS[0][0:128, 0:3], onesr[0:1, 0:128], s012[0:1, 0:3],
         start=True, stop=True, inc=[(TT, 1)])
    P.wait("vector", TT)
    P.op("vector", "tensor_scalar", s012b[:, 0:2], PS[0][0:128, 0:2], 2.0, None, ALU.mult)
    P.op("vector", "tensor_copy", s012b[:, 2:3], PS[0][0:128, 2:3], inc=[(VV, 1)])
    P.bar()

    def emit_layernorm():
        for tt4 in range(4):
            for c in range(NCH):
                xc = x_sb[:, c * S + tt4 * 512: c * S + (tt4 + 1) * 512]
                P.wait("scalar", TT, P.mark(TT))
                P.op("scalar", "activation", ln_sq[:, 0:512], xc, AF.Square,
                     inc=[(SS, 1)])
                P.wait("tensor", SS)
                P.op("tensor", "matmul", PS[0][0:1, 0:512], onesr[0:128, 0:1], xc,
                     start=(c == 0), stop=(c == NCH - 1))
                P.op("tensor", "matmul", PS[1][0:1, 0:512], onesr[0:128, 0:1],
                     ln_sq[:, 0:512], start=(c == 0), stop=(c == NCH - 1),
                     inc=[(TT, 1)])
            P.wait("vector", TT)
            m_r = lnr[0:1, 0:512]
            v_r = lnr[0:1, 512:1024]
            r_r = lnr[0:1, 1024:1536]
            b_r = lnr[0:1, 1536:2048]
            t_r = lnr[0:1, 2048:2560]
            P.op("vector", "tensor_scalar", m_r, PS[0][0:1, 0:512], 1.0 / D, None, ALU.mult)
            P.op("vector", "tensor_scalar", v_r, PS[1][0:1, 0:512], 1.0 / D, None, ALU.mult)
            P.op("vector", "tensor_tensor", t_r, m_r, m_r, ALU.mult)
            P.op("vector", "tensor_tensor", v_r, v_r, t_r, ALU.subtract)
            P.op("vector", "tensor_scalar", v_r, v_r, EPS_LN, None, ALU.add)
            P.op("vector", "reciprocal", v_r, v_r, inc=[(VV, 1)])
            P.wait("scalar", VV)
            P.op("scalar", "activation", r_r, v_r, AF.Sqrt, inc=[(SS, 1)])
            P.wait("vector", SS)
            P.op("vector", "tensor_tensor", b_r, m_r, r_r, ALU.mult)
            P.op("vector", "tensor_scalar", b_r, b_r, -1.0, None, ALU.mult,
                 inc=[(VV, 1)])
            P.wait("tensor", VV)
            P.op("tensor", "matmul", PS[2][0:128, 0:512], onesr[0:1, 0:128], r_r,
                 start=True, stop=True)
            P.op("tensor", "matmul", PS[3][0:128, 0:512], onesr[0:1, 0:128], b_r,
                 start=True, stop=True, inc=[(TT, 1)])
            P.wait("vector", TT)
            for c in range(NCH):
                cs = slice(c * S + tt4 * 512, c * S + (tt4 + 1) * 512)
                t0 = ln_t0[:, 0:512]
                P.op("vector", "tensor_tensor", t0, x_sb[:, cs], PS[2][0:128, 0:512], ALU.mult)
                P.op("vector", "tensor_tensor", t0, t0, PS[3][0:128, 0:512], ALU.add)
                P.op("vector", "tensor_copy", xn_h[:, cs], t0)
                P.op("vector", "tensor_tensor", xn_l[:, cs], t0, xn_h[:, cs], ALU.subtract)
            P.bump("vector", VV)
            P.wait("tensor", VV)
        P.bar()

    emit_layernorm()
    tap("xn_h", xn_h[:], (128, NCH * S), F16)
    tap("xn_l", xn_l[:], (128, NCH * S), F16)

    q_c = l0qkv[:, 0:S]
    k_c = l0qkv[:, S:2 * S]
    v_c = l0qkv[:, 2 * S:3 * S]
    scr = att_scr
    pad = att_pad

    qkv_specs = [("l0_wq", "l0_bq", "l0_cq", "l0_cqb", q_c),
                 ("l0_wk", "l0_bk", "l0_ck", "l0_ckb", k_c),
                 ("l0_wv", "l0_bv", "l0_cv", "l0_cvb", v_c)]

    for g in (range(NG) if _PH >= 2 else ()):
        for pi, (wnm, bnm, cnm, cbnm, dest) in enumerate(qkv_specs):
            P.wait("sync", VV, P.mark(VV))
            dma("sync", DW, land[:].rearrange("p (c j) -> p c j", j=128),
                dr[wnm].ap()[g].rearrange("(c p) j -> p c j", p=128))
            P.wait("vector", DW)
            P.wait("vector", TT, P.mark(TT))
            P.op("vector", "tensor_copy", w_h[:], land[:])
            P.op("vector", "tensor_tensor", w_l[:], land[:], w_h[:], ALU.subtract,
                 inc=[(VV, 1)])
            P.wait("tensor", VV)
            for tt4 in range(4):
                pb = PS[4 + (tt4 % 2)]
                P.wait("tensor", SS, P.mark(SS))
                for c in range(NCH):
                    xh_ = xn_h[:, c * S + tt4 * 512: c * S + (tt4 + 1) * 512]
                    xl_ = xn_l[:, c * S + tt4 * 512: c * S + (tt4 + 1) * 512]
                    lw_h = w_h[:, c * 128:(c + 1) * 128]
                    lw_l = w_l[:, c * 128:(c + 1) * 128]
                    P.op("tensor", "matmul", pb[0:128, 0:512], lw_h, xh_,
                         start=(c == 0), stop=False)
                    P.op("tensor", "matmul", pb[0:128, 0:512], lw_h, xl_,
                         start=False, stop=False)
                    P.op("tensor", "matmul", pb[0:128, 0:512], lw_l, xh_,
                         start=False, stop=(c == NCH - 1),
                         inc=([(TT, 1)] if c == NCH - 1 else ()))
                P.wait("scalar", TT)
                P.wait("scalar", VV, P.mark(VV))
                P.op("scalar", "activation", pad[:, 3:515], pb[0:128, 0:512],
                     AF.Identity, bias=bias[bnm][:, g:g + 1], scale=1.0, inc=[(SS, 1)])
                P.wait("vector", SS)
                if tt4 == 0:
                    P.op("vector", "memset", pad[:, 0:3], 0.0)
                dsl = slice(tt4 * 512, (tt4 + 1) * 512)
                cw = convw[cnm]
                P.op("vector", "tensor_scalar", dest[:, dsl], pad[:, 0:512],
                     cw[:, g * KS:g * KS + 1], bias[cbnm][:, g:g + 1],
                     ALU.mult, ALU.add)
                for kk in range(1, KS):
                    P.op("vector", "scalar_tensor_tensor", dest[:, dsl],
                         pad[:, kk:kk + 512], cw[:, g * KS + kk:g * KS + kk + 1],
                         dest[:, dsl], ALU.mult, ALU.add)
                P.op("vector", "tensor_copy", pad[:, 0:3], pad[:, 512:515],
                     inc=[(VV, 1)])
        for dest in (q_c, k_c):
            for tt4 in range(4):
                dsl = slice(tt4 * 512, (tt4 + 1) * 512)
                P.op("vector", "tensor_copy", scr[0:32, 0:512], dest[32:64, dsl])
                P.op("vector", "tensor_copy", scr[32:64, 0:512], dest[0:32, dsl])
                P.op("vector", "tensor_copy", scr[64:96, 0:512], dest[96:128, dsl])
                P.op("vector", "tensor_copy", scr[96:128, 0:512], dest[64:96, dsl])
                P.op("vector", "tensor_tensor", dest[:, dsl], dest[:, dsl],
                     cosT[:, dsl], ALU.mult)
                P.op("vector", "tensor_tensor", scr[:, 0:512], scr[:, 0:512],
                     ssinT[:, dsl], ALU.mult)
                P.op("vector", "tensor_tensor", dest[:, dsl], dest[:, dsl],
                     scr[:, 0:512], ALU.add)
            P.bump("vector", VV)
        for dest in (q_c, k_c):
            for tt4 in range(4):
                dsl = slice(tt4 * 512, (tt4 + 1) * 512)
                P.wait("scalar", VV, P.mark(VV))
                P.op("scalar", "activation", scr[:, 0:512], dest[:, dsl],
                     AF.Square, inc=[(SS, 1)])
                P.wait("tensor", SS)
                P.op("tensor", "matmul", PS[6][0:2, 0:512], ind2[0:128, 0:2],
                     scr[:, 0:512], start=True, stop=True, inc=[(TT, 1)])
                P.wait("vector", TT)
                P.op("vector", "tensor_scalar", att_s[0:2, 0:512], PS[6][0:2, 0:512],
                     EPS_L2, None, ALU.add)
                P.op("vector", "reciprocal", att_s[0:2, 0:512], att_s[0:2, 0:512],
                     inc=[(VV, 1)])
                P.wait("scalar", VV)
                P.op("scalar", "activation", att_s[0:2, 512:1024], att_s[0:2, 0:512],
                     AF.Sqrt, inc=[(SS, 1)])
                P.wait("tensor", SS)
                P.op("tensor", "matmul", PS[6][0:128, 0:512], ind2T[0:2, 0:128],
                     att_s[0:2, 512:1024], start=True, stop=True, inc=[(TT, 1)])
                P.wait("vector", TT)
                P.op("vector", "tensor_tensor", dest[:, dsl], dest[:, dsl],
                     PS[6][0:128, 0:512], ALU.mult, inc=[(VV, 1)])
                P.wait("tensor", VV)
            P.bump("vector", VV)
        if g == 0:
            tap("g0_q", q_c, (128, S))
            tap("g0_k", k_c, (128, S))
            tap("g0_v", v_c, (128, S))
        P.wait("vector", VV, P.mark(VV))
        P.op("vector", "memset", vaug0[:], 1.0)
        P.op("vector", "memset", vaug1[:], 1.0, inc=[(VV, 1)])
        P.wait("tensor", VV)
        for h in range(2):
            va = vaug0 if h == 0 else vaug1
            for tt16 in range(16):
                P.op("tensor", "transpose", PS[6][0:128, 0:64],
                     v_c[h * 64:(h + 1) * 64, tt16 * 128:(tt16 + 1) * 128],
                     ident[h * 64:(h + 1) * 64, h * 64:h * 64 + 64],
                     inc=[(TT, 1)])
                P.wait("vector", TT)
                P.op("vector", "tensor_copy", va[:, tt16 * 65:tt16 * 65 + 64],
                     PS[6][0:128, 0:64], inc=[(VV, 1)])
                P.wait("tensor", VV)
        P.wait("sync", VV, P.mark(VV))
        dma("sync", DW, land[:], dr["l0_wo"].ap()[g])
        P.wait("vector", DW)
        P.op("vector", "tensor_copy", w_h[:], land[:])
        P.op("vector", "tensor_tensor", w_l[:], land[:], w_h[:], ALU.subtract,
             inc=[(VV, 1)])
        for qt in range(16):
            kbs = list(range(max(0, qt - 4), qt + 1))
            for h in range(2):
                va = vaug0 if h == 0 else vaug1
                hs = slice(h * 64, (h + 1) * 64)
                psc = PS[0 + h]
                po = PS[2 + h]
                pt = pbuf[:, h * 128:(h + 1) * 128]
                P.wait("tensor", VV, P.mark(VV))
                P.wait("tensor", SS, P.mark(SS))
                for ki, kb in enumerate(kbs):
                    P.op("tensor", "matmul", psc[0:128, 0:128],
                         k_c[hs, kb * 128:(kb + 1) * 128],
                         q_c[hs, qt * 128:(qt + 1) * 128],
                         start=True, stop=True, inc=[(TT, 1)])
                    P.wait("scalar", TT)
                    P.op("scalar", "activation", pt, psc[0:128, 0:128], AF.Exp,
                         scale=float(SCALE), inc=[(SS, 1)])
                    if kb == qt or kb == qt - 4:
                        P.wait("vector", SS)
                        mk = maskd if kb == qt else mask4
                        P.op("vector", "tensor_tensor", pt, pt, mk[:], ALU.mult,
                             inc=[(VV, 1)])
                        P.wait("tensor", VV)
                    else:
                        P.wait("tensor", SS)
                    P.op("tensor", "matmul", po[0:65, 0:128],
                         va[:, kb * 65:(kb + 1) * 65], pt,
                         start=(ki == 0), stop=(ki == len(kbs) - 1),
                         inc=([(TT, 1)] if ki == len(kbs) - 1 else ()))
                P.wait("vector", TT)
                P.op("vector", "reciprocal", rz[64:65, 0:128], po[64:65, 0:128],
                     inc=[(VV, 1)])
                P.wait("scalar", TT)
                P.op("scalar", "activation", o_f[0:64, 0:128], po[0:64, 0:128],
                     AF.Copy, inc=[(SS, 1)])
                P.wait("tensor", VV)
                P.op("tensor", "matmul", PS[6][0:64, 0:128], onesr[64:65, 0:64],
                     rz[64:65, 0:128], start=True, stop=True, inc=[(TT, 1)])
                P.wait("vector", TT)
                P.wait("vector", SS)
                P.op("vector", "tensor_tensor", o_f[0:64, 0:128], o_f[0:64, 0:128],
                     PS[6][0:64, 0:128], ALU.mult)
                P.op("vector", "tensor_copy", oh_loc[0:64, 0:128], o_f[0:64, 0:128])
                P.op("vector", "tensor_tensor", ol_loc[0:64, 0:128],
                     o_f[0:64, 0:128], oh_loc[0:64, 0:128], ALU.subtract)
                P.op("vector", "tensor_copy",
                     og_h[h * 64:(h + 1) * 64, qt * 128:(qt + 1) * 128],
                     oh_loc[0:64, 0:128])
                P.op("vector", "tensor_copy",
                     og_l[h * 64:(h + 1) * 64, qt * 128:(qt + 1) * 128],
                     ol_loc[0:64, 0:128], inc=[(VV, 1)])
                P.wait("tensor", VV)
        if g == 0:
            tap("g0_og", og_h[:], (128, S), F16)
        P.wait("tensor", VV, P.mark(VV))
        for mc in range(NCH):
            for tt4 in range(4):
                pb = PS[4 + (tt4 % 2)]
                P.wait("tensor", VV, P.mark(VV))
                sl = slice(tt4 * 512, (tt4 + 1) * 512)
                P.op("tensor", "matmul", pb[0:128, 0:512],
                     w_h[:, mc * 128:(mc + 1) * 128], og_h[:, sl],
                     start=True, stop=False)
                P.op("tensor", "matmul", pb[0:128, 0:512],
                     w_h[:, mc * 128:(mc + 1) * 128], og_l[:, sl],
                     start=False, stop=False)
                P.op("tensor", "matmul", pb[0:128, 0:512],
                     w_l[:, mc * 128:(mc + 1) * 128], og_h[:, sl],
                     start=False, stop=True, inc=[(TT, 1)])
                P.wait("vector", TT)
                xs = x_sb[:, mc * S + tt4 * 512: mc * S + (tt4 + 1) * 512]
                if g == NG - 1:
                    P.op("vector", "scalar_tensor_tensor", xs, pb[0:128, 0:512],
                         bias["l0_bo"][:, mc:mc + 1], xs, ALU.add, ALU.add,
                         inc=[(VV, 1)])
                else:
                    P.op("vector", "tensor_tensor", xs, xs, pb[0:128, 0:512],
                         ALU.add, inc=[(VV, 1)])
        P.bar(("tensor", "vector", "scalar", "sync"))
    P.bar()
    tap("x_attn", x_sb[:], (128, NCH * S))

    for c in (range(NCH) if _PH >= 3 else ()):
        P.wait("sync", VV, P.mark(VV))
        dma("sync", DW, k0b_a[:], dr["l0_k0"].ap()[c * 128:(c + 1) * 128, :])
        dma("sync", DW, k0b_b[:], dr["l0_k0"].ap()[D + c * 128:D + (c + 1) * 128, :])
        P.wait("vector", DW)
        P.wait("vector", DG, P.mark(DG))
        P.op("vector", "tensor_scalar", ftmp[:], k0b_a[:], 2.0, None, ALU.mult)
        P.op("vector", "tensor_scalar", fb32[:], k0b_b[:, 0:2048], 2.0, None, ALU.mult)
        P.op("vector", "tensor_tensor", f16t[:, 0:2048], ftmp[:, 0:2048], fb32[:],
             ALU.subtract)
        P.op("vector", "tensor_scalar", fb32[:], k0b_b[:, 2048:4096], 2.0, None, ALU.mult)
        P.op("vector", "tensor_tensor", f16t[:, 2048:4096], ftmp[:, 2048:4096],
             fb32[:], ALU.subtract, inc=[(VV, 1)])
        P.wait("gpsimd", VV)
        dma("gpsimd", DG, w0scr.ap()[:, c, :, :].rearrange("s p m -> p s m"),
            f16t[:].rearrange("p (s m) -> p s m", m=256))
    for c in (range(32) if _PH >= 3 else ()):
        kb_buf = k0b_a if c % 2 == 0 else k0b_b
        P.wait("sync", VV, P.mark(VV))
        dma("sync", DW, kb_buf[:], dr["l0_k1"].ap()[c * 128:(c + 1) * 128, :])
        P.wait("vector", DW)
        P.wait("vector", DG, P.mark(DG))
        P.op("vector", "tensor_scalar", f16t[:], kb_buf[:], 2.0, -1.0, ALU.mult,
             ALU.add, inc=[(VV, 1)])
        P.wait("gpsimd", VV)
        dma("gpsimd", DG, w1scr.ap()[:, c, :, :].rearrange("s p m -> p s m"),
            f16t[:].rearrange("p (s m) -> p s m", m=128))
    for c in (range(32) if _PH >= 3 else ()):
        kb_buf = k0b_a if c % 2 == 0 else k0b_b
        P.wait("sync", VV, P.mark(VV))
        dma("sync", DW, kb_buf[:, 0:1024], dr["l0_k2"].ap()[c * 128:(c + 1) * 128, :])
        P.wait("vector", DW)
        P.wait("vector", DG, P.mark(DG))
        P.op("vector", "tensor_scalar", f16t[:, 0:1024], kb_buf[:, 0:1024], 2.0,
             -1.0, ALU.mult, ALU.add, inc=[(VV, 1)])
        P.wait("gpsimd", VV)
        dma("gpsimd", DG, w2scr.ap()[c], f16t[:, 0:1024])
    P.bar()

    w0b = [w0bt[:, 0:2048], w0bt[:, 2048:4096]]
    w1b = [w1bt[:, 0:4096], w1bt[:, 4096:8192]]
    w2b = [w2bt[:, 0:1024], w2bt[:, 1024:2048]]
    h1h = h1t[:, 0:256]
    h1l = h1t[:, 256:512]
    h0f = ffn_h0f[:, 0:FT]

    for t in (range(NFT) if _PH >= 4 else ()):
        def tsl(c, t=t):
            return slice(c * S + t * FT, c * S + (t + 1) * FT)
        for sidx in range(16):
            wb = w0b[sidx % 2]
            P.wait("sync", TT, P.mark(TT))
            dma("sync", DW, wb.rearrange("p (c m) -> p c m", m=256),
                w0scr.ap()[sidx].rearrange("c p m -> p c m"))
            P.wait("tensor", DW)
            for mcc in range(2):
                pb = PS[0]
                psl = slice(mcc * 128, mcc * 128 + 128) if False else slice(0, FT)
                P.wait("tensor", VV, P.mark(VV))
                for c in range(NCH):
                    lw = wb[:, c * 256 + mcc * 128: c * 256 + mcc * 128 + 128]
                    P.op("tensor", "matmul", pb[0:128, 0:FT], lw, xn_h[:, tsl(c)],
                         start=(c == 0), stop=False)
                    P.op("tensor", "matmul", pb[0:128, 0:FT], lw, xn_l[:, tsl(c)],
                         start=False, stop=(c == NCH - 1),
                         inc=([(TT, 1)] if c == NCH - 1 else ()))
                P.wait("vector", TT)
                mo = sidx * 2 + mcc
                P.op("vector", "tensor_scalar", h0f, pb[0:128, 0:FT],
                     s012b[:, 0:1], 0.0, ALU.mult, ALU.max, inc=[(VV, 1)])
                P.wait("scalar", VV)
                P.op("scalar", "activation", h0_h[:, mo * FT:(mo + 1) * FT], h0f,
                     AF.Copy, inc=[(SS, 1)])
                P.wait("vector", SS)
                P.op("vector", "tensor_tensor", h0_l[:, mo * FT:(mo + 1) * FT],
                     h0f, h0_h[:, mo * FT:(mo + 1) * FT], ALU.subtract,
                     inc=[(VV, 1)])
        P.bar(("tensor", "vector", "scalar", "sync"))
        if t == 0:
            tap("h0_t0", h0_h[:], (128, 32 * FT), F16)
        for s2 in range(32):
            wb = w1b[s2 % 2]
            w2bb = w2b[s2 % 2]
            P.wait("sync", TT, P.mark(TT))
            dma("sync", DW, wb.rearrange("p (c m) -> p c m", m=128),
                w1scr.ap()[s2].rearrange("c p m -> p c m"))
            dma("sync", DW, w2bb, w2scr.ap()[s2])
            P.wait("tensor", DW)
            pb = PS[1]
            P.wait("tensor", VV, P.mark(VV))
            P.wait("tensor", SS, P.mark(SS))
            for c in range(32):
                lw = wb[:, c * 128:(c + 1) * 128]
                P.op("tensor", "matmul", pb[0:128, 0:FT], lw,
                     h0_h[:, c * FT:(c + 1) * FT], start=(c == 0), stop=False)
                P.op("tensor", "matmul", pb[0:128, 0:FT], lw,
                     h0_l[:, c * FT:(c + 1) * FT], start=False, stop=(c == 31),
                     inc=([(TT, 1)] if c == 31 else ()))
            P.wait("vector", TT)
            P.op("vector", "tensor_scalar", h0f, pb[0:128, 0:FT],
                 s012b[:, 1:2], 0.0, ALU.mult, ALU.max, inc=[(VV, 1)])
            P.wait("scalar", VV)
            P.op("scalar", "activation", h1h[:, 0:FT], h0f, AF.Copy, inc=[(SS, 1)])
            P.wait("vector", SS)
            P.op("vector", "tensor_tensor", h1l[:, 0:FT], h0f, h1h[:, 0:FT],
                 ALU.subtract, inc=[(VV, 1)])
            P.wait("tensor", VV)
            for mc2 in range(NCH):
                yb = PS[4 + mc2 // 2]
                ysl = slice((mc2 % 2) * 256, (mc2 % 2) * 256 + FT)
                P.op("tensor", "matmul", yb[0:128, ysl],
                     w2bb[:, mc2 * 128:(mc2 + 1) * 128], h1h[:, 0:FT],
                     start=(s2 == 0 and mc2 % 2 == 0), stop=False)
                P.op("tensor", "matmul", yb[0:128, ysl],
                     w2bb[:, mc2 * 128:(mc2 + 1) * 128], h1l[:, 0:FT],
                     start=False, stop=(s2 == 31 and mc2 % 2 == 1),
                     inc=([(TT, 1)] if (s2 == 31 and mc2 == NCH - 1) else ()))
        P.wait("vector", TT)
        for mc2 in range(NCH):
            yb = PS[4 + mc2 // 2]
            ysl = slice((mc2 % 2) * 256, (mc2 % 2) * 256 + FT)
            xs = x_sb[:, tsl(mc2)]
            P.op("vector", "scalar_tensor_tensor", xs, yb[0:128, ysl],
                 s012b[:, 2:3], xs, ALU.mult, ALU.add, inc=[(VV, 1)])
        P.wait("tensor", VV)
        P.bar(("tensor", "vector", "scalar", "sync"))
    P.bar()
    tap("x_l0", x_sb[:], (128, NCH * S))

    if _PH < 5:
        dma("gpsimd", OUT, out.ap().rearrange("c p t -> p c t"),
            x_sb[:].rearrange("p (c t) -> p c t", t=S))
        P.wait("gpsimd", OUT)
        P.finalize()
        return nc

    emit_layernorm()

    dma("sync", DW, sels_sb[:], dr["sels"].ap())
    dma("sync", DW, stack4_sb[:], dr["stack4"].ap())
    dma("sync", DW, wlr_sb[:].rearrange("p (c j) -> p c j", j=32),
        dr["l1_wlr"].ap().rearrange("(c p) j -> p c j", p=128))
    dma("sync", DW, wbe_sb[:].rearrange("p (c j) -> p c j", j=32),
        dr["l1_wbeta"].ap().rearrange("(c p) j -> p c j", p=128))
    P.wait("vector", DW)
    for wf, wh_, wl_ in ((wlr_sb, wlr_h, wlr_l), (wbe_sb, wbe_h, wbe_l)):
        P.op("vector", "tensor_copy", wh_[:], wf[:])
        P.op("vector", "tensor_tensor", wl_[:], wf[:], wh_[:], ALU.subtract,
             inc=[(VV, 1)])
    P.wait("tensor", VV)
    for (wh_, wl_, scl, lrc_dst) in ((wlr_h, wlr_l, 0.001 / CH, lrc_a),
                                     (wbe_h, wbe_l, 0.9 / CH, lrc_b)):
        P.wait("scalar", VV, P.mark(VV))
        for tt4 in range(4):
            pb = PS[0]
            sl = slice(tt4 * 512, (tt4 + 1) * 512)
            P.wait("tensor", SS, P.mark(SS))
            for c in range(NCH):
                xh_ = xn_h[:, c * S + tt4 * 512: c * S + (tt4 + 1) * 512]
                xl_ = xn_l[:, c * S + tt4 * 512: c * S + (tt4 + 1) * 512]
                P.op("tensor", "matmul", pb[0:32, 0:512],
                     wh_[:, c * 32:(c + 1) * 32], xh_, start=(c == 0), stop=False)
                P.op("tensor", "matmul", pb[0:32, 0:512],
                     wh_[:, c * 32:(c + 1) * 32], xl_, start=False, stop=False)
                P.op("tensor", "matmul", pb[0:32, 0:512],
                     wl_[:, c * 32:(c + 1) * 32], xh_, start=False,
                     stop=(c == NCH - 1),
                     inc=([(TT, 1)] if c == NCH - 1 else ()))
            P.wait("scalar", TT)
            for pc8 in range(8):
                cck = tt4 * 8 + pc8
                P.op("scalar", "activation", lr_sb[0:32, 0:CH],
                     pb[0:32, pc8 * CH:(pc8 + 1) * CH], AF.Sigmoid,
                     accum_out=scan_t[0:32, cck:cck + 1],
                     inc=([(SS, 1)] if pc8 == 7 else ()))
        P.wait("vector", SS)
        P.op("vector", "tensor_scalar", lrc_dst[:], scan_t[0:32, 0:NCK],
             float(scl), None, ALU.mult, inc=[(VV, 1)])
    tap("lrc_aT", lrc_a[:], (32, NCK))
    tap("lrc_bT", lrc_b[:], (32, NCK))
    P.wait("tensor", VV)
    P.op("tensor", "matmul", PS[2][0:128, 0:NCK], stack4_sb[0:32, 0:128],
         lrc_a[:], start=True, stop=True)
    P.op("tensor", "matmul", PS[2][0:128, NCK:2 * NCK], stack4_sb[0:32, 0:128],
         lrc_b[:], start=True, stop=True, inc=[(TT, 1)])
    P.wait("vector", TT)
    P.op("vector", "tensor_copy", lrc4_sb[:], PS[2][0:128, 0:2 * NCK],
         inc=[(VV, 1)])
    P.wait("tensor", VV)
    for q in range(32):
        ty, j = divmod(q, 8)
        base = 32 * (q % 4)
        fb = 128 * (q // 4)
        src_ofs = NCK if ty in (0, 2) else 0
        P.op("tensor", "matmul", PS[3][0:128, 0:NCK],
             sels_sb[base:base + 32, fb:fb + 128],
             lrc4_sb[base:base + 32, src_ofs:src_ofs + NCK],
             start=True, stop=True, tile_position=(base, 0), inc=[(TT, 1)])
        P.wait("vector", TT)
        P.op("vector", "tensor_copy", scal[:, q * NCK:(q + 1) * NCK],
             PS[3][0:128, 0:NCK], inc=[(VV, 1)])
        P.wait("tensor", VV)
    P.bar()

    for c in range(NCH):
        P.wait("sync", SS, P.mark(SS))
        dma("sync", DW, land2[:], dr["l1_fw"].ap()[c * 128:(c + 1) * 128, :])
        P.wait("scalar", DW)
        P.op("scalar", "activation", u_t[:], land2[:], AF.Abs,
             accum_out=absr[:, c:c + 1], inc=[(SS, 1)])
    P.wait("scalar", SS)
    P.op("scalar", "activation", u_t[:, 0:8], absr[:, 0:8], AF.Copy,
         accum_out=absr[:, 8:9], inc=[(SS, 1)])
    P.wait("tensor", SS)
    P.op("tensor", "matmul", PS[0][0:1, 0:1], onesr[0:128, 0:1], absr[:, 8:9],
         start=True, stop=True, inc=[(TT, 1)])
    P.wait("vector", TT)
    P.op("vector", "tensor_scalar", ffn_s[0:1, 0:1], PS[0][0:1, 0:1],
         1.0 / (D * D), EPS_Q, ALU.mult, ALU.add, inc=[(VV, 1)])
    P.wait("vector", VV)
    P.op("vector", "tensor_scalar", ffn_s[0:1, 1:2], ffn_s[0:1, 0:1],
         1.0 / 127.0, None, ALU.mult)
    P.op("vector", "reciprocal", ffn_s[0:1, 2:3], ffn_s[0:1, 0:1])
    P.op("vector", "tensor_scalar", ffn_s[0:1, 3:4], ffn_s[0:1, 0:1],
         0.5, None, ALU.mult, inc=[(VV, 1)])
    P.wait("tensor", VV)
    P.op("tensor", "matmul", PS[1][0:128, 0:3], onesr[0:1, 0:128],
         ffn_s[0:1, 1:4], start=True, stop=True, inc=[(TT, 1)])
    P.wait("vector", TT)
    P.op("vector", "tensor_copy", wsb[:, 0:3], PS[1][0:128, 0:3], inc=[(VV, 1)])
    for c in range(NCH):
        P.wait("sync", VV, P.mark(VV))
        dma("sync", DW, land2[:], dr["l1_fw"].ap()[c * 128:(c + 1) * 128, :])
        P.wait("vector", DW)
        P.wait("vector", DG, P.mark(DG))
        P.op("vector", "tensor_scalar", g_t[:], land2[:], wsb[:, 2:3], None,
             ALU.is_ge)
        P.op("vector", "tensor_scalar", u_t[:], land2[:], -1.0, wsb[:, 2:3],
             ALU.mult, ALU.is_ge)
        P.op("vector", "tensor_tensor", wq16c[:], g_t[:], u_t[:], ALU.subtract,
             inc=[(VV, 1)])
        P.wait("gpsimd", VV)
        dma("gpsimd", DG, wqscr.ap()[c], wq16c[:])
        if c == 0:
            tap("wq_c0", wq16c[:], (128, D), F16)
    if "wq_all" in taps:
        t_ = nc.dram_tensor("tap_wq_all", [NCH, 128, D], F16, kind="ExternalOutput")
        P.bar()
        dma("gpsimd", OUT, t_.ap(), wqscr.ap())
        P.bar()
    P.bar()

    for dst, nm in ((st_win, "l1_winit"), (st_wout, "l1_wout"),
                    (st_woutT, "l1_woutT")):
        dma("sync", DW, dst[:].rearrange("p (j d) -> p j d", d=DH),
            dr[nm].ap().rearrange("j p d -> p j d"))
    for e in ("tensor", "vector"):
        P.wait(e, DW)
    P.bar()

    if _PH < 6:
        dma("gpsimd", OUT, out.ap().rearrange("c p t -> p c t"),
            x_sb[:].rearrange("p (c t) -> p c t", t=S))
        P.wait("gpsimd", OUT)
        P.finalize()
        return nc

    q_l1 = l1qkv[:, 0:S]
    k_l1 = l1qkv[:, S:2 * S]
    v_l1 = l1qkv[:, 2 * S:3 * S]
    o_l1 = l1_o[:, 0:S]
    wqb = [wqb0, wqb1]

    for t in range(NL1T):
        lands = (land3, land3b)
        pj_marks = {}
        for pi in range(3):
            for j in range(NG):
                idx = pi * NG + j
                if idx == 0:
                    dma("sync", DW, lands[0][:].rearrange("p (c m) -> p c m", m=128),
                        dr["l1_wqkv"].ap()[0, 0].rearrange("(c p) m -> p c m", p=128))
                    pj_marks[0] = P.mark(DW)
                if idx + 1 < 24:
                    pi2, j2 = divmod(idx + 1, NG)
                    P.wait("sync", VV, P.mark(VV))
                    dma("sync", DW,
                        lands[(idx + 1) % 2][:].rearrange("p (c m) -> p c m", m=128),
                        dr["l1_wqkv"].ap()[pi2, j2].rearrange("(c p) m -> p c m", p=128))
                    pj_marks[idx + 1] = P.mark(DW)
                lnd = lands[idx % 2][:]
                P.wait("vector", DW, pj_marks[idx])
                P.wait("vector", TT, P.mark(TT))
                P.op("vector", "tensor_copy", w3h[:], lnd)
                P.op("vector", "tensor_tensor", w3l[:], lnd, w3h[:], ALU.subtract,
                     inc=[(VV, 1)])
                P.wait("tensor", VV)
                pb = PS[0]
                P.wait("tensor", SS, P.mark(SS))
                for c in range(NCH):
                    xh_ = xn_h[:, c * S + t * L1T: c * S + (t + 1) * L1T]
                    xl_ = xn_l[:, c * S + t * L1T: c * S + (t + 1) * L1T]
                    P.op("tensor", "matmul", pb[0:128, 0:L1T],
                         w3h[:, c * 128:(c + 1) * 128], xh_, start=(c == 0),
                         stop=False)
                    P.op("tensor", "matmul", pb[0:128, 0:L1T],
                         w3h[:, c * 128:(c + 1) * 128], xl_, start=False, stop=False)
                    P.op("tensor", "matmul", pb[0:128, 0:L1T],
                         w3l[:, c * 128:(c + 1) * 128], xh_, start=False,
                         stop=(c == NCH - 1),
                         inc=([(TT, 1)] if c == NCH - 1 else ()))
                P.wait("scalar", TT)
                dest = (q_l1, k_l1, v_l1)[pi]
                fn = AF.Silu if pi < 2 else AF.Copy
                P.op("scalar", "activation", dest[:, j * L1T:(j + 1) * L1T],
                     pb[0:128, 0:L1T], fn, inc=[(SS, 1)])
        P.bar(("tensor", "vector", "scalar", "sync"))
        if t == 0:
            tap("q_l1_t0", q_l1, (128, S))
        for cc in (range(CPT) if _PH >= 7 else ()):
            c_g = t * CPT + cc

            def csl(j, cc=cc):
                return slice(j * L1T + cc * CH, j * L1T + (cc + 1) * CH)

            P.wait("tensor", VV, P.mark(VV))
            for j in range(NG):
                kb_map = (PS[0], PS[1], PS[6], PS[7])
                for h in range(2):
                    hs = slice(h * 64, (h + 1) * 64)
                    P.op("tensor", "transpose", kb_map[2 * h][0:64, 0:64],
                         k_l1[hs, csl(j)], ident[hs, h * 64:h * 64 + 64])
                    P.op("tensor", "transpose", kb_map[2 * h + 1][0:64, 0:64],
                         v_l1[hs, csl(j)], ident[hs, h * 64:h * 64 + 64],
                         inc=([(TT, 1)] if h == 1 else ()))
                P.wait("vector", TT)
                for h in range(2):
                    hs = slice(h * 64, (h + 1) * 64)
                    P.op("vector", "tensor_copy", knd[hs, j * DH:(j + 1) * DH],
                         kb_map[2 * h][0:64, 0:64])
                    P.op("vector", "tensor_copy", vnd[hs, j * DH:(j + 1) * DH],
                         kb_map[2 * h + 1][0:64, 0:64],
                         inc=([(VV, 1)] if h == 1 else ()))
                P.wait("tensor", VV)
            for j in (range(NG) if _SC >= 1 else ()):
                zb = PS[2 + j // 4]
                zo = (j % 4) * 128
                for h in range(2):
                    hs = slice(h * 64, (h + 1) * 64)
                    P.op("tensor", "matmul", zb[hs, zo:zo + 64],
                         st_win[hs, j * DH:(j + 1) * DH], q_l1[hs, csl(j)],
                         start=True, stop=True,
                         tile_position=(h * 64, h * 64))
                    P.op("tensor", "matmul", zb[hs, zo + 64:zo + 128],
                         st_win[hs, j * DH:(j + 1) * DH], k_l1[hs, csl(j)],
                         start=True, stop=True,
                         tile_position=(h * 64, h * 64),
                         inc=([(TT, 1)] if (j % 4 == 3 and h == 1) else ()))
            P.wait("vector", TT)
            for j in (range(NG) if _SC >= 2 else ()):
                zb = PS[2 + j // 4]
                zo = (j % 4) * 128
                for qk in range(2):
                    idx = j * 2 + qk
                    P.op("vector", "reduce_max", sm_s[:, idx:idx + 1],
                         zb[0:128, zo + qk * 64: zo + qk * 64 + 64], axis=AX.X,
                         negate=True, inc=([(VV, 1)] if idx == 15 else ()))
            P.wait("scalar", VV)
            P.wait("scalar", TT, P.mark(TT))
            for j in (range(NG) if _SC >= 2 else ()):
                zb = PS[2 + j // 4]
                zo = (j % 4) * 128
                for qk in range(2):
                    idx = j * 2 + qk
                    eb = PS[4 + idx // 8]
                    eo = (idx % 8) * 64
                    P.op("scalar", "activation", eb[0:128, eo:eo + 64],
                         zb[0:128, zo + qk * 64: zo + qk * 64 + 64], AF.Exp,
                         bias=sm_s[:, idx:idx + 1], scale=1.0,
                         accum_out=sm_s[:, 32 + idx:33 + idx],
                         inc=([(SS, 1)] if idx == 15 else ()))
            P.wait("vector", SS)
            if _SC >= 2:
                P.op("vector", "reciprocal", sm_s[:, 48:64], sm_s[:, 32:48],
                     inc=[(VV, 1)])
                P.wait("vector", VV)
            for j in (range(NG) if _SC >= 2 else ()):
                for qk in range(2):
                    idx = j * 2 + qk
                    eb = PS[4 + idx // 8]
                    eo = (idx % 8) * 64
                    P.op("vector", "tensor_scalar",
                         pq_sb[:, idx * DH:(idx + 1) * DH],
                         eb[0:128, eo:eo + 64],
                         sm_s[:, 48 + idx:49 + idx], None, ALU.mult,
                         inc=([(VV, 1)] if idx == 15 else ()))
            P.wait("tensor", VV)
            if t == 0 and cc == 0:
                tap("sc_pq", pq_sb[:], (128, 1024))
                tap("sc_knd", knd[:], (128, 512))
                tap("sc_vnd", vnd[:], (128, 512))
                tap("sc_smax", sm_s[:, 0:16], (128, 16))
                tap("sc_ssum", sm_s[:, 32:64], (128, 32))
            for j in (range(NG) if _SC >= 3 else ()):
                pq_t = pq_sb[:, (2 * j) * DH:(2 * j) * DH + DH]
                p_t = pq_sb[:, (2 * j + 1) * DH:(2 * j + 1) * DH + DH]
                for h in range(2):
                    hs = slice(h * 64, (h + 1) * 64)
                    P.op("tensor", "matmul", PS[4][hs, 0:64],
                         st_wout[hs, j * DH:(j + 1) * DH], pq_t[hs, 0:DH],
                         start=True, stop=True, tile_position=(h * 64, h * 64))
                    P.op("tensor", "matmul", PS[5][hs, 0:64],
                         v_l1[hs, csl(j)], st_woutT[hs, j * DH:(j + 1) * DH],
                         start=True, stop=True, tile_position=(h * 64, h * 64))
                    P.op("tensor", "transpose",
                         (PS[6] if h == 0 else PS[7])[0:64, 0:64],
                         p_t[hs, 0:DH], ident[hs, h * 64:h * 64 + 64],
                         inc=([(TT, 1)] if h == 1 else ()))
                P.wait("vector", TT)
                P.op("vector", "tensor_copy",
                     o_l1[:, j * L1T + cc * CH: j * L1T + (cc + 1) * CH],
                     PS[4][0:128, 0:64])
                P.op("vector", "tensor_copy", ptb[0:64, j * DH:(j + 1) * DH],
                     PS[6][0:64, 0:64])
                P.op("vector", "tensor_copy", ptb[64:128, j * DH:(j + 1) * DH],
                     PS[7][0:64, 0:64])
                P.op("vector", "tensor_tensor",
                     gzb[:, j * DH:(j + 1) * DH], ptb[:, j * DH:(j + 1) * DH],
                     PS[5][0:128, 0:64], ALU.mult)
                P.op("vector", "reduce_sum", sm_s[:, j:j + 1],
                     gzb[:, j * DH:(j + 1) * DH], axis=AX.X, inc=[(VV, 1)])
                P.wait("vector", VV)
                P.op("vector", "tensor_scalar", scan_t[:, 0:64],
                     ptb[:, j * DH:(j + 1) * DH], sm_s[:, j:j + 1], None, ALU.mult)
                P.op("vector", "tensor_tensor", gzb[:, j * DH:(j + 1) * DH],
                     scan_t[:, 0:64], gzb[:, j * DH:(j + 1) * DH], ALU.subtract,
                     inc=[(VV, 1)])
                P.wait("tensor", VV)
            if t == 0 and cc == 0:
                tap("sc_ptb", ptb[:], (128, 512))
                tap("sc_gzb", gzb[:], (128, 512))
                tap("sc_ol", o_l1[:, 0:64], (128, 64))
            for j in (range(NG) if _SC >= 4 else ()):
                for h in range(2):
                    hs = slice(h * 64, (h + 1) * 64)
                    P.op("tensor", "matmul", PS[4][hs, 0:64],
                         ptb[hs, j * DH:(j + 1) * DH], vnd[hs, j * DH:(j + 1) * DH],
                         start=True, stop=True, tile_position=(h * 64, h * 64))
                    P.op("tensor", "matmul", PS[5][hs, 0:64],
                         vnd[hs, j * DH:(j + 1) * DH], ptb[hs, j * DH:(j + 1) * DH],
                         start=True, stop=True, tile_position=(h * 64, h * 64))
                    P.op("tensor", "matmul", PS[6][hs, 0:64],
                         knd[hs, j * DH:(j + 1) * DH], gzb[hs, j * DH:(j + 1) * DH],
                         start=True, stop=True, tile_position=(h * 64, h * 64),
                         inc=([(TT, 1)] if h == 1 else ()))
                P.wait("vector", TT)
                jc = j * NCK + c_g
                sl_ = slice(j * DH, (j + 1) * DH)
                P.op("vector", "tensor_scalar", st_wout[:, sl_], st_wout[:, sl_],
                     scal[:, 16 * NCK + jc:16 * NCK + jc + 1], None, ALU.mult)
                P.op("vector", "scalar_tensor_tensor", st_wout[:, sl_],
                     PS[4][0:128, 0:64],
                     scal[:, 24 * NCK + jc:24 * NCK + jc + 1],
                     st_wout[:, sl_], ALU.mult, ALU.add)
                P.op("vector", "tensor_scalar", st_woutT[:, sl_], st_woutT[:, sl_],
                     scal[:, 16 * NCK + jc:16 * NCK + jc + 1], None, ALU.mult)
                P.op("vector", "scalar_tensor_tensor", st_woutT[:, sl_],
                     PS[5][0:128, 0:64],
                     scal[:, 24 * NCK + jc:24 * NCK + jc + 1],
                     st_woutT[:, sl_], ALU.mult, ALU.add)
                P.op("vector", "tensor_scalar", st_win[:, sl_], st_win[:, sl_],
                     scal[:, 0 * NCK + jc:0 * NCK + jc + 1], None, ALU.mult)
                P.op("vector", "scalar_tensor_tensor", st_win[:, sl_],
                     PS[6][0:128, 0:64],
                     scal[:, 8 * NCK + jc:8 * NCK + jc + 1],
                     st_win[:, sl_], ALU.mult, ALU.add, inc=[(VV, 1)])
                P.wait("tensor", VV)
        if t == 0:
            tap("o_l1_t0", o_l1[:, 0:S], (128, S))
        if t == 1:
            tap("o_l1_t1", o_l1[:, 0:S], (128, S))
            tap("stwin_t1", st_win[:], (128, NG * DH))
        if t == 0:
            tap("sc_win", st_win[:], (128, 512))
        P.wait("tensor", VV, P.mark(VV))
        P.wait("tensor", SS, P.mark(SS))
        for j in (range(NG) if _PH >= 8 else ()):
            P.wait("sync", VV, P.mark(VV))
            dma("sync", DW, land3[:], dr["l1_wo"].ap()[j])
            P.wait("vector", DW)
            P.wait("vector", TT, P.mark(TT))
            P.op("vector", "tensor_copy", w3h[:], land3[:])
            P.op("vector", "tensor_tensor", w3l[:], land3[:], w3h[:], ALU.subtract)
            P.op("vector", "tensor_copy", ojh[:, 0:L1T], o_l1[:, j * L1T:(j + 1) * L1T])
            P.op("vector", "tensor_tensor", ojl[:, 0:L1T],
                 o_l1[:, j * L1T:(j + 1) * L1T], ojh[:, 0:L1T], ALU.subtract,
                 inc=[(VV, 1)])
            P.wait("tensor", VV)
            for mc in range(NCH):
                yb = PS[4 + mc // 2]
                ysl = slice((mc % 2) * 256, (mc % 2) * 256 + L1T)
                P.op("tensor", "matmul", yb[0:128, ysl],
                     w3h[:, mc * 128:(mc + 1) * 128], ojh[:, 0:L1T],
                     start=(j == 0 and mc % 2 == 0), stop=False)
                P.op("tensor", "matmul", yb[0:128, ysl],
                     w3h[:, mc * 128:(mc + 1) * 128], ojl[:, 0:L1T],
                     start=False, stop=False)
                P.op("tensor", "matmul", yb[0:128, ysl],
                     w3l[:, mc * 128:(mc + 1) * 128], ojh[:, 0:L1T],
                     start=False, stop=(j == NG - 1 and mc % 2 == 1),
                     inc=([(TT, 1)] if mc == NCH - 1 else ()))
        P.wait("vector", TT)
        for mc in (range(NCH) if _PH >= 8 else ()):
            yb = PS[4 + mc // 2]
            ysl = slice((mc % 2) * 256, (mc % 2) * 256 + L1T)
            xs = x_sb[:, mc * S + t * L1T: mc * S + (t + 1) * L1T]
            P.op("vector", "tensor_tensor", xs, xs, yb[0:128, ysl], ALU.add,
                 inc=[(VV, 1)])
        P.wait("tensor", VV)
        P.bar(("tensor", "vector", "scalar", "sync"))
        if t == 0:
            tap("x_oproj_t0", x_sb[:], (128, NCH * S))
        if _PH < 9:
            continue
        am = sm_s[:, 0:2]
        r1 = sm_s[:, 4:6]
        sc2 = sm_s[:, 8:10]
        b2s = ffn_s[:, 0:256]
        t3 = ffn_s[:, 256:384]
        aqp = aqp16[:, 0:128]
        srow = ffn_s[0:1, 512:768]
        for sub in range(2):
            for kb in range(NCH):
                tok = slice(kb * S + t * L1T + sub * 128,
                            kb * S + t * L1T + sub * 128 + 128)
                P.wait("vector", TT, P.mark(TT))
                P.op("vector", "tensor_tensor", t3, xn_h[:, tok], xn_l[:, tok],
                     ALU.add, inc=[(VV, 1)])
                P.wait("tensor", VV)
                P.op("tensor", "transpose", PS[1][0:128, 0:128], t3, ident[:],
                     inc=[(TT, 1)])
                P.wait("vector", TT)
                P.op("vector", "tensor_copy",
                     xt_sub[:, kb * 128:(kb + 1) * 128], PS[1][0:128, 0:128],
                     inc=[(VV, 1)])
            P.op("vector", "reduce_max", am[:, sub:sub + 1], xt_sub[:],
                 axis=AX.X, apply_absolute_value=True, inc=[(VV, 1)])
            P.wait("vector", VV)
            P.op("vector", "tensor_scalar", am[:, sub:sub + 1],
                 am[:, sub:sub + 1], EPS_Q, None, ALU.add)
            P.op("vector", "reciprocal", r1[:, sub:sub + 1], am[:, sub:sub + 1],
                 inc=[(VV, 1)])
            P.wait("vector", VV)
            P.op("vector", "tensor_scalar", r1[:, sub:sub + 1],
                 r1[:, sub:sub + 1], 127.0, None, ALU.mult)
            P.op("vector", "tensor_scalar", sc2[:, sub:sub + 1],
                 am[:, sub:sub + 1], wsb[:, 0:1], None, ALU.mult, inc=[(VV, 1)])
            P.wait("vector", VV)
            if t == 0 and sub == 0:
                tap("f_sc2e", sm_s[:, 8:10], (128, 2))
                tap("f_wsb", wsb[:, 0:3], (128, 3))
            P.op("vector", "tensor_scalar", xt_sub[:], xt_sub[:],
                 r1[:, sub:sub + 1], None, ALU.mult, inc=[(VV, 1)])
            P.wait("vector", VV)
            P.op("vector", "tensor_scalar", xt_sub[:], xt_sub[:],
                 127.0, -128.0, ALU.min, ALU.max, inc=[(VV, 1)])
            P.wait("vector", VV)
            P.op("vector", "tensor_scalar", rnd_t[:], xt_sub[:], 128.5, None,
                 ALU.add, inc=[(VV, 1)])
            P.wait("vector", VV)
            for b_ in (128.0, 64.0, 32.0, 16.0, 8.0, 4.0, 2.0, 1.0):
                P.op("vector", "tensor_single_scalar", rnd_m[:], rnd_t[:], b_,
                     ALU.is_ge, inc=[(VV, 1)])
                P.wait("vector", VV)
                P.op("vector", "scalar_tensor_tensor", rnd_t[:], rnd_m[:], -b_,
                     rnd_t[:], ALU.mult, ALU.add, inc=[(VV, 1)])
                P.wait("vector", VV)
            P.op("vector", "tensor_scalar", xt_sub[:], xt_sub[:], 0.5, None,
                 ALU.add, inc=[(VV, 1)])
            P.wait("vector", VV)
            P.op("vector", "tensor_tensor", xt_sub[:], xt_sub[:], rnd_t[:],
                 ALU.subtract, inc=[(VV, 1)])
            if t == 0 and sub == 0:
                tap("aq_t0", xt_sub[:], (128, 1024))
            if t == 0 and sub == 1:
                tap("aq_t0s1", xt_sub[:], (128, 1024))
            for kb in range(NCH):
                wq_ = wqb[kb % 2]
                P.wait("sync", TT, P.mark(TT))
                dma("sync", DW, wq_[:], wqscr.ap()[kb])
                P.wait("tensor", VV)
                P.op("tensor", "transpose", PS[1][0:128, 0:128],
                     xt_sub[:, kb * 128:(kb + 1) * 128], ident[:], inc=[(TT, 1)])
                P.wait("vector", TT)
                P.op("vector", "tensor_copy", aqp, PS[1][0:128, 0:128],
                     inc=[(VV, 1)])
                P.wait("tensor", DW)
                P.wait("tensor", VV)
                if sub == 0 and kb == 0:
                    P.wait("tensor", SS, P.mark(SS))
                for mc in range(NCH):
                    yb = PS[4 + mc // 2]
                    ysl = slice((mc % 2) * 256 + sub * 128,
                                (mc % 2) * 256 + sub * 128 + 128)
                    P.op("tensor", "matmul", yb[0:128, ysl],
                         wq_[:, mc * 128:(mc + 1) * 128], aqp,
                         start=(sub == 0 and kb == 0 and mc % 2 == 0),
                         stop=(sub == 1 and kb == NCH - 1 and mc % 2 == 1),
                         inc=([(TT, 1)] if mc == NCH - 1 else ()))
            P.op("tensor", "transpose", PS[2 + sub][0:1, 0:128],
                 sc2[:, sub:sub + 1], ident[:], inc=[(TT, 1)])
            P.wait("vector", TT)
            P.op("vector", "tensor_copy", srow[0:1, sub * 128:sub * 128 + 128],
                 PS[2 + sub][0:1, 0:128], inc=[(VV, 1)])
            P.wait("tensor", VV)
        for sub in range(2):
            P.op("tensor", "matmul", PS[2][0:128, 0:128], onesr[0:1, 0:128],
                 srow[0:1, sub * 128:sub * 128 + 128], start=True, stop=True,
                 inc=[(TT, 1)])
            P.wait("vector", TT)
            P.op("vector", "tensor_copy", b2s[:, sub * 128:sub * 128 + 128],
                 PS[2][0:128, 0:128], inc=[(VV, 1)])
            P.wait("tensor", VV)
        for mc in range(NCH):
            yb = PS[4 + mc // 2]
            for sub in range(2):
                ysl = slice((mc % 2) * 256 + sub * 128,
                            (mc % 2) * 256 + sub * 128 + 128)
                P.op("vector", "tensor_tensor", t3,
                     b2s[:, sub * 128:sub * 128 + 128], yb[0:128, ysl], ALU.mult)
                xs = x_sb[:, mc * S + t * L1T + sub * 128:
                          mc * S + t * L1T + sub * 128 + 128]
                P.op("vector", "tensor_tensor", xs, xs, t3, ALU.add,
                     inc=[(VV, 1)])
        P.wait("tensor", VV)
        if t == 0:
            tap("f_b2s", ffn_s[:, 0:256], (128, 256))
            tap("f_srow", ffn_s[0:1, 512:768], (1, 256))
            tap("f_sc2", sm_s[:, 8:10], (128, 2))
        P.bar(("tensor", "vector", "scalar", "sync"))
        if t == 0:
            tap("x_tile0", x_sb[:], (128, NCH * S))
    P.bar()

    dma("gpsimd", OUT, out.ap().rearrange("c p t -> p c t"),
        x_sb[:].rearrange("p (c t) -> p c t", t=S))
    P.wait("gpsimd", OUT)
    P.finalize()
    return nc


def kernel(**inputs):
    cm, pc = _host_stage(inputs)
    nc = build_program(_TAPS)
    in_maps = []
    for b in range(8):
        m = dict(cm)
        m.update(pc[b])
        in_maps.append(m)
    res = run_bass_kernel_spmd(nc, in_maps, core_ids=list(range(8)))
    outs = []
    for b in range(8):
        o = np.asarray(res.results[b]["out"])
        outs.append(np.ascontiguousarray(o.reshape(D, S).T))
    return np.stack(outs)


# version 05
# speedup vs baseline: 1.0353x; 1.0353x over previous
import contextlib
import numpy as np

import concourse.bass as bass
import concourse.mybir as mybir
from concourse.bass_utils import run_bass_kernel_spmd

F32 = mybir.dt.float32
F16 = mybir.dt.float16
I32 = mybir.dt.int32
U8 = mybir.dt.uint8
AF = mybir.ActivationFunctionType
ALU = mybir.AluOpType
AX = mybir.AxisListType

D = 1024
S = 2048
NCH = 8
DH = 64
WIN = 512
SCALE = DH ** -0.5
NG = 8
KS = 4
FT = 256
NFT = S // FT
L1T = 256
NL1T = S // L1T
CH = 64
NCK = S // CH
CPT = L1T // CH
EPS_LN = 1e-5
EPS_L2 = 1e-12
EPS_Q = 1e-8

ENGS = ("sync", "tensor", "vector", "scalar", "gpsimd")

_TAPS = ()


class Prog:
    def __init__(self, nc):
        self.nc = nc
        self.q = {e: [] for e in ENGS}
        self.cnt = {}
        self.order = []

    def sem(self, name):
        if name not in self.cnt:
            self.cnt[name] = 0
            self.order.append(name)
        return name

    def op(self, eng, meth, *args, inc=(), **kw):
        incs = []
        for s, k in inc:
            self.sem(s)
            self.cnt[s] += k
            incs.append((s, k))
        self.q[eng].append(("op", meth, args, kw, incs))

    def wait(self, eng, semname, val=None):
        self.sem(semname)
        v = self.cnt[semname] if val is None else val
        if v > 0:
            self.q[eng].append(("wait", semname, v))

    def mark(self, semname):
        self.sem(semname)
        return self.cnt[semname]

    def _last_real(self, eng):
        for item in reversed(self.q[eng]):
            if item[0] == "op" and item[1] not in ("nop", "dma_start"):
                return item
        return None

    def bump(self, eng, sem):
        self.sem(sem)
        item = self._last_real(eng)
        if item is not None and any(s0 == sem for s0, _ in item[4]):
            return
        if item is not None and not item[4]:
            self.cnt[sem] += 1
            item[4].append((sem, 1))
            return
        if item is None:
            self.cnt[sem] += 1
            self.q[eng].append(("op", "nop", (), {}, [(sem, 1)]))
            return
        self.cnt[sem] += 1
        if eng == "vector":
            self.q[eng].append(("op", "memset", (self.fence_buf[0:1, 0:1], 0.0),
                                {}, [(sem, 1)]))
        elif eng == "scalar":
            self.q[eng].append(("op", "copy",
                                (self.fence_buf[0:1, 1:2], self.fence_buf[0:1, 0:1]),
                                {}, [(sem, 1)]))
        elif eng == "gpsimd":
            self.q[eng].append(("op", "memset", (self.fence_buf[0:1, 2:3], 0.0),
                                {}, [(sem, 1)]))
        else:
            raise RuntimeError("no filler op for engine %s (sem %s)" % (eng, sem))

    def _fence(self, eng):
        item = self._last_real(eng)
        if item is not None and item[4]:
            s0, _ = item[4][-1]
            return (s0, self.cnt[s0])
        s0 = "fn_" + eng
        self.sem(s0)
        self.cnt[s0] += 1
        if item is not None:
            item[4].append((s0, 1))
        else:
            self.q[eng].append(("op", "nop", (), {}, [(s0, 1)]))
        return (s0, self.cnt[s0])

    def bar(self, engs=ENGS):
        fences = {e: self._fence(e) for e in engs}
        dw = self.cnt.get("dw", 0)
        dg = self.cnt.get("dg", 0)
        for e in engs:
            for e2 in engs:
                if e2 != e:
                    self.wait(e, *fences[e2])
            self.wait(e, "dw", dw)
            self.wait(e, "dg", dg)

    def finalize(self):
        nc = self.nc
        with contextlib.ExitStack() as st:
            handles = {n: st.enter_context(nc.semaphore(n)) for n in self.order}
            blk = st.enter_context(nc.Block())

            def make_runner(lst):
                def runner(E):
                    for item in lst:
                        if item[0] == "op":
                            _, meth, args, kw, incs = item
                            r = getattr(E, meth)(*args, **kw)
                            for s, k in incs:
                                r = r.then_inc(handles[s], k)
                        else:
                            _, s, v = item
                            E.wait_ge(handles[s], v)
                return runner

            for eng in ENGS:
                getattr(blk, eng)(make_runner(self.q[eng]))



def _host_consts():
    c = {}
    inv_freq = (1.0 / 10000.0 ** (np.arange(0, DH, 2, dtype=np.float32) / DH)).astype(np.float32)
    t = np.arange(S, dtype=np.float32)
    freqs = t[:, None] * inv_freq[None, :]
    emb = np.concatenate([freqs, freqs], axis=-1).astype(np.float32)
    cos = np.cos(emb).astype(np.float32)
    sin = np.sin(emb).astype(np.float32)
    ssin = sin.copy()
    ssin[:, :DH // 2] *= -1.0
    c["cosT"] = np.ascontiguousarray(np.tile(cos.T, (2, 1)))
    c["ssinT"] = np.ascontiguousarray(np.tile(ssin.T, (2, 1)))
    i = np.arange(128)
    c["maskd"] = (i[None, :] >= i[:, None]).astype(np.float32)
    c["mask4"] = (i[None, :] < i[:, None]).astype(np.float32)
    c["ident"] = np.eye(128, dtype=np.float32)
    c["ident16"] = np.eye(128, dtype=np.float16)
    c["onesr"] = np.ones((128, 128), dtype=np.float32)
    ind2 = np.zeros((128, 2), dtype=np.float32)
    ind2[:64, 0] = 1.0
    ind2[64:, 1] = 1.0
    c["ind2"] = ind2
    c["ind2T"] = np.ascontiguousarray(ind2.T)
    stack4 = np.zeros((32, 128), dtype=np.float32)
    for b in range(4):
        stack4[:, 32 * b:32 * (b + 1)] = np.eye(32, dtype=np.float32)
    c["stack4"] = stack4
    sels = np.zeros((128, 8 * 128), dtype=np.float32)
    for q in range(32):
        ty, j = divmod(q, 8)
        base = 32 * (q % 4)
        fb = 128 * (q // 4)
        sgn = -1.0 if ty == 1 else 1.0
        off = 0 if ty in (0, 1) else 16
        for p in range(128):
            r = off + 2 * j + (1 if p >= 64 else 0)
            sels[base + r, fb + p] = sgn
    c["sels"] = sels
    return c


def _host_stage(inputs):
    f = np.ascontiguousarray
    cm = dict(_host_consts())

    cm["l0_wq"] = f(inputs["L0_attn_Wq"].T.reshape(D, NG, 128).transpose(1, 0, 2))
    cm["l0_wk"] = f(inputs["L0_attn_Wk"].T.reshape(D, NG, 128).transpose(1, 0, 2))
    cm["l0_wv"] = f(inputs["L0_attn_Wv"].T.reshape(D, NG, 128).transpose(1, 0, 2))
    cm["l0_wo"] = f(inputs["L0_attn_Wo"].T.reshape(NG, 128, D))
    for nm, key in (("l0_bq", "L0_attn_bq"), ("l0_bk", "L0_attn_bk"),
                    ("l0_bv", "L0_attn_bv"), ("l0_bo", "L0_attn_bo"),
                    ("l0_cqb", "L0_attn_cqb"), ("l0_ckb", "L0_attn_ckb"),
                    ("l0_cvb", "L0_attn_cvb")):
        cm[nm] = f(inputs[key].reshape(NG, 128).T)
    for nm, key in (("l0_cq", "L0_attn_cqw"), ("l0_ck", "L0_attn_ckw"),
                    ("l0_cv", "L0_attn_cvw")):
        cm[nm] = f(inputs[key].reshape(NG, 128, KS).transpose(1, 0, 2).reshape(128, NG * KS))

    cm["l0_k0"] = f(inputs["L0_ffn_k0"].view(np.uint8))
    cm["l0_k1"] = f(inputs["L0_ffn_k1"].view(np.uint8))
    cm["l0_k2"] = f(inputs["L0_ffn_k2"].view(np.uint8))
    cm["l0_s"] = f(np.stack([np.asarray(inputs["L0_ffn_s0"]),
                             np.asarray(inputs["L0_ffn_s1"]),
                             np.asarray(inputs["L0_ffn_s2"])]).reshape(1, 3).astype(np.float32))

    qkv = np.stack([inputs["L1_attn_Wq"].T, inputs["L1_attn_Wk"].T,
                    inputs["L1_attn_Wv"].T])
    cm["l1_wqkv"] = f(qkv.reshape(3, D, NG, 128).transpose(0, 2, 1, 3))
    cm["l1_wo"] = f(inputs["L1_attn_Wo"].T.reshape(NG, 128, D))
    cm["l1_wlr"] = f(inputs["L1_attn_Wlr"].T)
    cm["l1_wbeta"] = f(inputs["L1_attn_Wbeta"].T)
    wii = inputs["L1_attn_W_in_init"][0]
    woi = inputs["L1_attn_W_out_init"][0]
    cm["l1_winit"] = f(wii.transpose(1, 2, 0).reshape(NG, 2 * DH, DH))
    cm["l1_wout"] = f(woi.transpose(1, 0, 2).reshape(NG, 2 * DH, DH))
    cm["l1_woutT"] = f(woi.transpose(1, 2, 0).reshape(NG, 2 * DH, DH))
    cm["l1_fw"] = f(inputs["L1_ffn_weight"].T)

    x = inputs["L0_attn_x"]
    pc = [{"x_in": f(x[b].T.reshape(NCH, 128, S))} for b in range(8)]
    return cm, pc



def build_program(taps=()):
    import os as _os
    _PH = int(_os.environ.get("KERNEL_PHASES", "9"))
    _SC = int(_os.environ.get("KERNEL_SCAN", "5"))
    nc = bass.Bass("TRN2", target_bir_lowering=False,
                   detect_race_conditions=not _os.environ.get("KERNEL_NO_RD"))
    taps = set(taps)

    dr = {}

    def din(name, shape, dt=F32):
        dr[name] = nc.dram_tensor(name, list(shape), dt, kind="ExternalInput")

    din("x_in", (NCH, 128, S))
    din("cosT", (128, S)); din("ssinT", (128, S))
    din("maskd", (128, 128)); din("mask4", (128, 128))
    din("ident", (128, 128)); din("ident16", (128, 128), F16)
    din("onesr", (128, 128))
    din("ind2", (128, 2)); din("ind2T", (2, 128))
    din("stack4", (32, 128)); din("sels", (128, 1024))
    din("l0_wq", (NG, D, 128)); din("l0_wk", (NG, D, 128)); din("l0_wv", (NG, D, 128))
    din("l0_wo", (NG, 128, D))
    for nm in ("l0_bq", "l0_bk", "l0_bv", "l0_bo", "l0_cqb", "l0_ckb", "l0_cvb"):
        din(nm, (128, NG))
    for nm in ("l0_cq", "l0_ck", "l0_cv"):
        din(nm, (128, NG * KS))
    din("l0_k0", (2 * D, 4096), U8)
    din("l0_k1", (4096, 4096), U8)
    din("l0_k2", (4096, D), U8)
    din("l0_s", (1, 3))
    din("l1_wqkv", (3, NG, D, 128)); din("l1_wo", (NG, 128, D))
    din("l1_wlr", (D, 32)); din("l1_wbeta", (D, 32))
    din("l1_winit", (NG, 128, DH)); din("l1_wout", (NG, 128, DH)); din("l1_woutT", (NG, 128, DH))
    din("l1_fw", (D, D))

    out = nc.dram_tensor("out", [NCH, 128, S], F32, kind="ExternalOutput")
    w0scr = nc.dram_tensor("w0scr", [16, NCH, 128, 256], F16, kind="Internal")
    w1scr = nc.dram_tensor("w1scr", [32, 32, 128, 128], F16, kind="Internal")
    w2scr = nc.dram_tensor("w2scr", [32, 128, D], F16, kind="Internal")
    wqscr = nc.dram_tensor("wqscr", [NCH, 128, D], F16, kind="Internal")

    x_sb = nc.alloc_sbuf_tensor("x_sb", [128, NCH * S], F32)
    xn_h = nc.alloc_sbuf_tensor("xn_h", [128, NCH * S], F16)
    xn_l = nc.alloc_sbuf_tensor("xn_l", [128, NCH * S], F16)
    ident = nc.alloc_sbuf_tensor("ident_b", [128, 128], F32)
    ident16 = nc.alloc_sbuf_tensor("ident16_b", [128, 128], F16)
    onesr = nc.alloc_sbuf_tensor("onesr_b", [128, 128], F32)
    bias = {}
    for nm in ("l0_bq", "l0_bk", "l0_bv", "l0_bo", "l0_cqb", "l0_ckb", "l0_cvb"):
        bias[nm] = nc.alloc_sbuf_tensor(nm + "_s", [128, NG], F32)
    convw = {}
    for nm in ("l0_cq", "l0_ck", "l0_cv"):
        convw[nm] = nc.alloc_sbuf_tensor(nm + "_s", [128, NG * KS], F32)
    s012 = nc.alloc_sbuf_tensor("s012_b", [1, 3], F32)
    s012b = nc.alloc_sbuf_tensor("s012b", [128, 3], F32)
    fence_buf = nc.alloc_sbuf_tensor("fence_buf", [1, 4], F32)
    wsb = nc.alloc_sbuf_tensor("wsb_p", [128, 8], F32)

    AR0 = ((nc.sbuf_base + 63) // 64) * 64
    TOPB = nc.sbuf_top
    _sz = {F32: 4, F16: 2, I32: 4, U8: 1}

    def at(name, shape, dt, off):
        assert AR0 + off + shape[1] * _sz[dt] <= TOPB, (
            "arena overflow: %s needs %d..%d top %d" %
            (name, AR0 + off, AR0 + off + shape[1] * _sz[dt], TOPB))
        return nc.alloc_sbuf_tensor_at(name, shape, dt, offset=AR0 + off)

    cosT = at("cosT_b", [128, S], F32, 0)
    ssinT = at("ssinT_b", [128, S], F32, 8192)
    maskd = at("maskd_b", [128, 128], F32, 16384)
    mask4 = at("mask4_b", [128, 128], F32, 16896)
    ind2T = at("ind2T_b", [2, 128], F32, 17408)
    ind2 = at("ind2_b", [128, 2], F32, 17920)
    vaug0 = at("vaug0", [128, 16 * 65], F32, 17952)
    vaug1 = at("vaug1", [128, 16 * 65], F32, 22112)
    pbuf = at("pbuf", [128, 2 * 128], F32, 26272)
    l0qkv = at("l0qkv", [128, 3 * S], F32, 27296)
    ln_sq = at("ln_sq", [128, 512], F32, 27296)
    ln_t0 = at("ln_t0", [128, 512], F32, 29344)
    lnr = at("lnr", [1, 2560], F32, 31392)
    att_scr = at("att_scr", [128, 512], F32, 51872)
    att_pad = at("att_pad", [128, 520], F32, 53920)
    og_h = at("og_h", [128, S], F16, 56000)
    og_l = at("og_l", [128, S], F16, 60096)
    land = at("land", [128, D], F32, 64192)
    w_h = at("w_h", [128, D], F16, 68288)
    w_l = at("w_l", [128, D], F16, 70336)
    rz = at("rz", [128, 128], F32, 72384)
    o_f = at("o_f", [128, 128], F32, 72896)
    oh_loc = at("oh_loc", [128, 128], F16, 73408)
    ol_loc = at("ol_loc", [128, 128], F16, 73664)
    att_s = at("att_s", [2, 1024], F32, 73920)
    pbuf4 = at("pbuf4", [128, 512], F32, 73920)

    k0b_a = at("k0b_a", [128, 4096], U8, 0)
    k0b_b = at("k0b_b", [128, 4096], U8, 4096)
    h0_h = at("h0_h", [128, 32 * FT], F16, 8192)
    h0_l = at("h0_l", [128, 32 * FT], F16, 24576)
    ftmp = at("ftmp", [128, 4096], F32, 8192)
    f16t = at("f16t", [128, 4096], F16, 24576)
    fb32 = at("fb32", [128, 2048], F32, 32768)
    w0bt = at("w0bt", [128, 2 * 2048], F16, 40960)
    w1bt = at("w1bt", [128, 2 * 4096], F16, 49152)
    w2bt = at("w2bt", [128, 2 * 1024], F16, 65536)
    h1t = at("h1t", [128, 512], F16, 69632)
    ffn_h0f = at("ffn_h0f", [128, FT], F32, 70656)

    scal = at("scal", [128, 1024], F32, 0)
    st_win = at("st_win", [128, NG * DH], F32, 4096)
    st_wout = at("st_wout", [128, NG * DH], F32, 6144)
    st_woutT = at("st_woutT", [128, NG * DH], F32, 8192)
    knd = at("knd", [128, NG * DH], F32, 10240)
    vnd = at("vnd", [128, NG * DH], F32, 12288)
    ptb = at("ptb", [128, NG * DH], F32, 14336)
    gzb = at("gzb", [128, NG * DH], F32, 16384)
    sm_s = at("sm_s", [128, 64], F32, 18432)
    pq_sb = at("pq_sb", [128, NG * 2 * DH], F32, 18688)
    scan_t = at("scan_t", [128, 64], F32, 22816)
    lrc_a = at("lrc_a", [32, NCK], F32, 23072)
    lrc_b = at("lrc_b", [32, NCK], F32, 23072 + 4 * NCK)
    lrc4_sb = at("lrc4_sb", [128, 2 * NCK], F32, 23328)
    sels_sb = at("sels_sb", [128, 1024], F32, 23584)
    stack4_sb = at("stack4_sb", [32, 128], F32, 27680)
    lr_sb = at("lr_sb", [32, S], F32, 28192)
    wlr_sb = at("wlr_sb", [128, 256], F32, 36384)
    wbe_sb = at("wbe_sb", [128, 256], F32, 37408)
    wlr_h = at("wlr_h", [128, 256], F16, 38432)
    wlr_l = at("wlr_l", [128, 256], F16, 38944)
    wbe_h = at("wbe_h", [128, 256], F16, 39456)
    wbe_l = at("wbe_l", [128, 256], F16, 39968)
    land2 = at("land2", [128, D], F32, 40480)
    u_t = at("u_t", [128, D], F32, 44576)
    g_t = at("g_t", [128, D], F32, 48672)
    absr = at("absr", [128, 16], F32, 52768)
    wq16c = at("wq16c", [128, D], F16, 52832)
    l1qkv = at("l1qkv", [128, 3 * S], F32, 23584)
    l1_o = at("l1_o", [128, S], F32, 48160)
    xt_sub = at("xt_sub", [128, 1024], F32, 48160)
    c16 = at("c16", [128, 1024], F16, 52256)
    rnd_m = at("rnd_m", [128, 1024], F32, 52256)
    land3b = at("land3b", [128, D], F32, 52256)
    rnd_t = at("rnd_t", [128, 1024], F32, 57376)
    ojh = at("ojh", [128, 256], F16, 56352)
    ojl = at("ojl", [128, 256], F16, 56864)
    land3 = at("land3", [128, D], F32, 57376)
    w3h = at("w3h", [128, D], F16, 61472)
    w3l = at("w3l", [128, D], F16, 63520)
    i32b = at("i32b", [128, 512], I32, 65568)
    wqb0 = at("wqb0", [128, D], F16, 67616)
    wqb1 = at("wqb1", [128, D], F16, 69664)
    ffn_s = at("ffn_s", [128, 768], F32, 71712)
    aqp16 = at("aqp16", [128, 128], F16, 74784)

    PS = [nc.alloc_psum_tensor("ps%d" % i, [128, 512], F32) for i in range(8)]

    P = Prog(nc)
    P.fence_buf = fence_buf
    DW, DG, DS = "dw", "dg", "ds"
    TT, VV, SS, OUT = "t", "v", "s", "o"
    for s_ in (DW, DG, DS, TT, VV, SS, OUT):
        P.sem(s_)

    def dma(eng, sem, out_, in__):
        P.op(eng, "dma_start", inc=[(sem, 16)], out=out_, in_=in__)

    tap_names = []

    def tap(name, ap, shape, dt=F32):
        if name not in taps:
            return
        t_ = nc.dram_tensor("tap_" + name, list(shape), dt, kind="ExternalOutput")
        tap_names.append("tap_" + name)
        P.bar()
        dma("gpsimd", OUT, t_.ap(), ap)
        P.bar()

    dma("gpsimd", DG, x_sb[:].rearrange("p (c t) -> p c t", t=S),
        dr["x_in"].ap().rearrange("c p t -> p c t"))
    for sb, nm in ((cosT, "cosT"), (ssinT, "ssinT"), (maskd, "maskd"),
                   (mask4, "mask4"), (ident, "ident"), (ident16, "ident16"),
                   (onesr, "onesr"),
                   (ind2, "ind2"), (ind2T, "ind2T"), (s012, "l0_s")):
        dma("sync", DW, sb[:], dr[nm].ap())
    for nm in bias:
        dma("sync", DW, bias[nm][:], dr[nm].ap())
    for nm in convw:
        dma("sync", DW, convw[nm][:], dr[nm].ap())
    for e in ("tensor", "vector", "scalar"):
        P.wait(e, DW)
        P.wait(e, DG)

    P.op("tensor", "matmul", PS[0][0:128, 0:3], onesr[0:1, 0:128], s012[0:1, 0:3],
         start=True, stop=True, inc=[(TT, 1)])
    P.wait("vector", TT)
    P.op("vector", "tensor_scalar", s012b[:, 0:2], PS[0][0:128, 0:2], 2.0, None, ALU.mult)
    P.op("vector", "tensor_copy", s012b[:, 2:3], PS[0][0:128, 2:3], inc=[(VV, 1)])
    P.bar()

    def emit_layernorm():
        for tt4 in range(4):
            for c in range(NCH):
                xc = x_sb[:, c * S + tt4 * 512: c * S + (tt4 + 1) * 512]
                P.wait("scalar", TT, P.mark(TT))
                P.op("scalar", "activation", ln_sq[:, 0:512], xc, AF.Square,
                     inc=[(SS, 1)])
                P.wait("tensor", SS)
                P.op("tensor", "matmul", PS[0][0:1, 0:512], onesr[0:128, 0:1], xc,
                     start=(c == 0), stop=(c == NCH - 1))
                P.op("tensor", "matmul", PS[1][0:1, 0:512], onesr[0:128, 0:1],
                     ln_sq[:, 0:512], start=(c == 0), stop=(c == NCH - 1),
                     inc=[(TT, 1)])
            P.wait("vector", TT)
            m_r = lnr[0:1, 0:512]
            v_r = lnr[0:1, 512:1024]
            r_r = lnr[0:1, 1024:1536]
            b_r = lnr[0:1, 1536:2048]
            t_r = lnr[0:1, 2048:2560]
            P.op("vector", "tensor_scalar", m_r, PS[0][0:1, 0:512], 1.0 / D, None, ALU.mult)
            P.op("vector", "tensor_scalar", v_r, PS[1][0:1, 0:512], 1.0 / D, None, ALU.mult)
            P.op("vector", "tensor_tensor", t_r, m_r, m_r, ALU.mult)
            P.op("vector", "tensor_tensor", v_r, v_r, t_r, ALU.subtract)
            P.op("vector", "tensor_scalar", v_r, v_r, EPS_LN, None, ALU.add)
            P.op("vector", "reciprocal", v_r, v_r, inc=[(VV, 1)])
            P.wait("scalar", VV)
            P.op("scalar", "activation", r_r, v_r, AF.Sqrt, inc=[(SS, 1)])
            P.wait("vector", SS)
            P.op("vector", "tensor_tensor", b_r, m_r, r_r, ALU.mult)
            P.op("vector", "tensor_scalar", b_r, b_r, -1.0, None, ALU.mult,
                 inc=[(VV, 1)])
            P.wait("tensor", VV)
            P.op("tensor", "matmul", PS[2][0:128, 0:512], onesr[0:1, 0:128], r_r,
                 start=True, stop=True)
            P.op("tensor", "matmul", PS[3][0:128, 0:512], onesr[0:1, 0:128], b_r,
                 start=True, stop=True, inc=[(TT, 1)])
            P.wait("vector", TT)
            for c in range(NCH):
                cs = slice(c * S + tt4 * 512, c * S + (tt4 + 1) * 512)
                t0 = ln_t0[:, 0:512]
                P.op("vector", "tensor_tensor", t0, x_sb[:, cs], PS[2][0:128, 0:512], ALU.mult)
                P.op("vector", "tensor_tensor", t0, t0, PS[3][0:128, 0:512], ALU.add)
                P.op("vector", "tensor_copy", xn_h[:, cs], t0)
                P.op("vector", "tensor_tensor", xn_l[:, cs], t0, xn_h[:, cs], ALU.subtract)
            P.bump("vector", VV)
            P.wait("tensor", VV)
        P.bar()

    emit_layernorm()
    tap("xn_h", xn_h[:], (128, NCH * S), F16)
    tap("xn_l", xn_l[:], (128, NCH * S), F16)

    q_c = l0qkv[:, 0:S]
    k_c = l0qkv[:, S:2 * S]
    v_c = l0qkv[:, 2 * S:3 * S]
    scr = att_scr
    pad = att_pad

    qkv_specs = [("l0_wq", "l0_bq", "l0_cq", "l0_cqb", q_c),
                 ("l0_wk", "l0_bk", "l0_ck", "l0_ckb", k_c),
                 ("l0_wv", "l0_bv", "l0_cv", "l0_cvb", v_c)]

    for g in (range(NG) if _PH >= 2 else ()):
        for pi, (wnm, bnm, cnm, cbnm, dest) in enumerate(qkv_specs):
            P.wait("sync", VV, P.mark(VV))
            dma("sync", DW, land[:].rearrange("p (c j) -> p c j", j=128),
                dr[wnm].ap()[g].rearrange("(c p) j -> p c j", p=128))
            P.wait("vector", DW)
            P.wait("vector", TT, P.mark(TT))
            P.op("vector", "tensor_copy", w_h[:], land[:])
            P.op("vector", "tensor_tensor", w_l[:], land[:], w_h[:], ALU.subtract,
                 inc=[(VV, 1)])
            P.wait("tensor", VV)
            for tt4 in range(4):
                pb = PS[4 + (tt4 % 2)]
                P.wait("tensor", SS, P.mark(SS))
                for c in range(NCH):
                    xh_ = xn_h[:, c * S + tt4 * 512: c * S + (tt4 + 1) * 512]
                    xl_ = xn_l[:, c * S + tt4 * 512: c * S + (tt4 + 1) * 512]
                    lw_h = w_h[:, c * 128:(c + 1) * 128]
                    lw_l = w_l[:, c * 128:(c + 1) * 128]
                    P.op("tensor", "matmul", pb[0:128, 0:512], lw_h, xh_,
                         start=(c == 0), stop=False)
                    P.op("tensor", "matmul", pb[0:128, 0:512], lw_h, xl_,
                         start=False, stop=False)
                    P.op("tensor", "matmul", pb[0:128, 0:512], lw_l, xh_,
                         start=False, stop=(c == NCH - 1),
                         inc=([(TT, 1)] if c == NCH - 1 else ()))
                P.wait("scalar", TT)
                P.wait("scalar", VV, P.mark(VV))
                P.op("scalar", "activation", pad[:, 3:515], pb[0:128, 0:512],
                     AF.Identity, bias=bias[bnm][:, g:g + 1], scale=1.0, inc=[(SS, 1)])
                P.wait("vector", SS)
                if tt4 == 0:
                    P.op("vector", "memset", pad[:, 0:3], 0.0)
                dsl = slice(tt4 * 512, (tt4 + 1) * 512)
                cw = convw[cnm]
                P.op("vector", "tensor_scalar", dest[:, dsl], pad[:, 0:512],
                     cw[:, g * KS:g * KS + 1], bias[cbnm][:, g:g + 1],
                     ALU.mult, ALU.add)
                for kk in range(1, KS):
                    P.op("vector", "scalar_tensor_tensor", dest[:, dsl],
                         pad[:, kk:kk + 512], cw[:, g * KS + kk:g * KS + kk + 1],
                         dest[:, dsl], ALU.mult, ALU.add)
                P.op("vector", "tensor_copy", pad[:, 0:3], pad[:, 512:515],
                     inc=[(VV, 1)])
        for dest in (q_c, k_c):
            for tt4 in range(4):
                dsl = slice(tt4 * 512, (tt4 + 1) * 512)
                P.op("vector", "tensor_copy", scr[0:32, 0:512], dest[32:64, dsl])
                P.op("vector", "tensor_copy", scr[32:64, 0:512], dest[0:32, dsl])
                P.op("vector", "tensor_copy", scr[64:96, 0:512], dest[96:128, dsl])
                P.op("vector", "tensor_copy", scr[96:128, 0:512], dest[64:96, dsl])
                P.op("vector", "tensor_tensor", dest[:, dsl], dest[:, dsl],
                     cosT[:, dsl], ALU.mult)
                P.op("vector", "tensor_tensor", scr[:, 0:512], scr[:, 0:512],
                     ssinT[:, dsl], ALU.mult)
                P.op("vector", "tensor_tensor", dest[:, dsl], dest[:, dsl],
                     scr[:, 0:512], ALU.add)
            P.bump("vector", VV)
        for dest in (q_c, k_c):
            for tt4 in range(4):
                dsl = slice(tt4 * 512, (tt4 + 1) * 512)
                P.wait("scalar", VV, P.mark(VV))
                P.op("scalar", "activation", scr[:, 0:512], dest[:, dsl],
                     AF.Square, inc=[(SS, 1)])
                P.wait("tensor", SS)
                P.op("tensor", "matmul", PS[6][0:2, 0:512], ind2[0:128, 0:2],
                     scr[:, 0:512], start=True, stop=True, inc=[(TT, 1)])
                P.wait("vector", TT)
                P.op("vector", "tensor_scalar", att_s[0:2, 0:512], PS[6][0:2, 0:512],
                     EPS_L2, None, ALU.add)
                P.op("vector", "reciprocal", att_s[0:2, 0:512], att_s[0:2, 0:512],
                     inc=[(VV, 1)])
                P.wait("scalar", VV)
                P.op("scalar", "activation", att_s[0:2, 512:1024], att_s[0:2, 0:512],
                     AF.Sqrt, inc=[(SS, 1)])
                P.wait("tensor", SS)
                P.op("tensor", "matmul", PS[6][0:128, 0:512], ind2T[0:2, 0:128],
                     att_s[0:2, 512:1024], start=True, stop=True, inc=[(TT, 1)])
                P.wait("vector", TT)
                P.op("vector", "tensor_tensor", dest[:, dsl], dest[:, dsl],
                     PS[6][0:128, 0:512], ALU.mult, inc=[(VV, 1)])
                P.wait("tensor", VV)
            P.bump("vector", VV)
        if g == 0:
            tap("g0_q", q_c, (128, S))
            tap("g0_k", k_c, (128, S))
            tap("g0_v", v_c, (128, S))
        P.wait("vector", VV, P.mark(VV))
        P.op("vector", "memset", vaug0[:], 1.0)
        P.op("vector", "memset", vaug1[:], 1.0, inc=[(VV, 1)])
        P.wait("tensor", VV)
        for h in range(2):
            va = vaug0 if h == 0 else vaug1
            for tt16 in range(16):
                P.op("tensor", "transpose", PS[6][0:128, 0:64],
                     v_c[h * 64:(h + 1) * 64, tt16 * 128:(tt16 + 1) * 128],
                     ident[h * 64:(h + 1) * 64, h * 64:h * 64 + 64],
                     inc=[(TT, 1)])
                P.wait("vector", TT)
                P.op("vector", "tensor_copy", va[:, tt16 * 65:tt16 * 65 + 64],
                     PS[6][0:128, 0:64], inc=[(VV, 1)])
                P.wait("tensor", VV)
        P.wait("sync", VV, P.mark(VV))
        dma("sync", DW, land[:], dr["l0_wo"].ap()[g])
        P.wait("vector", DW)
        P.op("vector", "tensor_copy", w_h[:], land[:])
        P.op("vector", "tensor_tensor", w_l[:], land[:], w_h[:], ALU.subtract,
             inc=[(VV, 1)])
        for qt in range(16):
            kbs = list(range(max(0, qt - 4), qt + 1))
            for h in range(2):
                va = vaug0 if h == 0 else vaug1
                hs = slice(h * 64, (h + 1) * 64)
                po = PS[4 + h]
                P.wait("tensor", VV, P.mark(VV))
                P.wait("tensor", SS, P.mark(SS))
                exp_marks = {}
                mask_marks = {}

                def emit_scores(ki, kb, h=h, hs=hs, qt=qt):
                    psc = PS[2 * h + ki % 2]
                    P.op("tensor", "matmul", psc[0:128, 0:128],
                         k_c[hs, kb * 128:(kb + 1) * 128],
                         q_c[hs, qt * 128:(qt + 1) * 128],
                         start=True, stop=True, inc=[(TT, 1)])

                def emit_exp(ki, kb, h=h, qt=qt):
                    psc = PS[2 * h + ki % 2]
                    pt = pbuf4[:, (2 * h + ki % 2) * 128:(2 * h + ki % 2) * 128 + 128]
                    P.wait("scalar", TT, P.mark(TT))
                    P.op("scalar", "activation", pt, psc[0:128, 0:128], AF.Exp,
                         scale=float(SCALE), inc=[(SS, 1)])
                    exp_marks[ki] = P.mark(SS)
                    if kb == qt or kb == qt - 4:
                        P.wait("vector", SS, exp_marks[ki])
                        mk = maskd if kb == qt else mask4
                        P.op("vector", "tensor_tensor", pt, pt, mk[:], ALU.mult,
                             inc=[(VV, 1)])
                        mask_marks[ki] = P.mark(VV)

                emit_scores(0, kbs[0])
                emit_exp(0, kbs[0])
                for ki, kb in enumerate(kbs):
                    if ki + 1 < len(kbs):
                        emit_scores(ki + 1, kbs[ki + 1])
                        emit_exp(ki + 1, kbs[ki + 1])
                    pt = pbuf4[:, (2 * h + ki % 2) * 128:(2 * h + ki % 2) * 128 + 128]
                    if ki in mask_marks:
                        P.wait("tensor", VV, mask_marks[ki])
                    else:
                        P.wait("tensor", SS, exp_marks[ki])
                    P.op("tensor", "matmul", po[0:65, 0:128],
                         va[:, kbs[ki] * 65:(kbs[ki] + 1) * 65], pt,
                         start=(ki == 0), stop=(ki == len(kbs) - 1),
                         inc=([(TT, 1)] if ki == len(kbs) - 1 else ()))
                P.wait("vector", TT)
                P.op("vector", "reciprocal", rz[64:65, 0:128], po[64:65, 0:128],
                     inc=[(VV, 1)])
                P.wait("scalar", TT)
                P.op("scalar", "activation", o_f[0:64, 0:128], po[0:64, 0:128],
                     AF.Copy, inc=[(SS, 1)])
                P.wait("tensor", VV)
                P.op("tensor", "matmul", PS[6][0:64, 0:128], onesr[64:65, 0:64],
                     rz[64:65, 0:128], start=True, stop=True, inc=[(TT, 1)])
                P.wait("vector", TT)
                P.wait("vector", SS)
                P.op("vector", "tensor_tensor", o_f[0:64, 0:128], o_f[0:64, 0:128],
                     PS[6][0:64, 0:128], ALU.mult)
                P.op("vector", "tensor_copy", oh_loc[0:64, 0:128], o_f[0:64, 0:128])
                P.op("vector", "tensor_tensor", ol_loc[0:64, 0:128],
                     o_f[0:64, 0:128], oh_loc[0:64, 0:128], ALU.subtract)
                P.op("vector", "tensor_copy",
                     og_h[h * 64:(h + 1) * 64, qt * 128:(qt + 1) * 128],
                     oh_loc[0:64, 0:128])
                P.op("vector", "tensor_copy",
                     og_l[h * 64:(h + 1) * 64, qt * 128:(qt + 1) * 128],
                     ol_loc[0:64, 0:128], inc=[(VV, 1)])
                P.wait("tensor", VV)
        if g == 0:
            tap("g0_og", og_h[:], (128, S), F16)
        P.wait("tensor", VV, P.mark(VV))
        for mc in range(NCH):
            for tt4 in range(4):
                pb = PS[4 + (tt4 % 2)]
                P.wait("tensor", VV, P.mark(VV))
                sl = slice(tt4 * 512, (tt4 + 1) * 512)
                P.op("tensor", "matmul", pb[0:128, 0:512],
                     w_h[:, mc * 128:(mc + 1) * 128], og_h[:, sl],
                     start=True, stop=False)
                P.op("tensor", "matmul", pb[0:128, 0:512],
                     w_h[:, mc * 128:(mc + 1) * 128], og_l[:, sl],
                     start=False, stop=False)
                P.op("tensor", "matmul", pb[0:128, 0:512],
                     w_l[:, mc * 128:(mc + 1) * 128], og_h[:, sl],
                     start=False, stop=True, inc=[(TT, 1)])
                P.wait("vector", TT)
                xs = x_sb[:, mc * S + tt4 * 512: mc * S + (tt4 + 1) * 512]
                if g == NG - 1:
                    P.op("vector", "scalar_tensor_tensor", xs, pb[0:128, 0:512],
                         bias["l0_bo"][:, mc:mc + 1], xs, ALU.add, ALU.add,
                         inc=[(VV, 1)])
                else:
                    P.op("vector", "tensor_tensor", xs, xs, pb[0:128, 0:512],
                         ALU.add, inc=[(VV, 1)])
        P.bar(("tensor", "vector", "scalar", "sync"))
    P.bar()
    tap("x_attn", x_sb[:], (128, NCH * S))

    for c in (range(NCH) if _PH >= 3 else ()):
        P.wait("sync", VV, P.mark(VV))
        dma("sync", DW, k0b_a[:], dr["l0_k0"].ap()[c * 128:(c + 1) * 128, :])
        dma("sync", DW, k0b_b[:], dr["l0_k0"].ap()[D + c * 128:D + (c + 1) * 128, :])
        P.wait("vector", DW)
        P.wait("vector", DG, P.mark(DG))
        P.op("vector", "tensor_scalar", ftmp[:], k0b_a[:], 2.0, None, ALU.mult)
        P.op("vector", "tensor_scalar", fb32[:], k0b_b[:, 0:2048], 2.0, None, ALU.mult)
        P.op("vector", "tensor_tensor", f16t[:, 0:2048], ftmp[:, 0:2048], fb32[:],
             ALU.subtract)
        P.op("vector", "tensor_scalar", fb32[:], k0b_b[:, 2048:4096], 2.0, None, ALU.mult)
        P.op("vector", "tensor_tensor", f16t[:, 2048:4096], ftmp[:, 2048:4096],
             fb32[:], ALU.subtract, inc=[(VV, 1)])
        P.wait("gpsimd", VV)
        dma("gpsimd", DG, w0scr.ap()[:, c, :, :].rearrange("s p m -> p s m"),
            f16t[:].rearrange("p (s m) -> p s m", m=256))
    for c in (range(32) if _PH >= 3 else ()):
        kb_buf = k0b_a if c % 2 == 0 else k0b_b
        P.wait("sync", VV, P.mark(VV))
        dma("sync", DW, kb_buf[:], dr["l0_k1"].ap()[c * 128:(c + 1) * 128, :])
        P.wait("vector", DW)
        P.wait("vector", DG, P.mark(DG))
        P.op("vector", "tensor_scalar", f16t[:], kb_buf[:], 2.0, -1.0, ALU.mult,
             ALU.add, inc=[(VV, 1)])
        P.wait("gpsimd", VV)
        dma("gpsimd", DG, w1scr.ap()[:, c, :, :].rearrange("s p m -> p s m"),
            f16t[:].rearrange("p (s m) -> p s m", m=128))
    for c in (range(32) if _PH >= 3 else ()):
        kb_buf = k0b_a if c % 2 == 0 else k0b_b
        P.wait("sync", VV, P.mark(VV))
        dma("sync", DW, kb_buf[:, 0:1024], dr["l0_k2"].ap()[c * 128:(c + 1) * 128, :])
        P.wait("vector", DW)
        P.wait("vector", DG, P.mark(DG))
        P.op("vector", "tensor_scalar", f16t[:, 0:1024], kb_buf[:, 0:1024], 2.0,
             -1.0, ALU.mult, ALU.add, inc=[(VV, 1)])
        P.wait("gpsimd", VV)
        dma("gpsimd", DG, w2scr.ap()[c], f16t[:, 0:1024])
    P.bar()

    w0b = [w0bt[:, 0:2048], w0bt[:, 2048:4096]]
    w1b = [w1bt[:, 0:4096], w1bt[:, 4096:8192]]
    w2b = [w2bt[:, 0:1024], w2bt[:, 1024:2048]]
    h1h = h1t[:, 0:256]
    h1l = h1t[:, 256:512]
    h0f = ffn_h0f[:, 0:FT]

    for t in (range(NFT) if _PH >= 4 else ()):
        def tsl(c, t=t):
            return slice(c * S + t * FT, c * S + (t + 1) * FT)
        for sidx in range(16):
            wb = w0b[sidx % 2]
            P.wait("sync", TT, P.mark(TT))
            dma("sync", DW, wb.rearrange("p (c m) -> p c m", m=256),
                w0scr.ap()[sidx].rearrange("c p m -> p c m"))
            P.wait("tensor", DW)
            for mcc in range(2):
                pb = PS[0]
                psl = slice(mcc * 128, mcc * 128 + 128) if False else slice(0, FT)
                P.wait("tensor", VV, P.mark(VV))
                for c in range(NCH):
                    lw = wb[:, c * 256 + mcc * 128: c * 256 + mcc * 128 + 128]
                    P.op("tensor", "matmul", pb[0:128, 0:FT], lw, xn_h[:, tsl(c)],
                         start=(c == 0), stop=False)
                    P.op("tensor", "matmul", pb[0:128, 0:FT], lw, xn_l[:, tsl(c)],
                         start=False, stop=(c == NCH - 1),
                         inc=([(TT, 1)] if c == NCH - 1 else ()))
                P.wait("vector", TT)
                mo = sidx * 2 + mcc
                P.op("vector", "tensor_scalar", h0f, pb[0:128, 0:FT],
                     s012b[:, 0:1], 0.0, ALU.mult, ALU.max, inc=[(VV, 1)])
                P.wait("scalar", VV)
                P.op("scalar", "activation", h0_h[:, mo * FT:(mo + 1) * FT], h0f,
                     AF.Copy, inc=[(SS, 1)])
                P.wait("vector", SS)
                P.op("vector", "tensor_tensor", h0_l[:, mo * FT:(mo + 1) * FT],
                     h0f, h0_h[:, mo * FT:(mo + 1) * FT], ALU.subtract,
                     inc=[(VV, 1)])
        P.bar(("tensor", "vector", "scalar", "sync"))
        if t == 0:
            tap("h0_t0", h0_h[:], (128, 32 * FT), F16)
        for s2 in range(32):
            wb = w1b[s2 % 2]
            w2bb = w2b[s2 % 2]
            P.wait("sync", TT, P.mark(TT))
            dma("sync", DW, wb.rearrange("p (c m) -> p c m", m=128),
                w1scr.ap()[s2].rearrange("c p m -> p c m"))
            dma("sync", DW, w2bb, w2scr.ap()[s2])
            P.wait("tensor", DW)
            pb = PS[1]
            P.wait("tensor", VV, P.mark(VV))
            P.wait("tensor", SS, P.mark(SS))
            for c in range(32):
                lw = wb[:, c * 128:(c + 1) * 128]
                P.op("tensor", "matmul", pb[0:128, 0:FT], lw,
                     h0_h[:, c * FT:(c + 1) * FT], start=(c == 0), stop=False)
                P.op("tensor", "matmul", pb[0:128, 0:FT], lw,
                     h0_l[:, c * FT:(c + 1) * FT], start=False, stop=(c == 31),
                     inc=([(TT, 1)] if c == 31 else ()))
            P.wait("vector", TT)
            P.op("vector", "tensor_scalar", h0f, pb[0:128, 0:FT],
                 s012b[:, 1:2], 0.0, ALU.mult, ALU.max, inc=[(VV, 1)])
            P.wait("scalar", VV)
            P.op("scalar", "activation", h1h[:, 0:FT], h0f, AF.Copy, inc=[(SS, 1)])
            P.wait("vector", SS)
            P.op("vector", "tensor_tensor", h1l[:, 0:FT], h0f, h1h[:, 0:FT],
                 ALU.subtract, inc=[(VV, 1)])
            P.wait("tensor", VV)
            for mc2 in range(NCH):
                yb = PS[4 + mc2 // 2]
                ysl = slice((mc2 % 2) * 256, (mc2 % 2) * 256 + FT)
                P.op("tensor", "matmul", yb[0:128, ysl],
                     w2bb[:, mc2 * 128:(mc2 + 1) * 128], h1h[:, 0:FT],
                     start=(s2 == 0 and mc2 % 2 == 0), stop=False)
                P.op("tensor", "matmul", yb[0:128, ysl],
                     w2bb[:, mc2 * 128:(mc2 + 1) * 128], h1l[:, 0:FT],
                     start=False, stop=(s2 == 31 and mc2 % 2 == 1),
                     inc=([(TT, 1)] if (s2 == 31 and mc2 == NCH - 1) else ()))
        P.wait("vector", TT)
        for mc2 in range(NCH):
            yb = PS[4 + mc2 // 2]
            ysl = slice((mc2 % 2) * 256, (mc2 % 2) * 256 + FT)
            xs = x_sb[:, tsl(mc2)]
            P.op("vector", "scalar_tensor_tensor", xs, yb[0:128, ysl],
                 s012b[:, 2:3], xs, ALU.mult, ALU.add, inc=[(VV, 1)])
        P.wait("tensor", VV)
        P.bar(("tensor", "vector", "scalar", "sync"))
    P.bar()
    tap("x_l0", x_sb[:], (128, NCH * S))

    if _PH < 5:
        dma("gpsimd", OUT, out.ap().rearrange("c p t -> p c t"),
            x_sb[:].rearrange("p (c t) -> p c t", t=S))
        P.wait("gpsimd", OUT)
        P.finalize()
        return nc

    emit_layernorm()

    dma("sync", DW, sels_sb[:], dr["sels"].ap())
    dma("sync", DW, stack4_sb[:], dr["stack4"].ap())
    dma("sync", DW, wlr_sb[:].rearrange("p (c j) -> p c j", j=32),
        dr["l1_wlr"].ap().rearrange("(c p) j -> p c j", p=128))
    dma("sync", DW, wbe_sb[:].rearrange("p (c j) -> p c j", j=32),
        dr["l1_wbeta"].ap().rearrange("(c p) j -> p c j", p=128))
    P.wait("vector", DW)
    for wf, wh_, wl_ in ((wlr_sb, wlr_h, wlr_l), (wbe_sb, wbe_h, wbe_l)):
        P.op("vector", "tensor_copy", wh_[:], wf[:])
        P.op("vector", "tensor_tensor", wl_[:], wf[:], wh_[:], ALU.subtract,
             inc=[(VV, 1)])
    P.wait("tensor", VV)
    for (wh_, wl_, scl, lrc_dst) in ((wlr_h, wlr_l, 0.001 / CH, lrc_a),
                                     (wbe_h, wbe_l, 0.9 / CH, lrc_b)):
        P.wait("scalar", VV, P.mark(VV))
        for tt4 in range(4):
            pb = PS[0]
            sl = slice(tt4 * 512, (tt4 + 1) * 512)
            P.wait("tensor", SS, P.mark(SS))
            for c in range(NCH):
                xh_ = xn_h[:, c * S + tt4 * 512: c * S + (tt4 + 1) * 512]
                xl_ = xn_l[:, c * S + tt4 * 512: c * S + (tt4 + 1) * 512]
                P.op("tensor", "matmul", pb[0:32, 0:512],
                     wh_[:, c * 32:(c + 1) * 32], xh_, start=(c == 0), stop=False)
                P.op("tensor", "matmul", pb[0:32, 0:512],
                     wh_[:, c * 32:(c + 1) * 32], xl_, start=False, stop=False)
                P.op("tensor", "matmul", pb[0:32, 0:512],
                     wl_[:, c * 32:(c + 1) * 32], xh_, start=False,
                     stop=(c == NCH - 1),
                     inc=([(TT, 1)] if c == NCH - 1 else ()))
            P.wait("scalar", TT)
            for pc8 in range(8):
                cck = tt4 * 8 + pc8
                P.op("scalar", "activation", lr_sb[0:32, 0:CH],
                     pb[0:32, pc8 * CH:(pc8 + 1) * CH], AF.Sigmoid,
                     accum_out=scan_t[0:32, cck:cck + 1],
                     inc=([(SS, 1)] if pc8 == 7 else ()))
        P.wait("vector", SS)
        P.op("vector", "tensor_scalar", lrc_dst[:], scan_t[0:32, 0:NCK],
             float(scl), None, ALU.mult, inc=[(VV, 1)])
    tap("lrc_aT", lrc_a[:], (32, NCK))
    tap("lrc_bT", lrc_b[:], (32, NCK))
    P.wait("tensor", VV)
    P.op("tensor", "matmul", PS[2][0:128, 0:NCK], stack4_sb[0:32, 0:128],
         lrc_a[:], start=True, stop=True)
    P.op("tensor", "matmul", PS[2][0:128, NCK:2 * NCK], stack4_sb[0:32, 0:128],
         lrc_b[:], start=True, stop=True, inc=[(TT, 1)])
    P.wait("vector", TT)
    P.op("vector", "tensor_copy", lrc4_sb[:], PS[2][0:128, 0:2 * NCK],
         inc=[(VV, 1)])
    P.wait("tensor", VV)
    for q in range(32):
        ty, j = divmod(q, 8)
        base = 32 * (q % 4)
        fb = 128 * (q // 4)
        src_ofs = NCK if ty in (0, 2) else 0
        P.op("tensor", "matmul", PS[3][0:128, 0:NCK],
             sels_sb[base:base + 32, fb:fb + 128],
             lrc4_sb[base:base + 32, src_ofs:src_ofs + NCK],
             start=True, stop=True, tile_position=(base, 0), inc=[(TT, 1)])
        P.wait("vector", TT)
        P.op("vector", "tensor_copy", scal[:, q * NCK:(q + 1) * NCK],
             PS[3][0:128, 0:NCK], inc=[(VV, 1)])
        P.wait("tensor", VV)
    P.bar()

    for c in range(NCH):
        P.wait("sync", SS, P.mark(SS))
        dma("sync", DW, land2[:], dr["l1_fw"].ap()[c * 128:(c + 1) * 128, :])
        P.wait("scalar", DW)
        P.op("scalar", "activation", u_t[:], land2[:], AF.Abs,
             accum_out=absr[:, c:c + 1], inc=[(SS, 1)])
    P.wait("scalar", SS)
    P.op("scalar", "activation", u_t[:, 0:8], absr[:, 0:8], AF.Copy,
         accum_out=absr[:, 8:9], inc=[(SS, 1)])
    P.wait("tensor", SS)
    P.op("tensor", "matmul", PS[0][0:1, 0:1], onesr[0:128, 0:1], absr[:, 8:9],
         start=True, stop=True, inc=[(TT, 1)])
    P.wait("vector", TT)
    P.op("vector", "tensor_scalar", ffn_s[0:1, 0:1], PS[0][0:1, 0:1],
         1.0 / (D * D), EPS_Q, ALU.mult, ALU.add, inc=[(VV, 1)])
    P.wait("vector", VV)
    P.op("vector", "tensor_scalar", ffn_s[0:1, 1:2], ffn_s[0:1, 0:1],
         1.0 / 127.0, None, ALU.mult)
    P.op("vector", "reciprocal", ffn_s[0:1, 2:3], ffn_s[0:1, 0:1])
    P.op("vector", "tensor_scalar", ffn_s[0:1, 3:4], ffn_s[0:1, 0:1],
         0.5, None, ALU.mult, inc=[(VV, 1)])
    P.wait("tensor", VV)
    P.op("tensor", "matmul", PS[1][0:128, 0:3], onesr[0:1, 0:128],
         ffn_s[0:1, 1:4], start=True, stop=True, inc=[(TT, 1)])
    P.wait("vector", TT)
    P.op("vector", "tensor_copy", wsb[:, 0:3], PS[1][0:128, 0:3], inc=[(VV, 1)])
    for c in range(NCH):
        P.wait("sync", VV, P.mark(VV))
        dma("sync", DW, land2[:], dr["l1_fw"].ap()[c * 128:(c + 1) * 128, :])
        P.wait("vector", DW)
        P.wait("vector", DG, P.mark(DG))
        P.op("vector", "tensor_scalar", g_t[:], land2[:], wsb[:, 2:3], None,
             ALU.is_ge)
        P.op("vector", "tensor_scalar", u_t[:], land2[:], -1.0, wsb[:, 2:3],
             ALU.mult, ALU.is_ge)
        P.op("vector", "tensor_tensor", wq16c[:], g_t[:], u_t[:], ALU.subtract,
             inc=[(VV, 1)])
        P.wait("gpsimd", VV)
        dma("gpsimd", DG, wqscr.ap()[c], wq16c[:])
        if c == 0:
            tap("wq_c0", wq16c[:], (128, D), F16)
    if "wq_all" in taps:
        t_ = nc.dram_tensor("tap_wq_all", [NCH, 128, D], F16, kind="ExternalOutput")
        P.bar()
        dma("gpsimd", OUT, t_.ap(), wqscr.ap())
        P.bar()
    P.bar()

    for dst, nm in ((st_win, "l1_winit"), (st_wout, "l1_wout"),
                    (st_woutT, "l1_woutT")):
        dma("sync", DW, dst[:].rearrange("p (j d) -> p j d", d=DH),
            dr[nm].ap().rearrange("j p d -> p j d"))
    for e in ("tensor", "vector"):
        P.wait(e, DW)
    P.bar()

    if _PH < 6:
        dma("gpsimd", OUT, out.ap().rearrange("c p t -> p c t"),
            x_sb[:].rearrange("p (c t) -> p c t", t=S))
        P.wait("gpsimd", OUT)
        P.finalize()
        return nc

    q_l1 = l1qkv[:, 0:S]
    k_l1 = l1qkv[:, S:2 * S]
    v_l1 = l1qkv[:, 2 * S:3 * S]
    o_l1 = l1_o[:, 0:S]
    wqb = [wqb0, wqb1]

    for t in range(NL1T):
        lands = (land3, land3b)
        pj_marks = {}
        for pi in range(3):
            for j in range(NG):
                idx = pi * NG + j
                if idx == 0:
                    dma("sync", DW, lands[0][:].rearrange("p (c m) -> p c m", m=128),
                        dr["l1_wqkv"].ap()[0, 0].rearrange("(c p) m -> p c m", p=128))
                    pj_marks[0] = P.mark(DW)
                if idx + 1 < 24:
                    pi2, j2 = divmod(idx + 1, NG)
                    P.wait("sync", VV, P.mark(VV))
                    dma("sync", DW,
                        lands[(idx + 1) % 2][:].rearrange("p (c m) -> p c m", m=128),
                        dr["l1_wqkv"].ap()[pi2, j2].rearrange("(c p) m -> p c m", p=128))
                    pj_marks[idx + 1] = P.mark(DW)
                lnd = lands[idx % 2][:]
                P.wait("vector", DW, pj_marks[idx])
                P.wait("vector", TT, P.mark(TT))
                P.op("vector", "tensor_copy", w3h[:], lnd)
                P.op("vector", "tensor_tensor", w3l[:], lnd, w3h[:], ALU.subtract,
                     inc=[(VV, 1)])
                P.wait("tensor", VV)
                pb = PS[0]
                P.wait("tensor", SS, P.mark(SS))
                for c in range(NCH):
                    xh_ = xn_h[:, c * S + t * L1T: c * S + (t + 1) * L1T]
                    xl_ = xn_l[:, c * S + t * L1T: c * S + (t + 1) * L1T]
                    P.op("tensor", "matmul", pb[0:128, 0:L1T],
                         w3h[:, c * 128:(c + 1) * 128], xh_, start=(c == 0),
                         stop=False)
                    P.op("tensor", "matmul", pb[0:128, 0:L1T],
                         w3h[:, c * 128:(c + 1) * 128], xl_, start=False, stop=False)
                    P.op("tensor", "matmul", pb[0:128, 0:L1T],
                         w3l[:, c * 128:(c + 1) * 128], xh_, start=False,
                         stop=(c == NCH - 1),
                         inc=([(TT, 1)] if c == NCH - 1 else ()))
                P.wait("scalar", TT)
                dest = (q_l1, k_l1, v_l1)[pi]
                fn = AF.Silu if pi < 2 else AF.Copy
                P.op("scalar", "activation", dest[:, j * L1T:(j + 1) * L1T],
                     pb[0:128, 0:L1T], fn, inc=[(SS, 1)])
        P.bar(("tensor", "vector", "scalar", "sync"))
        if t == 0:
            tap("q_l1_t0", q_l1, (128, S))
        for cc in (range(CPT) if _PH >= 7 else ()):
            c_g = t * CPT + cc

            def csl(j, cc=cc):
                return slice(j * L1T + cc * CH, j * L1T + (cc + 1) * CH)

            P.wait("tensor", VV, P.mark(VV))
            for j in range(NG):
                kb_map = (PS[0], PS[1], PS[6], PS[7])
                for h in range(2):
                    hs = slice(h * 64, (h + 1) * 64)
                    P.op("tensor", "transpose", kb_map[2 * h][0:64, 0:64],
                         k_l1[hs, csl(j)], ident[hs, h * 64:h * 64 + 64])
                    P.op("tensor", "transpose", kb_map[2 * h + 1][0:64, 0:64],
                         v_l1[hs, csl(j)], ident[hs, h * 64:h * 64 + 64],
                         inc=([(TT, 1)] if h == 1 else ()))
                P.wait("vector", TT)
                for h in range(2):
                    hs = slice(h * 64, (h + 1) * 64)
                    P.op("vector", "tensor_copy", knd[hs, j * DH:(j + 1) * DH],
                         kb_map[2 * h][0:64, 0:64])
                    P.op("vector", "tensor_copy", vnd[hs, j * DH:(j + 1) * DH],
                         kb_map[2 * h + 1][0:64, 0:64],
                         inc=([(VV, 1)] if h == 1 else ()))
                P.wait("tensor", VV)
            for j in (range(NG) if _SC >= 1 else ()):
                zb = PS[2 + j // 4]
                zo = (j % 4) * 128
                for h in range(2):
                    hs = slice(h * 64, (h + 1) * 64)
                    P.op("tensor", "matmul", zb[hs, zo:zo + 64],
                         st_win[hs, j * DH:(j + 1) * DH], q_l1[hs, csl(j)],
                         start=True, stop=True,
                         tile_position=(h * 64, h * 64))
                    P.op("tensor", "matmul", zb[hs, zo + 64:zo + 128],
                         st_win[hs, j * DH:(j + 1) * DH], k_l1[hs, csl(j)],
                         start=True, stop=True,
                         tile_position=(h * 64, h * 64),
                         inc=([(TT, 1)] if (j % 4 == 3 and h == 1) else ()))
            P.wait("vector", TT)
            for j in (range(NG) if _SC >= 2 else ()):
                zb = PS[2 + j // 4]
                zo = (j % 4) * 128
                for qk in range(2):
                    idx = j * 2 + qk
                    P.op("vector", "reduce_max", sm_s[:, idx:idx + 1],
                         zb[0:128, zo + qk * 64: zo + qk * 64 + 64], axis=AX.X,
                         negate=True, inc=([(VV, 1)] if idx == 15 else ()))
            P.wait("scalar", VV)
            P.wait("scalar", TT, P.mark(TT))
            for j in (range(NG) if _SC >= 2 else ()):
                zb = PS[2 + j // 4]
                zo = (j % 4) * 128
                for qk in range(2):
                    idx = j * 2 + qk
                    eb = PS[4 + idx // 8]
                    eo = (idx % 8) * 64
                    P.op("scalar", "activation", eb[0:128, eo:eo + 64],
                         zb[0:128, zo + qk * 64: zo + qk * 64 + 64], AF.Exp,
                         bias=sm_s[:, idx:idx + 1], scale=1.0,
                         accum_out=sm_s[:, 32 + idx:33 + idx],
                         inc=([(SS, 1)] if idx == 15 else ()))
            P.wait("vector", SS)
            if _SC >= 2:
                P.op("vector", "reciprocal", sm_s[:, 48:64], sm_s[:, 32:48],
                     inc=[(VV, 1)])
                P.wait("vector", VV)
            for j in (range(NG) if _SC >= 2 else ()):
                for qk in range(2):
                    idx = j * 2 + qk
                    eb = PS[4 + idx // 8]
                    eo = (idx % 8) * 64
                    P.op("vector", "tensor_scalar",
                         pq_sb[:, idx * DH:(idx + 1) * DH],
                         eb[0:128, eo:eo + 64],
                         sm_s[:, 48 + idx:49 + idx], None, ALU.mult,
                         inc=([(VV, 1)] if idx == 15 else ()))
            P.wait("tensor", VV)
            if t == 0 and cc == 0:
                tap("sc_pq", pq_sb[:], (128, 1024))
                tap("sc_knd", knd[:], (128, 512))
                tap("sc_vnd", vnd[:], (128, 512))
                tap("sc_smax", sm_s[:, 0:16], (128, 16))
                tap("sc_ssum", sm_s[:, 32:64], (128, 32))
            for j in (range(NG) if _SC >= 3 else ()):
                pq_t = pq_sb[:, (2 * j) * DH:(2 * j) * DH + DH]
                p_t = pq_sb[:, (2 * j + 1) * DH:(2 * j + 1) * DH + DH]
                for h in range(2):
                    hs = slice(h * 64, (h + 1) * 64)
                    P.op("tensor", "matmul", PS[4][hs, 0:64],
                         st_wout[hs, j * DH:(j + 1) * DH], pq_t[hs, 0:DH],
                         start=True, stop=True, tile_position=(h * 64, h * 64))
                    P.op("tensor", "matmul", PS[5][hs, 0:64],
                         v_l1[hs, csl(j)], st_woutT[hs, j * DH:(j + 1) * DH],
                         start=True, stop=True, tile_position=(h * 64, h * 64))
                    P.op("tensor", "transpose",
                         (PS[6] if h == 0 else PS[7])[0:64, 0:64],
                         p_t[hs, 0:DH], ident[hs, h * 64:h * 64 + 64],
                         inc=([(TT, 1)] if h == 1 else ()))
                P.wait("vector", TT)
                P.op("vector", "tensor_copy",
                     o_l1[:, j * L1T + cc * CH: j * L1T + (cc + 1) * CH],
                     PS[4][0:128, 0:64])
                P.op("vector", "tensor_copy", ptb[0:64, j * DH:(j + 1) * DH],
                     PS[6][0:64, 0:64])
                P.op("vector", "tensor_copy", ptb[64:128, j * DH:(j + 1) * DH],
                     PS[7][0:64, 0:64])
                P.op("vector", "tensor_tensor",
                     gzb[:, j * DH:(j + 1) * DH], ptb[:, j * DH:(j + 1) * DH],
                     PS[5][0:128, 0:64], ALU.mult)
                P.op("vector", "reduce_sum", sm_s[:, j:j + 1],
                     gzb[:, j * DH:(j + 1) * DH], axis=AX.X, inc=[(VV, 1)])
                P.wait("vector", VV)
                P.op("vector", "tensor_scalar", scan_t[:, 0:64],
                     ptb[:, j * DH:(j + 1) * DH], sm_s[:, j:j + 1], None, ALU.mult)
                P.op("vector", "tensor_tensor", gzb[:, j * DH:(j + 1) * DH],
                     scan_t[:, 0:64], gzb[:, j * DH:(j + 1) * DH], ALU.subtract,
                     inc=[(VV, 1)])
                P.wait("tensor", VV)
            if t == 0 and cc == 0:
                tap("sc_ptb", ptb[:], (128, 512))
                tap("sc_gzb", gzb[:], (128, 512))
                tap("sc_ol", o_l1[:, 0:64], (128, 64))
            for j in (range(NG) if _SC >= 4 else ()):
                for h in range(2):
                    hs = slice(h * 64, (h + 1) * 64)
                    P.op("tensor", "matmul", PS[4][hs, 0:64],
                         ptb[hs, j * DH:(j + 1) * DH], vnd[hs, j * DH:(j + 1) * DH],
                         start=True, stop=True, tile_position=(h * 64, h * 64))
                    P.op("tensor", "matmul", PS[5][hs, 0:64],
                         vnd[hs, j * DH:(j + 1) * DH], ptb[hs, j * DH:(j + 1) * DH],
                         start=True, stop=True, tile_position=(h * 64, h * 64))
                    P.op("tensor", "matmul", PS[6][hs, 0:64],
                         knd[hs, j * DH:(j + 1) * DH], gzb[hs, j * DH:(j + 1) * DH],
                         start=True, stop=True, tile_position=(h * 64, h * 64),
                         inc=([(TT, 1)] if h == 1 else ()))
                P.wait("vector", TT)
                jc = j * NCK + c_g
                sl_ = slice(j * DH, (j + 1) * DH)
                P.op("vector", "tensor_scalar", st_wout[:, sl_], st_wout[:, sl_],
                     scal[:, 16 * NCK + jc:16 * NCK + jc + 1], None, ALU.mult)
                P.op("vector", "scalar_tensor_tensor", st_wout[:, sl_],
                     PS[4][0:128, 0:64],
                     scal[:, 24 * NCK + jc:24 * NCK + jc + 1],
                     st_wout[:, sl_], ALU.mult, ALU.add)
                P.op("vector", "tensor_scalar", st_woutT[:, sl_], st_woutT[:, sl_],
                     scal[:, 16 * NCK + jc:16 * NCK + jc + 1], None, ALU.mult)
                P.op("vector", "scalar_tensor_tensor", st_woutT[:, sl_],
                     PS[5][0:128, 0:64],
                     scal[:, 24 * NCK + jc:24 * NCK + jc + 1],
                     st_woutT[:, sl_], ALU.mult, ALU.add)
                P.op("vector", "tensor_scalar", st_win[:, sl_], st_win[:, sl_],
                     scal[:, 0 * NCK + jc:0 * NCK + jc + 1], None, ALU.mult)
                P.op("vector", "scalar_tensor_tensor", st_win[:, sl_],
                     PS[6][0:128, 0:64],
                     scal[:, 8 * NCK + jc:8 * NCK + jc + 1],
                     st_win[:, sl_], ALU.mult, ALU.add, inc=[(VV, 1)])
                P.wait("tensor", VV)
        if t == 0:
            tap("o_l1_t0", o_l1[:, 0:S], (128, S))
        if t == 1:
            tap("o_l1_t1", o_l1[:, 0:S], (128, S))
            tap("stwin_t1", st_win[:], (128, NG * DH))
        if t == 0:
            tap("sc_win", st_win[:], (128, 512))
        P.wait("tensor", VV, P.mark(VV))
        P.wait("tensor", SS, P.mark(SS))
        for j in (range(NG) if _PH >= 8 else ()):
            P.wait("sync", VV, P.mark(VV))
            dma("sync", DW, land3[:], dr["l1_wo"].ap()[j])
            P.wait("vector", DW)
            P.wait("vector", TT, P.mark(TT))
            P.op("vector", "tensor_copy", w3h[:], land3[:])
            P.op("vector", "tensor_tensor", w3l[:], land3[:], w3h[:], ALU.subtract)
            P.op("vector", "tensor_copy", ojh[:, 0:L1T], o_l1[:, j * L1T:(j + 1) * L1T])
            P.op("vector", "tensor_tensor", ojl[:, 0:L1T],
                 o_l1[:, j * L1T:(j + 1) * L1T], ojh[:, 0:L1T], ALU.subtract,
                 inc=[(VV, 1)])
            P.wait("tensor", VV)
            for mc in range(NCH):
                yb = PS[4 + mc // 2]
                ysl = slice((mc % 2) * 256, (mc % 2) * 256 + L1T)
                P.op("tensor", "matmul", yb[0:128, ysl],
                     w3h[:, mc * 128:(mc + 1) * 128], ojh[:, 0:L1T],
                     start=(j == 0 and mc % 2 == 0), stop=False)
                P.op("tensor", "matmul", yb[0:128, ysl],
                     w3h[:, mc * 128:(mc + 1) * 128], ojl[:, 0:L1T],
                     start=False, stop=False)
                P.op("tensor", "matmul", yb[0:128, ysl],
                     w3l[:, mc * 128:(mc + 1) * 128], ojh[:, 0:L1T],
                     start=False, stop=(j == NG - 1 and mc % 2 == 1),
                     inc=([(TT, 1)] if mc == NCH - 1 else ()))
        P.wait("vector", TT)
        for mc in (range(NCH) if _PH >= 8 else ()):
            yb = PS[4 + mc // 2]
            ysl = slice((mc % 2) * 256, (mc % 2) * 256 + L1T)
            xs = x_sb[:, mc * S + t * L1T: mc * S + (t + 1) * L1T]
            P.op("vector", "tensor_tensor", xs, xs, yb[0:128, ysl], ALU.add,
                 inc=[(VV, 1)])
        P.wait("tensor", VV)
        P.bar(("tensor", "vector", "scalar", "sync"))
        if t == 0:
            tap("x_oproj_t0", x_sb[:], (128, NCH * S))
        if _PH < 9:
            continue
        am = sm_s[:, 0:2]
        r1 = sm_s[:, 4:6]
        sc2 = sm_s[:, 8:10]
        b2s = ffn_s[:, 0:256]
        t3 = ffn_s[:, 256:384]
        aqp = aqp16[:, 0:128]
        srow = ffn_s[0:1, 512:768]
        for sub in range(2):
            for kb in range(NCH):
                tok = slice(kb * S + t * L1T + sub * 128,
                            kb * S + t * L1T + sub * 128 + 128)
                P.wait("vector", TT, P.mark(TT))
                P.op("vector", "tensor_tensor", t3, xn_h[:, tok], xn_l[:, tok],
                     ALU.add, inc=[(VV, 1)])
                P.wait("tensor", VV)
                P.op("tensor", "transpose", PS[1][0:128, 0:128], t3, ident[:],
                     inc=[(TT, 1)])
                P.wait("vector", TT)
                P.op("vector", "tensor_copy",
                     xt_sub[:, kb * 128:(kb + 1) * 128], PS[1][0:128, 0:128],
                     inc=[(VV, 1)])
            P.op("vector", "reduce_max", am[:, sub:sub + 1], xt_sub[:],
                 axis=AX.X, apply_absolute_value=True, inc=[(VV, 1)])
            P.wait("vector", VV)
            P.op("vector", "tensor_scalar", am[:, sub:sub + 1],
                 am[:, sub:sub + 1], EPS_Q, None, ALU.add)
            P.op("vector", "reciprocal", r1[:, sub:sub + 1], am[:, sub:sub + 1],
                 inc=[(VV, 1)])
            P.wait("vector", VV)
            P.op("vector", "tensor_scalar", r1[:, sub:sub + 1],
                 r1[:, sub:sub + 1], 127.0, None, ALU.mult)
            P.op("vector", "tensor_scalar", sc2[:, sub:sub + 1],
                 am[:, sub:sub + 1], wsb[:, 0:1], None, ALU.mult, inc=[(VV, 1)])
            P.wait("vector", VV)
            if t == 0 and sub == 0:
                tap("f_sc2e", sm_s[:, 8:10], (128, 2))
                tap("f_wsb", wsb[:, 0:3], (128, 3))
            P.op("vector", "tensor_scalar", xt_sub[:], xt_sub[:],
                 r1[:, sub:sub + 1], None, ALU.mult, inc=[(VV, 1)])
            P.wait("vector", VV)
            P.op("vector", "tensor_scalar", xt_sub[:], xt_sub[:],
                 127.0, -128.0, ALU.min, ALU.max, inc=[(VV, 1)])
            P.wait("vector", VV)
            P.op("vector", "tensor_scalar", rnd_t[:], xt_sub[:], 128.5, None,
                 ALU.add, inc=[(VV, 1)])
            P.wait("vector", VV)
            for b_ in (128.0, 64.0, 32.0, 16.0, 8.0, 4.0, 2.0, 1.0):
                P.op("vector", "tensor_single_scalar", rnd_m[:], rnd_t[:], b_,
                     ALU.is_ge, inc=[(VV, 1)])
                P.wait("vector", VV)
                P.op("vector", "scalar_tensor_tensor", rnd_t[:], rnd_m[:], -b_,
                     rnd_t[:], ALU.mult, ALU.add, inc=[(VV, 1)])
                P.wait("vector", VV)
            P.op("vector", "tensor_scalar", xt_sub[:], xt_sub[:], 0.5, None,
                 ALU.add, inc=[(VV, 1)])
            P.wait("vector", VV)
            P.op("vector", "tensor_tensor", xt_sub[:], xt_sub[:], rnd_t[:],
                 ALU.subtract, inc=[(VV, 1)])
            if t == 0 and sub == 0:
                tap("aq_t0", xt_sub[:], (128, 1024))
            if t == 0 and sub == 1:
                tap("aq_t0s1", xt_sub[:], (128, 1024))
            for kb in range(NCH):
                wq_ = wqb[kb % 2]
                P.wait("sync", TT, P.mark(TT))
                dma("sync", DW, wq_[:], wqscr.ap()[kb])
                P.wait("tensor", VV)
                P.op("tensor", "transpose", PS[1][0:128, 0:128],
                     xt_sub[:, kb * 128:(kb + 1) * 128], ident[:], inc=[(TT, 1)])
                P.wait("vector", TT)
                P.op("vector", "tensor_copy", aqp, PS[1][0:128, 0:128],
                     inc=[(VV, 1)])
                P.wait("tensor", DW)
                P.wait("tensor", VV)
                if sub == 0 and kb == 0:
                    P.wait("tensor", SS, P.mark(SS))
                for mc in range(NCH):
                    yb = PS[4 + mc // 2]
                    ysl = slice((mc % 2) * 256 + sub * 128,
                                (mc % 2) * 256 + sub * 128 + 128)
                    P.op("tensor", "matmul", yb[0:128, ysl],
                         wq_[:, mc * 128:(mc + 1) * 128], aqp,
                         start=(sub == 0 and kb == 0 and mc % 2 == 0),
                         stop=(sub == 1 and kb == NCH - 1 and mc % 2 == 1),
                         inc=([(TT, 1)] if mc == NCH - 1 else ()))
            P.op("tensor", "transpose", PS[2 + sub][0:1, 0:128],
                 sc2[:, sub:sub + 1], ident[:], inc=[(TT, 1)])
            P.wait("vector", TT)
            P.op("vector", "tensor_copy", srow[0:1, sub * 128:sub * 128 + 128],
                 PS[2 + sub][0:1, 0:128], inc=[(VV, 1)])
            P.wait("tensor", VV)
        for sub in range(2):
            P.op("tensor", "matmul", PS[2][0:128, 0:128], onesr[0:1, 0:128],
                 srow[0:1, sub * 128:sub * 128 + 128], start=True, stop=True,
                 inc=[(TT, 1)])
            P.wait("vector", TT)
            P.op("vector", "tensor_copy", b2s[:, sub * 128:sub * 128 + 128],
                 PS[2][0:128, 0:128], inc=[(VV, 1)])
            P.wait("tensor", VV)
        for mc in range(NCH):
            yb = PS[4 + mc // 2]
            for sub in range(2):
                ysl = slice((mc % 2) * 256 + sub * 128,
                            (mc % 2) * 256 + sub * 128 + 128)
                P.op("vector", "tensor_tensor", t3,
                     b2s[:, sub * 128:sub * 128 + 128], yb[0:128, ysl], ALU.mult)
                xs = x_sb[:, mc * S + t * L1T + sub * 128:
                          mc * S + t * L1T + sub * 128 + 128]
                P.op("vector", "tensor_tensor", xs, xs, t3, ALU.add,
                     inc=[(VV, 1)])
        P.wait("tensor", VV)
        if t == 0:
            tap("f_b2s", ffn_s[:, 0:256], (128, 256))
            tap("f_srow", ffn_s[0:1, 512:768], (1, 256))
            tap("f_sc2", sm_s[:, 8:10], (128, 2))
        P.bar(("tensor", "vector", "scalar", "sync"))
        if t == 0:
            tap("x_tile0", x_sb[:], (128, NCH * S))
    P.bar()

    dma("gpsimd", OUT, out.ap().rearrange("c p t -> p c t"),
        x_sb[:].rearrange("p (c t) -> p c t", t=S))
    P.wait("gpsimd", OUT)
    P.finalize()
    return nc


def kernel(**inputs):
    cm, pc = _host_stage(inputs)
    nc = build_program(_TAPS)
    in_maps = []
    for b in range(8):
        m = dict(cm)
        m.update(pc[b])
        in_maps.append(m)
    res = run_bass_kernel_spmd(nc, in_maps, core_ids=list(range(8)))
    outs = []
    for b in range(8):
        o = np.asarray(res.results[b]["out"])
        outs.append(np.ascontiguousarray(o.reshape(D, S).T))
    return np.stack(outs)


# version 06
# speedup vs baseline: 1.1332x; 1.0946x over previous
import contextlib
import numpy as np

import concourse.bass as bass
import concourse.mybir as mybir
from concourse.bass_utils import run_bass_kernel_spmd

F32 = mybir.dt.float32
F16 = mybir.dt.float16
I32 = mybir.dt.int32
U8 = mybir.dt.uint8
AF = mybir.ActivationFunctionType
ALU = mybir.AluOpType
AX = mybir.AxisListType

D = 1024
S = 2048
NCH = 8
DH = 64
WIN = 512
SCALE = DH ** -0.5
NG = 8
KS = 4
FT = 256
NFT = S // FT
L1T = 256
NL1T = S // L1T
CH = 64
NCK = S // CH
CPT = L1T // CH
EPS_LN = 1e-5
EPS_L2 = 1e-12
EPS_Q = 1e-8

ENGS = ("sync", "tensor", "vector", "scalar", "gpsimd")

_TAPS = ()


class Prog:
    def __init__(self, nc):
        self.nc = nc
        self.q = {e: [] for e in ENGS}
        self.cnt = {}
        self.order = []

    def sem(self, name):
        if name not in self.cnt:
            self.cnt[name] = 0
            self.order.append(name)
        return name

    def op(self, eng, meth, *args, inc=(), **kw):
        incs = []
        for s, k in inc:
            self.sem(s)
            self.cnt[s] += k
            incs.append((s, k))
        self.q[eng].append(("op", meth, args, kw, incs))

    def wait(self, eng, semname, val=None):
        self.sem(semname)
        v = self.cnt[semname] if val is None else val
        if v > 0:
            self.q[eng].append(("wait", semname, v))

    def mark(self, semname):
        self.sem(semname)
        return self.cnt[semname]

    def _last_real(self, eng):
        for item in reversed(self.q[eng]):
            if item[0] == "op" and item[1] not in ("nop", "dma_start"):
                return item
        return None

    def bump(self, eng, sem):
        self.sem(sem)
        item = self._last_real(eng)
        if item is not None and any(s0 == sem for s0, _ in item[4]):
            return
        if item is not None and not item[4]:
            self.cnt[sem] += 1
            item[4].append((sem, 1))
            return
        if item is None:
            self.cnt[sem] += 1
            self.q[eng].append(("op", "nop", (), {}, [(sem, 1)]))
            return
        self.cnt[sem] += 1
        if eng == "vector":
            self.q[eng].append(("op", "memset", (self.fence_buf[0:1, 0:1], 0.0),
                                {}, [(sem, 1)]))
        elif eng == "scalar":
            self.q[eng].append(("op", "copy",
                                (self.fence_buf[0:1, 1:2], self.fence_buf[0:1, 0:1]),
                                {}, [(sem, 1)]))
        elif eng == "gpsimd":
            self.q[eng].append(("op", "memset", (self.fence_buf[0:1, 2:3], 0.0),
                                {}, [(sem, 1)]))
        else:
            raise RuntimeError("no filler op for engine %s (sem %s)" % (eng, sem))

    def _fence(self, eng):
        item = self._last_real(eng)
        if item is not None and item[4]:
            s0, _ = item[4][-1]
            return (s0, self.cnt[s0])
        s0 = "fn_" + eng
        self.sem(s0)
        self.cnt[s0] += 1
        if item is not None:
            item[4].append((s0, 1))
        else:
            self.q[eng].append(("op", "nop", (), {}, [(s0, 1)]))
        return (s0, self.cnt[s0])

    def bar(self, engs=ENGS):
        fences = {e: self._fence(e) for e in engs}
        dw = self.cnt.get("dw", 0)
        dg = self.cnt.get("dg", 0)
        for e in engs:
            for e2 in engs:
                if e2 != e:
                    self.wait(e, *fences[e2])
            self.wait(e, "dw", dw)
            self.wait(e, "dg", dg)

    def finalize(self):
        nc = self.nc
        with contextlib.ExitStack() as st:
            handles = {n: st.enter_context(nc.semaphore(n)) for n in self.order}
            blk = st.enter_context(nc.Block())

            def make_runner(lst):
                def runner(E):
                    for item in lst:
                        if item[0] == "op":
                            _, meth, args, kw, incs = item
                            r = getattr(E, meth)(*args, **kw)
                            for s, k in incs:
                                r = r.then_inc(handles[s], k)
                        else:
                            _, s, v = item
                            E.wait_ge(handles[s], v)
                return runner

            for eng in ENGS:
                getattr(blk, eng)(make_runner(self.q[eng]))



def _host_consts():
    c = {}
    inv_freq = (1.0 / 10000.0 ** (np.arange(0, DH, 2, dtype=np.float32) / DH)).astype(np.float32)
    t = np.arange(S, dtype=np.float32)
    freqs = t[:, None] * inv_freq[None, :]
    emb = np.concatenate([freqs, freqs], axis=-1).astype(np.float32)
    cos = np.cos(emb).astype(np.float32)
    sin = np.sin(emb).astype(np.float32)
    ssin = sin.copy()
    ssin[:, :DH // 2] *= -1.0
    c["cosT"] = np.ascontiguousarray(np.tile(cos.T, (2, 1)))
    c["ssinT"] = np.ascontiguousarray(np.tile(ssin.T, (2, 1)))
    i = np.arange(128)
    c["maskd"] = (i[None, :] >= i[:, None]).astype(np.float32)
    c["mask4"] = (i[None, :] < i[:, None]).astype(np.float32)
    c["ident"] = np.eye(128, dtype=np.float32)
    c["ident16"] = np.eye(128, dtype=np.float16)
    c["onesr"] = np.ones((128, 128), dtype=np.float32)
    ind2 = np.zeros((128, 2), dtype=np.float32)
    ind2[:64, 0] = 1.0
    ind2[64:, 1] = 1.0
    c["ind2"] = ind2
    c["ind2T"] = np.ascontiguousarray(ind2.T)
    stack4 = np.zeros((32, 128), dtype=np.float32)
    for b in range(4):
        stack4[:, 32 * b:32 * (b + 1)] = np.eye(32, dtype=np.float32)
    c["stack4"] = stack4
    sels = np.zeros((128, 8 * 128), dtype=np.float32)
    for q in range(32):
        ty, j = divmod(q, 8)
        base = 32 * (q % 4)
        fb = 128 * (q // 4)
        sgn = -1.0 if ty == 1 else 1.0
        off = 0 if ty in (0, 1) else 16
        for p in range(128):
            r = off + 2 * j + (1 if p >= 64 else 0)
            sels[base + r, fb + p] = sgn
    c["sels"] = sels
    return c


def _host_stage(inputs):
    f = np.ascontiguousarray
    cm = dict(_host_consts())

    cm["l0_wq"] = f(inputs["L0_attn_Wq"].T.reshape(D, NG, 128).transpose(1, 0, 2))
    cm["l0_wk"] = f(inputs["L0_attn_Wk"].T.reshape(D, NG, 128).transpose(1, 0, 2))
    cm["l0_wv"] = f(inputs["L0_attn_Wv"].T.reshape(D, NG, 128).transpose(1, 0, 2))
    cm["l0_wo"] = f(inputs["L0_attn_Wo"].T.reshape(NG, 128, D))
    for nm, key in (("l0_bq", "L0_attn_bq"), ("l0_bk", "L0_attn_bk"),
                    ("l0_bv", "L0_attn_bv"), ("l0_bo", "L0_attn_bo"),
                    ("l0_cqb", "L0_attn_cqb"), ("l0_ckb", "L0_attn_ckb"),
                    ("l0_cvb", "L0_attn_cvb")):
        cm[nm] = f(inputs[key].reshape(NG, 128).T)
    for nm, key in (("l0_cq", "L0_attn_cqw"), ("l0_ck", "L0_attn_ckw"),
                    ("l0_cv", "L0_attn_cvw")):
        cm[nm] = f(inputs[key].reshape(NG, 128, KS).transpose(1, 0, 2).reshape(128, NG * KS))

    cm["l0_k0"] = f(inputs["L0_ffn_k0"].view(np.uint8))
    cm["l0_k1"] = f(inputs["L0_ffn_k1"].view(np.uint8))
    cm["l0_k2"] = f(inputs["L0_ffn_k2"].view(np.uint8))
    cm["l0_s"] = f(np.stack([np.asarray(inputs["L0_ffn_s0"]),
                             np.asarray(inputs["L0_ffn_s1"]),
                             np.asarray(inputs["L0_ffn_s2"])]).reshape(1, 3).astype(np.float32))

    qkv = np.stack([inputs["L1_attn_Wq"].T, inputs["L1_attn_Wk"].T,
                    inputs["L1_attn_Wv"].T])
    cm["l1_wqkv"] = f(qkv.reshape(3, D, NG, 128).transpose(0, 2, 1, 3))
    cm["l1_wo"] = f(inputs["L1_attn_Wo"].T.reshape(NG, 128, D))
    cm["l1_wlr"] = f(inputs["L1_attn_Wlr"].T)
    cm["l1_wbeta"] = f(inputs["L1_attn_Wbeta"].T)
    wii = inputs["L1_attn_W_in_init"][0]
    woi = inputs["L1_attn_W_out_init"][0]
    cm["l1_winit"] = f(wii.transpose(1, 2, 0).reshape(NG, 2 * DH, DH))
    cm["l1_wout"] = f(woi.transpose(1, 0, 2).reshape(NG, 2 * DH, DH))
    cm["l1_woutT"] = f(woi.transpose(1, 2, 0).reshape(NG, 2 * DH, DH))
    cm["l1_fw"] = f(inputs["L1_ffn_weight"].T)

    x = inputs["L0_attn_x"]
    pc = [{"x_in": f(x[b].T.reshape(NCH, 128, S))} for b in range(8)]
    return cm, pc



def build_program(taps=()):
    import os as _os
    _PH = int(_os.environ.get("KERNEL_PHASES", "9"))
    _SC = int(_os.environ.get("KERNEL_SCAN", "5"))
    nc = bass.Bass("TRN2", target_bir_lowering=False,
                   detect_race_conditions=not _os.environ.get("KERNEL_NO_RD"))
    taps = set(taps)

    dr = {}

    def din(name, shape, dt=F32):
        dr[name] = nc.dram_tensor(name, list(shape), dt, kind="ExternalInput")

    din("x_in", (NCH, 128, S))
    din("cosT", (128, S)); din("ssinT", (128, S))
    din("maskd", (128, 128)); din("mask4", (128, 128))
    din("ident", (128, 128)); din("ident16", (128, 128), F16)
    din("onesr", (128, 128))
    din("ind2", (128, 2)); din("ind2T", (2, 128))
    din("stack4", (32, 128)); din("sels", (128, 1024))
    din("l0_wq", (NG, D, 128)); din("l0_wk", (NG, D, 128)); din("l0_wv", (NG, D, 128))
    din("l0_wo", (NG, 128, D))
    for nm in ("l0_bq", "l0_bk", "l0_bv", "l0_bo", "l0_cqb", "l0_ckb", "l0_cvb"):
        din(nm, (128, NG))
    for nm in ("l0_cq", "l0_ck", "l0_cv"):
        din(nm, (128, NG * KS))
    din("l0_k0", (2 * D, 4096), U8)
    din("l0_k1", (4096, 4096), U8)
    din("l0_k2", (4096, D), U8)
    din("l0_s", (1, 3))
    din("l1_wqkv", (3, NG, D, 128)); din("l1_wo", (NG, 128, D))
    din("l1_wlr", (D, 32)); din("l1_wbeta", (D, 32))
    din("l1_winit", (NG, 128, DH)); din("l1_wout", (NG, 128, DH)); din("l1_woutT", (NG, 128, DH))
    din("l1_fw", (D, D))

    out = nc.dram_tensor("out", [NCH, 128, S], F32, kind="ExternalOutput")
    w0scr = nc.dram_tensor("w0scr", [16, NCH, 128, 256], F16, kind="Internal")
    w1scr = nc.dram_tensor("w1scr", [32, 32, 128, 128], F16, kind="Internal")
    w2scr = nc.dram_tensor("w2scr", [32, 128, D], F16, kind="Internal")
    wqscr = nc.dram_tensor("wqscr", [NCH, 128, D], F16, kind="Internal")

    x_sb = nc.alloc_sbuf_tensor("x_sb", [128, NCH * S], F32)
    xn_h = nc.alloc_sbuf_tensor("xn_h", [128, NCH * S], F16)
    xn_l = nc.alloc_sbuf_tensor("xn_l", [128, NCH * S], F16)
    ident = nc.alloc_sbuf_tensor("ident_b", [128, 128], F32)
    ident16 = nc.alloc_sbuf_tensor("ident16_b", [128, 128], F16)
    onesr = nc.alloc_sbuf_tensor("onesr_b", [128, 128], F32)
    bias = {}
    for nm in ("l0_bq", "l0_bk", "l0_bv", "l0_bo", "l0_cqb", "l0_ckb", "l0_cvb"):
        bias[nm] = nc.alloc_sbuf_tensor(nm + "_s", [128, NG], F32)
    convw = {}
    for nm in ("l0_cq", "l0_ck", "l0_cv"):
        convw[nm] = nc.alloc_sbuf_tensor(nm + "_s", [128, NG * KS], F32)
    s012 = nc.alloc_sbuf_tensor("s012_b", [1, 3], F32)
    s012b = nc.alloc_sbuf_tensor("s012b", [128, 3], F32)
    fence_buf = nc.alloc_sbuf_tensor("fence_buf", [1, 4], F32)
    wsb = nc.alloc_sbuf_tensor("wsb_p", [128, 8], F32)

    AR0 = ((nc.sbuf_base + 63) // 64) * 64
    TOPB = nc.sbuf_top
    _sz = {F32: 4, F16: 2, I32: 4, U8: 1}

    def at(name, shape, dt, off):
        assert AR0 + off + shape[1] * _sz[dt] <= TOPB, (
            "arena overflow: %s needs %d..%d top %d" %
            (name, AR0 + off, AR0 + off + shape[1] * _sz[dt], TOPB))
        return nc.alloc_sbuf_tensor_at(name, shape, dt, offset=AR0 + off)

    cosT = at("cosT_b", [128, S], F32, 0)
    ssinT = at("ssinT_b", [128, S], F32, 8192)
    maskd = at("maskd_b", [128, 128], F32, 16384)
    mask4 = at("mask4_b", [128, 128], F32, 16896)
    ind2T = at("ind2T_b", [2, 128], F32, 17408)
    ind2 = at("ind2_b", [128, 2], F32, 17920)
    vaug0 = at("vaug0", [128, 16 * 65], F32, 17952)
    vaug1 = at("vaug1", [128, 16 * 65], F32, 22112)
    pbuf = at("pbuf", [128, 2 * 128], F32, 26272)
    l0qkv = at("l0qkv", [128, 3 * S], F32, 27296)
    ln_sq = at("ln_sq", [128, 512], F32, 27296)
    ln_t0 = at("ln_t0", [128, 512], F32, 29344)
    lnr = at("lnr", [1, 2560], F32, 31392)
    att_scr = at("att_scr", [128, 512], F32, 51872)
    att_pad = at("att_pad", [128, 520], F32, 53920)
    og_h = at("og_h", [128, S], F16, 56000)
    og_l = at("og_l", [128, S], F16, 60096)
    land = at("land", [128, D], F32, 64192)
    w_h = at("w_h", [128, D], F16, 68288)
    w_l = at("w_l", [128, D], F16, 70336)
    rz = at("rz", [128, 128], F32, 72384)
    o_f = at("o_f", [128, 128], F32, 72896)
    oh_loc = at("oh_loc", [128, 128], F16, 73408)
    ol_loc = at("ol_loc", [128, 128], F16, 73664)
    att_s = at("att_s", [2, 1024], F32, 73920)
    pbuf4 = at("pbuf4", [128, 512], F32, 73920)

    k0b_a = at("k0b_a", [128, 4096], U8, 0)
    k0b_b = at("k0b_b", [128, 4096], U8, 4096)
    h0_h = at("h0_h", [128, 32 * FT], F16, 8192)
    h0_l = at("h0_l", [128, 32 * FT], F16, 24576)
    ftmp = at("ftmp", [128, 4096], F32, 8192)
    f16t = at("f16t", [128, 4096], F16, 24576)
    fb32 = at("fb32", [128, 2048], F32, 32768)
    w0bt = at("w0bt", [128, 2 * 2048], F16, 40960)
    w1bt = at("w1bt", [128, 2 * 4096], F16, 49152)
    w2bt = at("w2bt", [128, 2 * 1024], F16, 65536)
    h1t = at("h1t", [128, 512], F16, 69632)
    ffn_h0f = at("ffn_h0f", [128, FT], F32, 70656)

    scal = at("scal", [128, 1024], F32, 0)
    st_win = at("st_win", [128, NG * DH], F32, 4096)
    st_wout = at("st_wout", [128, NG * DH], F32, 6144)
    st_woutT = at("st_woutT", [128, NG * DH], F32, 8192)
    knd = at("knd", [128, NG * DH], F32, 10240)
    vnd = at("vnd", [128, NG * DH], F32, 12288)
    ptb = at("ptb", [128, NG * DH], F32, 14336)
    gzb = at("gzb", [128, NG * DH], F32, 16384)
    sm_s = at("sm_s", [128, 64], F32, 18432)
    pq_sb = at("pq_sb", [128, NG * 2 * DH], F32, 18688)
    scan_t = at("scan_t", [128, 64], F32, 22816)
    lrc_a = at("lrc_a", [32, NCK], F32, 23072)
    lrc_b = at("lrc_b", [32, NCK], F32, 23072 + 4 * NCK)
    lrc4_sb = at("lrc4_sb", [128, 2 * NCK], F32, 23328)
    sels_sb = at("sels_sb", [128, 1024], F32, 23584)
    stack4_sb = at("stack4_sb", [32, 128], F32, 27680)
    lr_sb = at("lr_sb", [32, S], F32, 28192)
    wlr_sb = at("wlr_sb", [128, 256], F32, 36384)
    wbe_sb = at("wbe_sb", [128, 256], F32, 37408)
    wlr_h = at("wlr_h", [128, 256], F16, 38432)
    wlr_l = at("wlr_l", [128, 256], F16, 38944)
    wbe_h = at("wbe_h", [128, 256], F16, 39456)
    wbe_l = at("wbe_l", [128, 256], F16, 39968)
    land2 = at("land2", [128, D], F32, 40480)
    u_t = at("u_t", [128, D], F32, 44576)
    g_t = at("g_t", [128, D], F32, 48672)
    absr = at("absr", [128, 16], F32, 52768)
    wq16c = at("wq16c", [128, D], F16, 52832)
    l1qkv = at("l1qkv", [128, 3 * S], F32, 23584)
    l1_o = at("l1_o", [128, S], F32, 48160)
    xt_sub = at("xt_sub", [128, 1024], F32, 48160)
    c16 = at("c16", [128, 1024], F16, 52256)
    rnd_m = at("rnd_m", [128, 1024], F32, 52256)
    land3b = at("land3b", [128, D], F32, 52256)
    rnd_t = at("rnd_t", [128, 1024], F32, 57376)
    ojh = at("ojh", [128, 256], F16, 56352)
    ojl = at("ojl", [128, 256], F16, 56864)
    land3 = at("land3", [128, D], F32, 57376)
    w3h = at("w3h", [128, D], F16, 61472)
    w3l = at("w3l", [128, D], F16, 63520)
    i32b = at("i32b", [128, 512], I32, 65568)
    wqb0 = at("wqb0", [128, D], F16, 67616)
    wqb1 = at("wqb1", [128, D], F16, 69664)
    ffn_s = at("ffn_s", [128, 768], F32, 71712)
    aqp16 = at("aqp16", [128, 128], F16, 74784)

    PS = [nc.alloc_psum_tensor("ps%d" % i, [128, 512], F32) for i in range(8)]

    P = Prog(nc)
    P.fence_buf = fence_buf
    DW, DG, DS = "dw", "dg", "ds"
    TT, VV, SS, OUT = "t", "v", "s", "o"
    for s_ in (DW, DG, DS, TT, VV, SS, OUT):
        P.sem(s_)

    def dma(eng, sem, out_, in__):
        P.op(eng, "dma_start", inc=[(sem, 16)], out=out_, in_=in__)

    tap_names = []

    def tap(name, ap, shape, dt=F32):
        if name not in taps:
            return
        t_ = nc.dram_tensor("tap_" + name, list(shape), dt, kind="ExternalOutput")
        tap_names.append("tap_" + name)
        P.bar()
        dma("gpsimd", OUT, t_.ap(), ap)
        P.bar()

    dma("gpsimd", DG, x_sb[:].rearrange("p (c t) -> p c t", t=S),
        dr["x_in"].ap().rearrange("c p t -> p c t"))
    for sb, nm in ((cosT, "cosT"), (ssinT, "ssinT"), (maskd, "maskd"),
                   (mask4, "mask4"), (ident, "ident"), (ident16, "ident16"),
                   (onesr, "onesr"),
                   (ind2, "ind2"), (ind2T, "ind2T"), (s012, "l0_s")):
        dma("sync", DW, sb[:], dr[nm].ap())
    for nm in bias:
        dma("sync", DW, bias[nm][:], dr[nm].ap())
    for nm in convw:
        dma("sync", DW, convw[nm][:], dr[nm].ap())
    for e in ("tensor", "vector", "scalar"):
        P.wait(e, DW)
        P.wait(e, DG)

    P.op("tensor", "matmul", PS[0][0:128, 0:3], onesr[0:1, 0:128], s012[0:1, 0:3],
         start=True, stop=True, inc=[(TT, 1)])
    P.wait("vector", TT)
    P.op("vector", "tensor_scalar", s012b[:, 0:2], PS[0][0:128, 0:2], 2.0, None, ALU.mult)
    P.op("vector", "tensor_copy", s012b[:, 2:3], PS[0][0:128, 2:3], inc=[(VV, 1)])
    P.bar()

    def emit_layernorm():
        for tt4 in range(4):
            for c in range(NCH):
                xc = x_sb[:, c * S + tt4 * 512: c * S + (tt4 + 1) * 512]
                P.wait("scalar", TT, P.mark(TT))
                P.op("scalar", "activation", ln_sq[:, 0:512], xc, AF.Square,
                     inc=[(SS, 1)])
                P.wait("tensor", SS)
                P.op("tensor", "matmul", PS[0][0:1, 0:512], onesr[0:128, 0:1], xc,
                     start=(c == 0), stop=(c == NCH - 1))
                P.op("tensor", "matmul", PS[1][0:1, 0:512], onesr[0:128, 0:1],
                     ln_sq[:, 0:512], start=(c == 0), stop=(c == NCH - 1),
                     inc=[(TT, 1)])
            P.wait("vector", TT)
            m_r = lnr[0:1, 0:512]
            v_r = lnr[0:1, 512:1024]
            r_r = lnr[0:1, 1024:1536]
            b_r = lnr[0:1, 1536:2048]
            t_r = lnr[0:1, 2048:2560]
            P.op("vector", "tensor_scalar", m_r, PS[0][0:1, 0:512], 1.0 / D, None, ALU.mult)
            P.op("vector", "tensor_scalar", v_r, PS[1][0:1, 0:512], 1.0 / D, None, ALU.mult)
            P.op("vector", "tensor_tensor", t_r, m_r, m_r, ALU.mult)
            P.op("vector", "tensor_tensor", v_r, v_r, t_r, ALU.subtract)
            P.op("vector", "tensor_scalar", v_r, v_r, EPS_LN, None, ALU.add)
            P.op("vector", "reciprocal", v_r, v_r, inc=[(VV, 1)])
            P.wait("scalar", VV)
            P.op("scalar", "activation", r_r, v_r, AF.Sqrt, inc=[(SS, 1)])
            P.wait("vector", SS)
            P.op("vector", "tensor_tensor", b_r, m_r, r_r, ALU.mult)
            P.op("vector", "tensor_scalar", b_r, b_r, -1.0, None, ALU.mult,
                 inc=[(VV, 1)])
            P.wait("tensor", VV)
            P.op("tensor", "matmul", PS[2][0:128, 0:512], onesr[0:1, 0:128], r_r,
                 start=True, stop=True)
            P.op("tensor", "matmul", PS[3][0:128, 0:512], onesr[0:1, 0:128], b_r,
                 start=True, stop=True, inc=[(TT, 1)])
            P.wait("vector", TT)
            for c in range(NCH):
                cs = slice(c * S + tt4 * 512, c * S + (tt4 + 1) * 512)
                t0 = ln_t0[:, 0:512]
                P.op("vector", "tensor_tensor", t0, x_sb[:, cs], PS[2][0:128, 0:512], ALU.mult)
                P.op("vector", "tensor_tensor", t0, t0, PS[3][0:128, 0:512], ALU.add)
                P.op("vector", "tensor_copy", xn_h[:, cs], t0)
                P.op("vector", "tensor_tensor", xn_l[:, cs], t0, xn_h[:, cs], ALU.subtract)
            P.bump("vector", VV)
            P.wait("tensor", VV)
        P.bar()

    emit_layernorm()
    tap("xn_h", xn_h[:], (128, NCH * S), F16)
    tap("xn_l", xn_l[:], (128, NCH * S), F16)

    q_c = l0qkv[:, 0:S]
    k_c = l0qkv[:, S:2 * S]
    v_c = l0qkv[:, 2 * S:3 * S]
    scr = att_scr
    pad = att_pad

    qkv_specs = [("l0_wq", "l0_bq", "l0_cq", "l0_cqb", q_c),
                 ("l0_wk", "l0_bk", "l0_ck", "l0_ckb", k_c),
                 ("l0_wv", "l0_bv", "l0_cv", "l0_cvb", v_c)]

    for g in (range(NG) if _PH >= 2 else ()):
        for pi, (wnm, bnm, cnm, cbnm, dest) in enumerate(qkv_specs):
            P.wait("sync", VV, P.mark(VV))
            dma("sync", DW, land[:].rearrange("p (c j) -> p c j", j=128),
                dr[wnm].ap()[g].rearrange("(c p) j -> p c j", p=128))
            P.wait("vector", DW)
            P.wait("vector", TT, P.mark(TT))
            P.op("vector", "tensor_copy", w_h[:], land[:])
            P.op("vector", "tensor_tensor", w_l[:], land[:], w_h[:], ALU.subtract,
                 inc=[(VV, 1)])
            P.wait("tensor", VV)
            for tt4 in range(4):
                pb = PS[4 + (tt4 % 2)]
                P.wait("tensor", SS, P.mark(SS))
                for c in range(NCH):
                    xh_ = xn_h[:, c * S + tt4 * 512: c * S + (tt4 + 1) * 512]
                    xl_ = xn_l[:, c * S + tt4 * 512: c * S + (tt4 + 1) * 512]
                    lw_h = w_h[:, c * 128:(c + 1) * 128]
                    lw_l = w_l[:, c * 128:(c + 1) * 128]
                    P.op("tensor", "matmul", pb[0:128, 0:512], lw_h, xh_,
                         start=(c == 0), stop=False)
                    P.op("tensor", "matmul", pb[0:128, 0:512], lw_h, xl_,
                         start=False, stop=False)
                    P.op("tensor", "matmul", pb[0:128, 0:512], lw_l, xh_,
                         start=False, stop=(c == NCH - 1),
                         inc=([(TT, 1)] if c == NCH - 1 else ()))
                P.wait("scalar", TT)
                P.wait("scalar", VV, P.mark(VV))
                P.op("scalar", "activation", pad[:, 3:515], pb[0:128, 0:512],
                     AF.Identity, bias=bias[bnm][:, g:g + 1], scale=1.0, inc=[(SS, 1)])
                P.wait("vector", SS)
                if tt4 == 0:
                    P.op("vector", "memset", pad[:, 0:3], 0.0)
                dsl = slice(tt4 * 512, (tt4 + 1) * 512)
                cw = convw[cnm]
                P.op("vector", "tensor_scalar", dest[:, dsl], pad[:, 0:512],
                     cw[:, g * KS:g * KS + 1], bias[cbnm][:, g:g + 1],
                     ALU.mult, ALU.add)
                for kk in range(1, KS):
                    P.op("vector", "scalar_tensor_tensor", dest[:, dsl],
                         pad[:, kk:kk + 512], cw[:, g * KS + kk:g * KS + kk + 1],
                         dest[:, dsl], ALU.mult, ALU.add)
                P.op("vector", "tensor_copy", pad[:, 0:3], pad[:, 512:515],
                     inc=[(VV, 1)])
        for dest in (q_c, k_c):
            for tt4 in range(4):
                dsl = slice(tt4 * 512, (tt4 + 1) * 512)
                P.op("vector", "tensor_copy", scr[0:32, 0:512], dest[32:64, dsl])
                P.op("vector", "tensor_copy", scr[32:64, 0:512], dest[0:32, dsl])
                P.op("vector", "tensor_copy", scr[64:96, 0:512], dest[96:128, dsl])
                P.op("vector", "tensor_copy", scr[96:128, 0:512], dest[64:96, dsl])
                P.op("vector", "tensor_tensor", dest[:, dsl], dest[:, dsl],
                     cosT[:, dsl], ALU.mult)
                P.op("vector", "tensor_tensor", scr[:, 0:512], scr[:, 0:512],
                     ssinT[:, dsl], ALU.mult)
                P.op("vector", "tensor_tensor", dest[:, dsl], dest[:, dsl],
                     scr[:, 0:512], ALU.add)
            P.bump("vector", VV)
        for dest in (q_c, k_c):
            for tt4 in range(4):
                dsl = slice(tt4 * 512, (tt4 + 1) * 512)
                P.wait("scalar", VV, P.mark(VV))
                P.op("scalar", "activation", scr[:, 0:512], dest[:, dsl],
                     AF.Square, inc=[(SS, 1)])
                P.wait("tensor", SS)
                P.op("tensor", "matmul", PS[6][0:2, 0:512], ind2[0:128, 0:2],
                     scr[:, 0:512], start=True, stop=True, inc=[(TT, 1)])
                P.wait("vector", TT)
                P.op("vector", "tensor_scalar", att_s[0:2, 0:512], PS[6][0:2, 0:512],
                     EPS_L2, None, ALU.add)
                P.op("vector", "reciprocal", att_s[0:2, 0:512], att_s[0:2, 0:512],
                     inc=[(VV, 1)])
                P.wait("scalar", VV)
                P.op("scalar", "activation", att_s[0:2, 512:1024], att_s[0:2, 0:512],
                     AF.Sqrt, inc=[(SS, 1)])
                P.wait("tensor", SS)
                P.op("tensor", "matmul", PS[6][0:128, 0:512], ind2T[0:2, 0:128],
                     att_s[0:2, 512:1024], start=True, stop=True, inc=[(TT, 1)])
                P.wait("vector", TT)
                P.op("vector", "tensor_tensor", dest[:, dsl], dest[:, dsl],
                     PS[6][0:128, 0:512], ALU.mult, inc=[(VV, 1)])
                P.wait("tensor", VV)
            P.bump("vector", VV)
        if g == 0:
            tap("g0_q", q_c, (128, S))
            tap("g0_k", k_c, (128, S))
            tap("g0_v", v_c, (128, S))
        P.wait("vector", VV, P.mark(VV))
        P.op("vector", "memset", vaug0[:], 1.0)
        P.op("vector", "memset", vaug1[:], 1.0, inc=[(VV, 1)])
        P.wait("tensor", VV)
        for h in range(2):
            va = vaug0 if h == 0 else vaug1
            for tt16 in range(16):
                P.op("tensor", "transpose", PS[6][0:128, 0:64],
                     v_c[h * 64:(h + 1) * 64, tt16 * 128:(tt16 + 1) * 128],
                     ident[h * 64:(h + 1) * 64, h * 64:h * 64 + 64],
                     inc=[(TT, 1)])
                P.wait("vector", TT)
                P.op("vector", "tensor_copy", va[:, tt16 * 65:tt16 * 65 + 64],
                     PS[6][0:128, 0:64], inc=[(VV, 1)])
                P.wait("tensor", VV)
        P.wait("sync", VV, P.mark(VV))
        dma("sync", DW, land[:], dr["l0_wo"].ap()[g])
        P.wait("vector", DW)
        P.op("vector", "tensor_copy", w_h[:], land[:])
        P.op("vector", "tensor_tensor", w_l[:], land[:], w_h[:], ALU.subtract,
             inc=[(VV, 1)])
        for qt in range(16):
            kbs = list(range(max(0, qt - 4), qt + 1))
            for h in range(2):
                va = vaug0 if h == 0 else vaug1
                hs = slice(h * 64, (h + 1) * 64)
                po = PS[4 + h]
                P.wait("tensor", VV, P.mark(VV))
                P.wait("tensor", SS, P.mark(SS))
                exp_marks = {}
                mask_marks = {}

                def emit_scores(ki, kb, h=h, hs=hs, qt=qt):
                    psc = PS[2 * h + ki % 2]
                    P.op("tensor", "matmul", psc[0:128, 0:128],
                         k_c[hs, kb * 128:(kb + 1) * 128],
                         q_c[hs, qt * 128:(qt + 1) * 128],
                         start=True, stop=True, inc=[(TT, 1)])

                def emit_exp(ki, kb, h=h, qt=qt):
                    psc = PS[2 * h + ki % 2]
                    pt = pbuf4[:, (2 * h + ki % 2) * 128:(2 * h + ki % 2) * 128 + 128]
                    P.wait("scalar", TT, P.mark(TT))
                    P.op("scalar", "activation", pt, psc[0:128, 0:128], AF.Exp,
                         scale=float(SCALE), inc=[(SS, 1)])
                    exp_marks[ki] = P.mark(SS)
                    if kb == qt or kb == qt - 4:
                        P.wait("vector", SS, exp_marks[ki])
                        mk = maskd if kb == qt else mask4
                        P.op("vector", "tensor_tensor", pt, pt, mk[:], ALU.mult,
                             inc=[(VV, 1)])
                        mask_marks[ki] = P.mark(VV)

                emit_scores(0, kbs[0])
                emit_exp(0, kbs[0])
                for ki, kb in enumerate(kbs):
                    if ki + 1 < len(kbs):
                        emit_scores(ki + 1, kbs[ki + 1])
                        emit_exp(ki + 1, kbs[ki + 1])
                    pt = pbuf4[:, (2 * h + ki % 2) * 128:(2 * h + ki % 2) * 128 + 128]
                    if ki in mask_marks:
                        P.wait("tensor", VV, mask_marks[ki])
                    else:
                        P.wait("tensor", SS, exp_marks[ki])
                    P.op("tensor", "matmul", po[0:65, 0:128],
                         va[:, kbs[ki] * 65:(kbs[ki] + 1) * 65], pt,
                         start=(ki == 0), stop=(ki == len(kbs) - 1),
                         inc=([(TT, 1)] if ki == len(kbs) - 1 else ()))
                P.wait("vector", TT)
                P.op("vector", "reciprocal", rz[64:65, 0:128], po[64:65, 0:128],
                     inc=[(VV, 1)])
                P.wait("scalar", TT)
                P.op("scalar", "activation", o_f[0:64, 0:128], po[0:64, 0:128],
                     AF.Copy, inc=[(SS, 1)])
                P.wait("tensor", VV)
                P.op("tensor", "matmul", PS[6][0:64, 0:128], onesr[64:65, 0:64],
                     rz[64:65, 0:128], start=True, stop=True, inc=[(TT, 1)])
                P.wait("vector", TT)
                P.wait("vector", SS)
                P.op("vector", "tensor_tensor", o_f[0:64, 0:128], o_f[0:64, 0:128],
                     PS[6][0:64, 0:128], ALU.mult)
                P.op("vector", "tensor_copy", oh_loc[0:64, 0:128], o_f[0:64, 0:128])
                P.op("vector", "tensor_tensor", ol_loc[0:64, 0:128],
                     o_f[0:64, 0:128], oh_loc[0:64, 0:128], ALU.subtract)
                P.op("vector", "tensor_copy",
                     og_h[h * 64:(h + 1) * 64, qt * 128:(qt + 1) * 128],
                     oh_loc[0:64, 0:128])
                P.op("vector", "tensor_copy",
                     og_l[h * 64:(h + 1) * 64, qt * 128:(qt + 1) * 128],
                     ol_loc[0:64, 0:128], inc=[(VV, 1)])
                P.wait("tensor", VV)
        if g == 0:
            tap("g0_og", og_h[:], (128, S), F16)
        P.wait("tensor", VV, P.mark(VV))
        for mc in range(NCH):
            for tt4 in range(4):
                pb = PS[4 + (tt4 % 2)]
                P.wait("tensor", VV, P.mark(VV))
                sl = slice(tt4 * 512, (tt4 + 1) * 512)
                P.op("tensor", "matmul", pb[0:128, 0:512],
                     w_h[:, mc * 128:(mc + 1) * 128], og_h[:, sl],
                     start=True, stop=False)
                P.op("tensor", "matmul", pb[0:128, 0:512],
                     w_h[:, mc * 128:(mc + 1) * 128], og_l[:, sl],
                     start=False, stop=False)
                P.op("tensor", "matmul", pb[0:128, 0:512],
                     w_l[:, mc * 128:(mc + 1) * 128], og_h[:, sl],
                     start=False, stop=True, inc=[(TT, 1)])
                P.wait("vector", TT)
                xs = x_sb[:, mc * S + tt4 * 512: mc * S + (tt4 + 1) * 512]
                if g == NG - 1:
                    P.op("vector", "scalar_tensor_tensor", xs, pb[0:128, 0:512],
                         bias["l0_bo"][:, mc:mc + 1], xs, ALU.add, ALU.add,
                         inc=[(VV, 1)])
                else:
                    P.op("vector", "tensor_tensor", xs, xs, pb[0:128, 0:512],
                         ALU.add, inc=[(VV, 1)])
        P.bar(("tensor", "vector", "scalar", "sync"))
    P.bar()
    tap("x_attn", x_sb[:], (128, NCH * S))

    for c in (range(NCH) if _PH >= 3 else ()):
        P.wait("sync", VV, P.mark(VV))
        dma("sync", DW, k0b_a[:], dr["l0_k0"].ap()[c * 128:(c + 1) * 128, :])
        dma("sync", DW, k0b_b[:], dr["l0_k0"].ap()[D + c * 128:D + (c + 1) * 128, :])
        P.wait("vector", DW)
        P.wait("vector", DG, P.mark(DG))
        P.op("vector", "tensor_scalar", ftmp[:], k0b_a[:], 2.0, None, ALU.mult)
        P.op("vector", "tensor_scalar", fb32[:], k0b_b[:, 0:2048], 2.0, None, ALU.mult)
        P.op("vector", "tensor_tensor", f16t[:, 0:2048], ftmp[:, 0:2048], fb32[:],
             ALU.subtract)
        P.op("vector", "tensor_scalar", fb32[:], k0b_b[:, 2048:4096], 2.0, None, ALU.mult)
        P.op("vector", "tensor_tensor", f16t[:, 2048:4096], ftmp[:, 2048:4096],
             fb32[:], ALU.subtract, inc=[(VV, 1)])
        P.wait("gpsimd", VV)
        dma("gpsimd", DG, w0scr.ap()[:, c, :, :].rearrange("s p m -> p s m"),
            f16t[:].rearrange("p (s m) -> p s m", m=256))
    for c in (range(32) if _PH >= 3 else ()):
        kb_buf = k0b_a if c % 2 == 0 else k0b_b
        P.wait("sync", VV, P.mark(VV))
        dma("sync", DW, kb_buf[:], dr["l0_k1"].ap()[c * 128:(c + 1) * 128, :])
        P.wait("vector", DW)
        P.wait("vector", DG, P.mark(DG))
        P.op("vector", "tensor_scalar", f16t[:], kb_buf[:], 2.0, -1.0, ALU.mult,
             ALU.add, inc=[(VV, 1)])
        P.wait("gpsimd", VV)
        dma("gpsimd", DG, w1scr.ap()[:, c, :, :].rearrange("s p m -> p s m"),
            f16t[:].rearrange("p (s m) -> p s m", m=128))
    for c in (range(32) if _PH >= 3 else ()):
        kb_buf = k0b_a if c % 2 == 0 else k0b_b
        P.wait("sync", VV, P.mark(VV))
        dma("sync", DW, kb_buf[:, 0:1024], dr["l0_k2"].ap()[c * 128:(c + 1) * 128, :])
        P.wait("vector", DW)
        P.wait("vector", DG, P.mark(DG))
        P.op("vector", "tensor_scalar", f16t[:, 0:1024], kb_buf[:, 0:1024], 2.0,
             -1.0, ALU.mult, ALU.add, inc=[(VV, 1)])
        P.wait("gpsimd", VV)
        dma("gpsimd", DG, w2scr.ap()[c], f16t[:, 0:1024])
    P.bar()

    w0b = [w0bt[:, 0:2048], w0bt[:, 2048:4096]]
    w1b = [w1bt[:, 0:4096], w1bt[:, 4096:8192]]
    w2b = [w2bt[:, 0:1024], w2bt[:, 1024:2048]]
    h1h = h1t[:, 0:256]
    h1l = h1t[:, 256:512]
    h0f = ffn_h0f[:, 0:FT]

    for t in (range(NFT) if _PH >= 4 else ()):
        def tsl(c, t=t):
            return slice(c * S + t * FT, c * S + (t + 1) * FT)
        for sidx in range(16):
            wb = w0b[sidx % 2]
            P.wait("sync", TT, P.mark(TT))
            dma("sync", DW, wb.rearrange("p (c m) -> p c m", m=256),
                w0scr.ap()[sidx].rearrange("c p m -> p c m"))
            P.wait("tensor", DW)
            for mcc in range(2):
                pb = PS[0]
                psl = slice(mcc * 128, mcc * 128 + 128) if False else slice(0, FT)
                P.wait("tensor", VV, P.mark(VV))
                for c in range(NCH):
                    lw = wb[:, c * 256 + mcc * 128: c * 256 + mcc * 128 + 128]
                    P.op("tensor", "matmul", pb[0:128, 0:FT], lw, xn_h[:, tsl(c)],
                         start=(c == 0), stop=False)
                    P.op("tensor", "matmul", pb[0:128, 0:FT], lw, xn_l[:, tsl(c)],
                         start=False, stop=(c == NCH - 1),
                         inc=([(TT, 1)] if c == NCH - 1 else ()))
                P.wait("vector", TT)
                mo = sidx * 2 + mcc
                P.op("vector", "tensor_scalar", h0f, pb[0:128, 0:FT],
                     s012b[:, 0:1], 0.0, ALU.mult, ALU.max, inc=[(VV, 1)])
                P.wait("scalar", VV)
                P.op("scalar", "activation", h0_h[:, mo * FT:(mo + 1) * FT], h0f,
                     AF.Copy, inc=[(SS, 1)])
                P.wait("vector", SS)
                P.op("vector", "tensor_tensor", h0_l[:, mo * FT:(mo + 1) * FT],
                     h0f, h0_h[:, mo * FT:(mo + 1) * FT], ALU.subtract,
                     inc=[(VV, 1)])
        P.bar(("tensor", "vector", "scalar", "sync"))
        if t == 0:
            tap("h0_t0", h0_h[:], (128, 32 * FT), F16)
        m1_marks = {}
        m1_evac = {}
        for s2 in range(32):
            if s2 == 0:
                dma("sync", DW, w1b[0].rearrange("p (c m) -> p c m", m=128),
                    w1scr.ap()[0].rearrange("c p m -> p c m"))
                dma("sync", DW, w2b[0], w2scr.ap()[0])
                m1_marks[0] = P.mark(DW)
            if s2 + 1 < 32:
                P.wait("sync", TT, P.mark(TT))
                dma("sync", DW, w1b[(s2 + 1) % 2].rearrange("p (c m) -> p c m", m=128),
                    w1scr.ap()[s2 + 1].rearrange("c p m -> p c m"))
                dma("sync", DW, w2b[(s2 + 1) % 2], w2scr.ap()[s2 + 1])
                m1_marks[s2 + 1] = P.mark(DW)
            wb = w1b[s2 % 2]
            w2bb = w2b[s2 % 2]
            P.wait("tensor", DW, m1_marks[s2])
            pb = (PS[1], PS[3])[s2 % 2]
            if s2 >= 2:
                P.wait("tensor", VV, m1_evac[s2 - 2])
            P.wait("tensor", SS, P.mark(SS))
            for c in range(32):
                lw = wb[:, c * 128:(c + 1) * 128]
                P.op("tensor", "matmul", pb[0:128, 0:FT], lw,
                     h0_h[:, c * FT:(c + 1) * FT], start=(c == 0), stop=False)
                P.op("tensor", "matmul", pb[0:128, 0:FT], lw,
                     h0_l[:, c * FT:(c + 1) * FT], start=False, stop=(c == 31),
                     inc=([(TT, 1)] if c == 31 else ()))
            P.wait("vector", TT)
            P.op("vector", "tensor_scalar", h0f, pb[0:128, 0:FT],
                 s012b[:, 1:2], 0.0, ALU.mult, ALU.max, inc=[(VV, 1)])
            m1_evac[s2] = P.mark(VV)
            P.wait("scalar", VV)
            P.op("scalar", "activation", h1h[:, 0:FT], h0f, AF.Copy, inc=[(SS, 1)])
            P.wait("vector", SS)
            P.op("vector", "tensor_tensor", h1l[:, 0:FT], h0f, h1h[:, 0:FT],
                 ALU.subtract, inc=[(VV, 1)])
            P.wait("tensor", VV)
            for mc2 in range(NCH):
                yb = PS[4 + mc2 // 2]
                ysl = slice((mc2 % 2) * 256, (mc2 % 2) * 256 + FT)
                P.op("tensor", "matmul", yb[0:128, ysl],
                     w2bb[:, mc2 * 128:(mc2 + 1) * 128], h1h[:, 0:FT],
                     start=(s2 == 0 and mc2 % 2 == 0), stop=False)
                P.op("tensor", "matmul", yb[0:128, ysl],
                     w2bb[:, mc2 * 128:(mc2 + 1) * 128], h1l[:, 0:FT],
                     start=False, stop=(s2 == 31 and mc2 % 2 == 1),
                     inc=([(TT, 1)] if (s2 == 31 and mc2 == NCH - 1) else ()))
        P.wait("vector", TT)
        for mc2 in range(NCH):
            yb = PS[4 + mc2 // 2]
            ysl = slice((mc2 % 2) * 256, (mc2 % 2) * 256 + FT)
            xs = x_sb[:, tsl(mc2)]
            P.op("vector", "scalar_tensor_tensor", xs, yb[0:128, ysl],
                 s012b[:, 2:3], xs, ALU.mult, ALU.add, inc=[(VV, 1)])
        P.wait("tensor", VV)
        P.bar(("tensor", "vector", "scalar", "sync"))
    P.bar()
    tap("x_l0", x_sb[:], (128, NCH * S))

    if _PH < 5:
        dma("gpsimd", OUT, out.ap().rearrange("c p t -> p c t"),
            x_sb[:].rearrange("p (c t) -> p c t", t=S))
        P.wait("gpsimd", OUT)
        P.finalize()
        return nc

    emit_layernorm()

    dma("sync", DW, sels_sb[:], dr["sels"].ap())
    dma("sync", DW, stack4_sb[:], dr["stack4"].ap())
    dma("sync", DW, wlr_sb[:].rearrange("p (c j) -> p c j", j=32),
        dr["l1_wlr"].ap().rearrange("(c p) j -> p c j", p=128))
    dma("sync", DW, wbe_sb[:].rearrange("p (c j) -> p c j", j=32),
        dr["l1_wbeta"].ap().rearrange("(c p) j -> p c j", p=128))
    P.wait("vector", DW)
    for wf, wh_, wl_ in ((wlr_sb, wlr_h, wlr_l), (wbe_sb, wbe_h, wbe_l)):
        P.op("vector", "tensor_copy", wh_[:], wf[:])
        P.op("vector", "tensor_tensor", wl_[:], wf[:], wh_[:], ALU.subtract,
             inc=[(VV, 1)])
    P.wait("tensor", VV)
    for (wh_, wl_, scl, lrc_dst) in ((wlr_h, wlr_l, 0.001 / CH, lrc_a),
                                     (wbe_h, wbe_l, 0.9 / CH, lrc_b)):
        P.wait("scalar", VV, P.mark(VV))
        for tt4 in range(4):
            pb = PS[0]
            sl = slice(tt4 * 512, (tt4 + 1) * 512)
            P.wait("tensor", SS, P.mark(SS))
            for c in range(NCH):
                xh_ = xn_h[:, c * S + tt4 * 512: c * S + (tt4 + 1) * 512]
                xl_ = xn_l[:, c * S + tt4 * 512: c * S + (tt4 + 1) * 512]
                P.op("tensor", "matmul", pb[0:32, 0:512],
                     wh_[:, c * 32:(c + 1) * 32], xh_, start=(c == 0), stop=False)
                P.op("tensor", "matmul", pb[0:32, 0:512],
                     wh_[:, c * 32:(c + 1) * 32], xl_, start=False, stop=False)
                P.op("tensor", "matmul", pb[0:32, 0:512],
                     wl_[:, c * 32:(c + 1) * 32], xh_, start=False,
                     stop=(c == NCH - 1),
                     inc=([(TT, 1)] if c == NCH - 1 else ()))
            P.wait("scalar", TT)
            for pc8 in range(8):
                cck = tt4 * 8 + pc8
                P.op("scalar", "activation", lr_sb[0:32, 0:CH],
                     pb[0:32, pc8 * CH:(pc8 + 1) * CH], AF.Sigmoid,
                     accum_out=scan_t[0:32, cck:cck + 1],
                     inc=([(SS, 1)] if pc8 == 7 else ()))
        P.wait("vector", SS)
        P.op("vector", "tensor_scalar", lrc_dst[:], scan_t[0:32, 0:NCK],
             float(scl), None, ALU.mult, inc=[(VV, 1)])
    tap("lrc_aT", lrc_a[:], (32, NCK))
    tap("lrc_bT", lrc_b[:], (32, NCK))
    P.wait("tensor", VV)
    P.op("tensor", "matmul", PS[2][0:128, 0:NCK], stack4_sb[0:32, 0:128],
         lrc_a[:], start=True, stop=True)
    P.op("tensor", "matmul", PS[2][0:128, NCK:2 * NCK], stack4_sb[0:32, 0:128],
         lrc_b[:], start=True, stop=True, inc=[(TT, 1)])
    P.wait("vector", TT)
    P.op("vector", "tensor_copy", lrc4_sb[:], PS[2][0:128, 0:2 * NCK],
         inc=[(VV, 1)])
    P.wait("tensor", VV)
    for q in range(32):
        ty, j = divmod(q, 8)
        base = 32 * (q % 4)
        fb = 128 * (q // 4)
        src_ofs = NCK if ty in (0, 2) else 0
        P.op("tensor", "matmul", PS[3][0:128, 0:NCK],
             sels_sb[base:base + 32, fb:fb + 128],
             lrc4_sb[base:base + 32, src_ofs:src_ofs + NCK],
             start=True, stop=True, tile_position=(base, 0), inc=[(TT, 1)])
        P.wait("vector", TT)
        P.op("vector", "tensor_copy", scal[:, q * NCK:(q + 1) * NCK],
             PS[3][0:128, 0:NCK], inc=[(VV, 1)])
        P.wait("tensor", VV)
    P.bar()

    for c in range(NCH):
        P.wait("sync", SS, P.mark(SS))
        dma("sync", DW, land2[:], dr["l1_fw"].ap()[c * 128:(c + 1) * 128, :])
        P.wait("scalar", DW)
        P.op("scalar", "activation", u_t[:], land2[:], AF.Abs,
             accum_out=absr[:, c:c + 1], inc=[(SS, 1)])
    P.wait("scalar", SS)
    P.op("scalar", "activation", u_t[:, 0:8], absr[:, 0:8], AF.Copy,
         accum_out=absr[:, 8:9], inc=[(SS, 1)])
    P.wait("tensor", SS)
    P.op("tensor", "matmul", PS[0][0:1, 0:1], onesr[0:128, 0:1], absr[:, 8:9],
         start=True, stop=True, inc=[(TT, 1)])
    P.wait("vector", TT)
    P.op("vector", "tensor_scalar", ffn_s[0:1, 0:1], PS[0][0:1, 0:1],
         1.0 / (D * D), EPS_Q, ALU.mult, ALU.add, inc=[(VV, 1)])
    P.wait("vector", VV)
    P.op("vector", "tensor_scalar", ffn_s[0:1, 1:2], ffn_s[0:1, 0:1],
         1.0 / 127.0, None, ALU.mult)
    P.op("vector", "reciprocal", ffn_s[0:1, 2:3], ffn_s[0:1, 0:1])
    P.op("vector", "tensor_scalar", ffn_s[0:1, 3:4], ffn_s[0:1, 0:1],
         0.5, None, ALU.mult, inc=[(VV, 1)])
    P.wait("tensor", VV)
    P.op("tensor", "matmul", PS[1][0:128, 0:3], onesr[0:1, 0:128],
         ffn_s[0:1, 1:4], start=True, stop=True, inc=[(TT, 1)])
    P.wait("vector", TT)
    P.op("vector", "tensor_copy", wsb[:, 0:3], PS[1][0:128, 0:3], inc=[(VV, 1)])
    for c in range(NCH):
        P.wait("sync", VV, P.mark(VV))
        dma("sync", DW, land2[:], dr["l1_fw"].ap()[c * 128:(c + 1) * 128, :])
        P.wait("vector", DW)
        P.wait("vector", DG, P.mark(DG))
        P.op("vector", "tensor_scalar", g_t[:], land2[:], wsb[:, 2:3], None,
             ALU.is_ge)
        P.op("vector", "tensor_scalar", u_t[:], land2[:], -1.0, wsb[:, 2:3],
             ALU.mult, ALU.is_ge)
        P.op("vector", "tensor_tensor", wq16c[:], g_t[:], u_t[:], ALU.subtract,
             inc=[(VV, 1)])
        P.wait("gpsimd", VV)
        dma("gpsimd", DG, wqscr.ap()[c], wq16c[:])
        if c == 0:
            tap("wq_c0", wq16c[:], (128, D), F16)
    if "wq_all" in taps:
        t_ = nc.dram_tensor("tap_wq_all", [NCH, 128, D], F16, kind="ExternalOutput")
        P.bar()
        dma("gpsimd", OUT, t_.ap(), wqscr.ap())
        P.bar()
    P.bar()

    for dst, nm in ((st_win, "l1_winit"), (st_wout, "l1_wout"),
                    (st_woutT, "l1_woutT")):
        dma("sync", DW, dst[:].rearrange("p (j d) -> p j d", d=DH),
            dr[nm].ap().rearrange("j p d -> p j d"))
    for e in ("tensor", "vector"):
        P.wait(e, DW)
    P.bar()

    if _PH < 6:
        dma("gpsimd", OUT, out.ap().rearrange("c p t -> p c t"),
            x_sb[:].rearrange("p (c t) -> p c t", t=S))
        P.wait("gpsimd", OUT)
        P.finalize()
        return nc

    q_l1 = l1qkv[:, 0:S]
    k_l1 = l1qkv[:, S:2 * S]
    v_l1 = l1qkv[:, 2 * S:3 * S]
    o_l1 = l1_o[:, 0:S]
    wqb = [wqb0, wqb1]

    for t in range(NL1T):
        lands = (land3, land3b)
        pj_marks = {}
        for pi in range(3):
            for j in range(NG):
                idx = pi * NG + j
                if idx == 0:
                    dma("sync", DW, lands[0][:].rearrange("p (c m) -> p c m", m=128),
                        dr["l1_wqkv"].ap()[0, 0].rearrange("(c p) m -> p c m", p=128))
                    pj_marks[0] = P.mark(DW)
                if idx + 1 < 24:
                    pi2, j2 = divmod(idx + 1, NG)
                    P.wait("sync", VV, P.mark(VV))
                    dma("sync", DW,
                        lands[(idx + 1) % 2][:].rearrange("p (c m) -> p c m", m=128),
                        dr["l1_wqkv"].ap()[pi2, j2].rearrange("(c p) m -> p c m", p=128))
                    pj_marks[idx + 1] = P.mark(DW)
                lnd = lands[idx % 2][:]
                P.wait("vector", DW, pj_marks[idx])
                P.wait("vector", TT, P.mark(TT))
                P.op("vector", "tensor_copy", w3h[:], lnd)
                P.op("vector", "tensor_tensor", w3l[:], lnd, w3h[:], ALU.subtract,
                     inc=[(VV, 1)])
                P.wait("tensor", VV)
                pb = PS[0]
                P.wait("tensor", SS, P.mark(SS))
                for c in range(NCH):
                    xh_ = xn_h[:, c * S + t * L1T: c * S + (t + 1) * L1T]
                    xl_ = xn_l[:, c * S + t * L1T: c * S + (t + 1) * L1T]
                    P.op("tensor", "matmul", pb[0:128, 0:L1T],
                         w3h[:, c * 128:(c + 1) * 128], xh_, start=(c == 0),
                         stop=False)
                    P.op("tensor", "matmul", pb[0:128, 0:L1T],
                         w3h[:, c * 128:(c + 1) * 128], xl_, start=False, stop=False)
                    P.op("tensor", "matmul", pb[0:128, 0:L1T],
                         w3l[:, c * 128:(c + 1) * 128], xh_, start=False,
                         stop=(c == NCH - 1),
                         inc=([(TT, 1)] if c == NCH - 1 else ()))
                P.wait("scalar", TT)
                dest = (q_l1, k_l1, v_l1)[pi]
                fn = AF.Silu if pi < 2 else AF.Copy
                P.op("scalar", "activation", dest[:, j * L1T:(j + 1) * L1T],
                     pb[0:128, 0:L1T], fn, inc=[(SS, 1)])
        P.bar(("tensor", "vector", "scalar", "sync"))
        if t == 0:
            tap("q_l1_t0", q_l1, (128, S))
        for cc in (range(CPT) if _PH >= 7 else ()):
            c_g = t * CPT + cc

            def csl(j, cc=cc):
                return slice(j * L1T + cc * CH, j * L1T + (cc + 1) * CH)

            P.wait("tensor", VV, P.mark(VV))
            for j in range(NG):
                kb_map = (PS[0], PS[1], PS[6], PS[7])
                for h in range(2):
                    hs = slice(h * 64, (h + 1) * 64)
                    P.op("tensor", "transpose", kb_map[2 * h][0:64, 0:64],
                         k_l1[hs, csl(j)], ident[hs, h * 64:h * 64 + 64])
                    P.op("tensor", "transpose", kb_map[2 * h + 1][0:64, 0:64],
                         v_l1[hs, csl(j)], ident[hs, h * 64:h * 64 + 64],
                         inc=([(TT, 1)] if h == 1 else ()))
                P.wait("vector", TT)
                for h in range(2):
                    hs = slice(h * 64, (h + 1) * 64)
                    P.op("vector", "tensor_copy", knd[hs, j * DH:(j + 1) * DH],
                         kb_map[2 * h][0:64, 0:64])
                    P.op("vector", "tensor_copy", vnd[hs, j * DH:(j + 1) * DH],
                         kb_map[2 * h + 1][0:64, 0:64],
                         inc=([(VV, 1)] if h == 1 else ()))
                P.wait("tensor", VV)
            for j in (range(NG) if _SC >= 1 else ()):
                zb = PS[2 + j // 4]
                zo = (j % 4) * 128
                for h in range(2):
                    hs = slice(h * 64, (h + 1) * 64)
                    P.op("tensor", "matmul", zb[hs, zo:zo + 64],
                         st_win[hs, j * DH:(j + 1) * DH], q_l1[hs, csl(j)],
                         start=True, stop=True,
                         tile_position=(h * 64, h * 64))
                    P.op("tensor", "matmul", zb[hs, zo + 64:zo + 128],
                         st_win[hs, j * DH:(j + 1) * DH], k_l1[hs, csl(j)],
                         start=True, stop=True,
                         tile_position=(h * 64, h * 64),
                         inc=([(TT, 1)] if (j % 4 == 3 and h == 1) else ()))
            P.wait("vector", TT)
            for j in (range(NG) if _SC >= 2 else ()):
                zb = PS[2 + j // 4]
                zo = (j % 4) * 128
                for qk in range(2):
                    idx = j * 2 + qk
                    P.op("vector", "reduce_max", sm_s[:, idx:idx + 1],
                         zb[0:128, zo + qk * 64: zo + qk * 64 + 64], axis=AX.X,
                         negate=True, inc=([(VV, 1)] if idx == 15 else ()))
            P.wait("scalar", VV)
            P.wait("scalar", TT, P.mark(TT))
            for j in (range(NG) if _SC >= 2 else ()):
                zb = PS[2 + j // 4]
                zo = (j % 4) * 128
                for qk in range(2):
                    idx = j * 2 + qk
                    eb = PS[4 + idx // 8]
                    eo = (idx % 8) * 64
                    P.op("scalar", "activation", eb[0:128, eo:eo + 64],
                         zb[0:128, zo + qk * 64: zo + qk * 64 + 64], AF.Exp,
                         bias=sm_s[:, idx:idx + 1], scale=1.0,
                         accum_out=sm_s[:, 32 + idx:33 + idx],
                         inc=([(SS, 1)] if idx == 15 else ()))
            P.wait("vector", SS)
            if _SC >= 2:
                P.op("vector", "reciprocal", sm_s[:, 48:64], sm_s[:, 32:48],
                     inc=[(VV, 1)])
                P.wait("vector", VV)
            for j in (range(NG) if _SC >= 2 else ()):
                for qk in range(2):
                    idx = j * 2 + qk
                    eb = PS[4 + idx // 8]
                    eo = (idx % 8) * 64
                    P.op("vector", "tensor_scalar",
                         pq_sb[:, idx * DH:(idx + 1) * DH],
                         eb[0:128, eo:eo + 64],
                         sm_s[:, 48 + idx:49 + idx], None, ALU.mult,
                         inc=([(VV, 1)] if idx == 15 else ()))
            P.wait("tensor", VV)
            if t == 0 and cc == 0:
                tap("sc_pq", pq_sb[:], (128, 1024))
                tap("sc_knd", knd[:], (128, 512))
                tap("sc_vnd", vnd[:], (128, 512))
                tap("sc_smax", sm_s[:, 0:16], (128, 16))
                tap("sc_ssum", sm_s[:, 32:64], (128, 32))
            for j in (range(NG) if _SC >= 3 else ()):
                pq_t = pq_sb[:, (2 * j) * DH:(2 * j) * DH + DH]
                p_t = pq_sb[:, (2 * j + 1) * DH:(2 * j + 1) * DH + DH]
                for h in range(2):
                    hs = slice(h * 64, (h + 1) * 64)
                    P.op("tensor", "matmul", PS[4][hs, 0:64],
                         st_wout[hs, j * DH:(j + 1) * DH], pq_t[hs, 0:DH],
                         start=True, stop=True, tile_position=(h * 64, h * 64))
                    P.op("tensor", "matmul", PS[5][hs, 0:64],
                         v_l1[hs, csl(j)], st_woutT[hs, j * DH:(j + 1) * DH],
                         start=True, stop=True, tile_position=(h * 64, h * 64))
                    P.op("tensor", "transpose",
                         (PS[6] if h == 0 else PS[7])[0:64, 0:64],
                         p_t[hs, 0:DH], ident[hs, h * 64:h * 64 + 64],
                         inc=([(TT, 1)] if h == 1 else ()))
                P.wait("vector", TT)
                P.op("vector", "tensor_copy",
                     o_l1[:, j * L1T + cc * CH: j * L1T + (cc + 1) * CH],
                     PS[4][0:128, 0:64])
                P.op("vector", "tensor_copy", ptb[0:64, j * DH:(j + 1) * DH],
                     PS[6][0:64, 0:64])
                P.op("vector", "tensor_copy", ptb[64:128, j * DH:(j + 1) * DH],
                     PS[7][0:64, 0:64])
                P.op("vector", "tensor_tensor",
                     gzb[:, j * DH:(j + 1) * DH], ptb[:, j * DH:(j + 1) * DH],
                     PS[5][0:128, 0:64], ALU.mult)
                P.op("vector", "reduce_sum", sm_s[:, j:j + 1],
                     gzb[:, j * DH:(j + 1) * DH], axis=AX.X, inc=[(VV, 1)])
                P.wait("vector", VV)
                P.op("vector", "tensor_scalar", scan_t[:, 0:64],
                     ptb[:, j * DH:(j + 1) * DH], sm_s[:, j:j + 1], None, ALU.mult)
                P.op("vector", "tensor_tensor", gzb[:, j * DH:(j + 1) * DH],
                     scan_t[:, 0:64], gzb[:, j * DH:(j + 1) * DH], ALU.subtract,
                     inc=[(VV, 1)])
                P.wait("tensor", VV)
            if t == 0 and cc == 0:
                tap("sc_ptb", ptb[:], (128, 512))
                tap("sc_gzb", gzb[:], (128, 512))
                tap("sc_ol", o_l1[:, 0:64], (128, 64))
            for j in (range(NG) if _SC >= 4 else ()):
                for h in range(2):
                    hs = slice(h * 64, (h + 1) * 64)
                    P.op("tensor", "matmul", PS[4][hs, 0:64],
                         ptb[hs, j * DH:(j + 1) * DH], vnd[hs, j * DH:(j + 1) * DH],
                         start=True, stop=True, tile_position=(h * 64, h * 64))
                    P.op("tensor", "matmul", PS[5][hs, 0:64],
                         vnd[hs, j * DH:(j + 1) * DH], ptb[hs, j * DH:(j + 1) * DH],
                         start=True, stop=True, tile_position=(h * 64, h * 64))
                    P.op("tensor", "matmul", PS[6][hs, 0:64],
                         knd[hs, j * DH:(j + 1) * DH], gzb[hs, j * DH:(j + 1) * DH],
                         start=True, stop=True, tile_position=(h * 64, h * 64),
                         inc=([(TT, 1)] if h == 1 else ()))
                P.wait("vector", TT)
                jc = j * NCK + c_g
                sl_ = slice(j * DH, (j + 1) * DH)
                P.op("vector", "tensor_scalar", st_wout[:, sl_], st_wout[:, sl_],
                     scal[:, 16 * NCK + jc:16 * NCK + jc + 1], None, ALU.mult)
                P.op("vector", "scalar_tensor_tensor", st_wout[:, sl_],
                     PS[4][0:128, 0:64],
                     scal[:, 24 * NCK + jc:24 * NCK + jc + 1],
                     st_wout[:, sl_], ALU.mult, ALU.add)
                P.op("vector", "tensor_scalar", st_woutT[:, sl_], st_woutT[:, sl_],
                     scal[:, 16 * NCK + jc:16 * NCK + jc + 1], None, ALU.mult)
                P.op("vector", "scalar_tensor_tensor", st_woutT[:, sl_],
                     PS[5][0:128, 0:64],
                     scal[:, 24 * NCK + jc:24 * NCK + jc + 1],
                     st_woutT[:, sl_], ALU.mult, ALU.add)
                P.op("vector", "tensor_scalar", st_win[:, sl_], st_win[:, sl_],
                     scal[:, 0 * NCK + jc:0 * NCK + jc + 1], None, ALU.mult)
                P.op("vector", "scalar_tensor_tensor", st_win[:, sl_],
                     PS[6][0:128, 0:64],
                     scal[:, 8 * NCK + jc:8 * NCK + jc + 1],
                     st_win[:, sl_], ALU.mult, ALU.add, inc=[(VV, 1)])
                P.wait("tensor", VV)
        if t == 0:
            tap("o_l1_t0", o_l1[:, 0:S], (128, S))
        if t == 1:
            tap("o_l1_t1", o_l1[:, 0:S], (128, S))
            tap("stwin_t1", st_win[:], (128, NG * DH))
        if t == 0:
            tap("sc_win", st_win[:], (128, 512))
        P.wait("tensor", VV, P.mark(VV))
        P.wait("tensor", SS, P.mark(SS))
        for j in (range(NG) if _PH >= 8 else ()):
            P.wait("sync", VV, P.mark(VV))
            dma("sync", DW, land3[:], dr["l1_wo"].ap()[j])
            P.wait("vector", DW)
            P.wait("vector", TT, P.mark(TT))
            P.op("vector", "tensor_copy", w3h[:], land3[:])
            P.op("vector", "tensor_tensor", w3l[:], land3[:], w3h[:], ALU.subtract)
            P.op("vector", "tensor_copy", ojh[:, 0:L1T], o_l1[:, j * L1T:(j + 1) * L1T])
            P.op("vector", "tensor_tensor", ojl[:, 0:L1T],
                 o_l1[:, j * L1T:(j + 1) * L1T], ojh[:, 0:L1T], ALU.subtract,
                 inc=[(VV, 1)])
            P.wait("tensor", VV)
            for mc in range(NCH):
                yb = PS[4 + mc // 2]
                ysl = slice((mc % 2) * 256, (mc % 2) * 256 + L1T)
                P.op("tensor", "matmul", yb[0:128, ysl],
                     w3h[:, mc * 128:(mc + 1) * 128], ojh[:, 0:L1T],
                     start=(j == 0 and mc % 2 == 0), stop=False)
                P.op("tensor", "matmul", yb[0:128, ysl],
                     w3h[:, mc * 128:(mc + 1) * 128], ojl[:, 0:L1T],
                     start=False, stop=False)
                P.op("tensor", "matmul", yb[0:128, ysl],
                     w3l[:, mc * 128:(mc + 1) * 128], ojh[:, 0:L1T],
                     start=False, stop=(j == NG - 1 and mc % 2 == 1),
                     inc=([(TT, 1)] if mc == NCH - 1 else ()))
        P.wait("vector", TT)
        for mc in (range(NCH) if _PH >= 8 else ()):
            yb = PS[4 + mc // 2]
            ysl = slice((mc % 2) * 256, (mc % 2) * 256 + L1T)
            xs = x_sb[:, mc * S + t * L1T: mc * S + (t + 1) * L1T]
            P.op("vector", "tensor_tensor", xs, xs, yb[0:128, ysl], ALU.add,
                 inc=[(VV, 1)])
        P.wait("tensor", VV)
        P.bar(("tensor", "vector", "scalar", "sync"))
        if t == 0:
            tap("x_oproj_t0", x_sb[:], (128, NCH * S))
        if _PH < 9:
            continue
        am = sm_s[:, 0:2]
        r1 = sm_s[:, 4:6]
        sc2 = sm_s[:, 8:10]
        b2s = ffn_s[:, 0:256]
        t3 = ffn_s[:, 256:384]
        aqp = aqp16[:, 0:128]
        srow = ffn_s[0:1, 512:768]
        for sub in range(2):
            for kb in range(NCH):
                tok = slice(kb * S + t * L1T + sub * 128,
                            kb * S + t * L1T + sub * 128 + 128)
                P.wait("vector", TT, P.mark(TT))
                P.op("vector", "tensor_tensor", t3, xn_h[:, tok], xn_l[:, tok],
                     ALU.add, inc=[(VV, 1)])
                P.wait("tensor", VV)
                P.op("tensor", "transpose", PS[1][0:128, 0:128], t3, ident[:],
                     inc=[(TT, 1)])
                P.wait("vector", TT)
                P.op("vector", "tensor_copy",
                     xt_sub[:, kb * 128:(kb + 1) * 128], PS[1][0:128, 0:128],
                     inc=[(VV, 1)])
            P.op("vector", "reduce_max", am[:, sub:sub + 1], xt_sub[:],
                 axis=AX.X, apply_absolute_value=True, inc=[(VV, 1)])
            P.wait("vector", VV)
            P.op("vector", "tensor_scalar", am[:, sub:sub + 1],
                 am[:, sub:sub + 1], EPS_Q, None, ALU.add)
            P.op("vector", "reciprocal", r1[:, sub:sub + 1], am[:, sub:sub + 1],
                 inc=[(VV, 1)])
            P.wait("vector", VV)
            P.op("vector", "tensor_scalar", r1[:, sub:sub + 1],
                 r1[:, sub:sub + 1], 127.0, None, ALU.mult)
            P.op("vector", "tensor_scalar", sc2[:, sub:sub + 1],
                 am[:, sub:sub + 1], wsb[:, 0:1], None, ALU.mult, inc=[(VV, 1)])
            P.wait("vector", VV)
            if t == 0 and sub == 0:
                tap("f_sc2e", sm_s[:, 8:10], (128, 2))
                tap("f_wsb", wsb[:, 0:3], (128, 3))
            P.op("vector", "tensor_scalar", xt_sub[:], xt_sub[:],
                 r1[:, sub:sub + 1], None, ALU.mult, inc=[(VV, 1)])
            P.wait("vector", VV)
            P.op("vector", "tensor_scalar", xt_sub[:], xt_sub[:],
                 127.0, -128.0, ALU.min, ALU.max, inc=[(VV, 1)])
            P.wait("vector", VV)
            P.op("vector", "tensor_scalar", rnd_t[:], xt_sub[:], 128.5, None,
                 ALU.add, inc=[(VV, 1)])
            P.wait("vector", VV)
            for b_ in (128.0, 64.0, 32.0, 16.0, 8.0, 4.0, 2.0, 1.0):
                P.op("vector", "tensor_single_scalar", rnd_m[:], rnd_t[:], b_,
                     ALU.is_ge, inc=[(VV, 1)])
                P.wait("vector", VV)
                P.op("vector", "scalar_tensor_tensor", rnd_t[:], rnd_m[:], -b_,
                     rnd_t[:], ALU.mult, ALU.add, inc=[(VV, 1)])
                P.wait("vector", VV)
            P.op("vector", "tensor_scalar", xt_sub[:], xt_sub[:], 0.5, None,
                 ALU.add, inc=[(VV, 1)])
            P.wait("vector", VV)
            P.op("vector", "tensor_tensor", xt_sub[:], xt_sub[:], rnd_t[:],
                 ALU.subtract, inc=[(VV, 1)])
            if t == 0 and sub == 0:
                tap("aq_t0", xt_sub[:], (128, 1024))
            if t == 0 and sub == 1:
                tap("aq_t0s1", xt_sub[:], (128, 1024))
            for kb in range(NCH):
                wq_ = wqb[kb % 2]
                P.wait("sync", TT, P.mark(TT))
                dma("sync", DW, wq_[:], wqscr.ap()[kb])
                P.wait("tensor", VV)
                P.op("tensor", "transpose", PS[1][0:128, 0:128],
                     xt_sub[:, kb * 128:(kb + 1) * 128], ident[:], inc=[(TT, 1)])
                P.wait("vector", TT)
                P.op("vector", "tensor_copy", aqp, PS[1][0:128, 0:128],
                     inc=[(VV, 1)])
                P.wait("tensor", DW)
                P.wait("tensor", VV)
                if sub == 0 and kb == 0:
                    P.wait("tensor", SS, P.mark(SS))
                for mc in range(NCH):
                    yb = PS[4 + mc // 2]
                    ysl = slice((mc % 2) * 256 + sub * 128,
                                (mc % 2) * 256 + sub * 128 + 128)
                    P.op("tensor", "matmul", yb[0:128, ysl],
                         wq_[:, mc * 128:(mc + 1) * 128], aqp,
                         start=(sub == 0 and kb == 0 and mc % 2 == 0),
                         stop=(sub == 1 and kb == NCH - 1 and mc % 2 == 1),
                         inc=([(TT, 1)] if mc == NCH - 1 else ()))
            P.op("tensor", "transpose", PS[2 + sub][0:1, 0:128],
                 sc2[:, sub:sub + 1], ident[:], inc=[(TT, 1)])
            P.wait("vector", TT)
            P.op("vector", "tensor_copy", srow[0:1, sub * 128:sub * 128 + 128],
                 PS[2 + sub][0:1, 0:128], inc=[(VV, 1)])
            P.wait("tensor", VV)
        for sub in range(2):
            P.op("tensor", "matmul", PS[2][0:128, 0:128], onesr[0:1, 0:128],
                 srow[0:1, sub * 128:sub * 128 + 128], start=True, stop=True,
                 inc=[(TT, 1)])
            P.wait("vector", TT)
            P.op("vector", "tensor_copy", b2s[:, sub * 128:sub * 128 + 128],
                 PS[2][0:128, 0:128], inc=[(VV, 1)])
            P.wait("tensor", VV)
        for mc in range(NCH):
            yb = PS[4 + mc // 2]
            for sub in range(2):
                ysl = slice((mc % 2) * 256 + sub * 128,
                            (mc % 2) * 256 + sub * 128 + 128)
                P.op("vector", "tensor_tensor", t3,
                     b2s[:, sub * 128:sub * 128 + 128], yb[0:128, ysl], ALU.mult)
                xs = x_sb[:, mc * S + t * L1T + sub * 128:
                          mc * S + t * L1T + sub * 128 + 128]
                P.op("vector", "tensor_tensor", xs, xs, t3, ALU.add,
                     inc=[(VV, 1)])
        P.wait("tensor", VV)
        if t == 0:
            tap("f_b2s", ffn_s[:, 0:256], (128, 256))
            tap("f_srow", ffn_s[0:1, 512:768], (1, 256))
            tap("f_sc2", sm_s[:, 8:10], (128, 2))
        P.bar(("tensor", "vector", "scalar", "sync"))
        if t == 0:
            tap("x_tile0", x_sb[:], (128, NCH * S))
    P.bar()

    dma("gpsimd", OUT, out.ap().rearrange("c p t -> p c t"),
        x_sb[:].rearrange("p (c t) -> p c t", t=S))
    P.wait("gpsimd", OUT)
    P.finalize()
    return nc


def kernel(**inputs):
    cm, pc = _host_stage(inputs)
    nc = build_program(_TAPS)
    in_maps = []
    for b in range(8):
        m = dict(cm)
        m.update(pc[b])
        in_maps.append(m)
    res = run_bass_kernel_spmd(nc, in_maps, core_ids=list(range(8)))
    outs = []
    for b in range(8):
        o = np.asarray(res.results[b]["out"])
        outs.append(np.ascontiguousarray(o.reshape(D, S).T))
    return np.stack(outs)
